# version 01 (final)
# baseline (speedup 1.0000x reference)
import numpy as np

import concourse.bass as bass
import concourse.mybir as mybir

F32 = mybir.dt.float32
BF16 = mybir.dt.bfloat16
U16 = mybir.dt.uint16
I32 = mybir.dt.int32
AF = mybir.ActivationFunctionType
ALU = mybir.AluOpType

PAIRS = [[0, 1], [2, 3], [4, 5], [6, 7]]
ALL8 = [[0, 1, 2, 3, 4, 5, 6, 7]]


def _apname(x):
    try:
        return x.tensor.name
    except AttributeError:
        return None


class Clock:
    def __init__(self, c, name):
        self.sem = c.sem(name)
        self.n = 0
        self.last_write = {}
        self.last_access = {}

    def dep(self, outs, ins):
        need = 0
        for ap in outs:
            nm = _apname(ap)
            if nm:
                need = max(need, self.last_access.get(nm, 0))
        for ap in ins:
            nm = _apname(ap)
            if nm:
                need = max(need, self.last_write.get(nm, 0))
        return need

    def record(self, outs, ins):
        self.n += 1
        for ap in outs:
            nm = _apname(ap)
            if nm:
                self.last_write[nm] = self.last_access[nm] = self.n
        for ap in ins:
            nm = _apname(ap)
            if nm:
                self.last_access[nm] = max(self.last_access.get(nm, 0), self.n)


class EW:

    def __init__(self, eng, clock):
        self.e = eng
        self.ck = clock

    def _go(self, outs, ins, fn):
        need = self.ck.dep(outs, ins)
        if need > 0:
            self.e.wait_ge(self.ck.sem, need)
        inst = fn()
        inst.then_inc(self.ck.sem, 1)
        self.ck.record(outs, ins)
        return inst

    def wait_ge(self, sem, v):
        self.e.wait_ge(sem, v)

    def copy(self, out, in_):
        return self._go([out], [in_], lambda: self.e.tensor_copy(out, in_))

    def memset(self, out, v):
        return self._go([out], [], lambda: self.e.memset(out, v))

    def tt(self, out, a, b, op):
        return self._go([out], [a, b], lambda: self.e.tensor_tensor(out, a, b, op))

    def ts(self, out, a, s1, s2, op0, op1=None, accum_out=None):
        outs = [out] + ([accum_out] if accum_out is not None else [])
        ins = [a] + ([s1] if not isinstance(s1, (int, float)) else [])
        if op1 is None:
            return self._go(outs, ins, lambda: self.e.tensor_scalar(
                out, a, s1, None, op0, accum_out=accum_out))
        return self._go(outs, ins, lambda: self.e.tensor_scalar(
            out, a, s1, s2, op0, op1, accum_out=accum_out))

    def stt(self, out, in0, scalar, in1, op0, op1, accum_out=None):
        outs = [out] + ([accum_out] if accum_out is not None else [])
        ins = [in0, in1] + ([scalar] if not isinstance(scalar, (int, float)) else [])
        return self._go(outs, ins, lambda: self.e.scalar_tensor_tensor(
            out, in0, scalar, in1, op0, op1, accum_out=accum_out))

    def scan(self, out, d0, d1, init, op0, op1):
        ins = [d0, d1] + ([init] if not isinstance(init, (int, float)) else [])
        return self._go([out], ins, lambda: self.e.tensor_tensor_scan(
            out, d0, d1, init, op0, op1))

    def reduce(self, out, in_, op, **kw):
        return self._go([out], [in_], lambda: self.e.tensor_reduce(
            out, in_, mybir.AxisListType.X, op, **kw))

    def reciprocal(self, out, in_):
        return self._go([out], [in_], lambda: self.e.reciprocal(out, in_))

    def select(self, out, mask, a, b):
        return self._go([out], [mask, a, b], lambda: self.e.select(out, mask, a, b))

    def iota(self, out, pattern, base=None, channel_multiplier=None):
        return self._go([out], [], lambda: self.e.iota(
            out, pattern, base=base, channel_multiplier=channel_multiplier,
            allow_small_or_imprecise_dtypes=True))

    def affine_select(self, out, in_, pattern, compare_op, fill, base=None,
                      channel_multiplier=None):
        return self._go([out], [in_], lambda: self.e.affine_select(
            out, in_, pattern, compare_op, fill, base=base,
            channel_multiplier=channel_multiplier))

    def act(self, out, in_, func, bias=0.0, scale=1.0, accum_out=None):
        outs = [out] + ([accum_out] if accum_out is not None else [])
        ins = [in_] + [x for x in (bias, scale) if not isinstance(x, (int, float))]
        return self._go(outs, ins, lambda: self.e.activation(
            out, in_, func, bias=bias, scale=scale, accum_out=accum_out))


class Ctx:
    def __init__(self, nc, ctx):
        self.nc = nc
        self.ctx = ctx
        self._root = ctx
        self._scopes = []
        self._init_rest()

    def push_scope(self, es):
        self._scopes.append(self.ctx)
        self.ctx = es

    def pop_scope(self):
        self.ctx = self._scopes.pop()

    def _init_rest(self):
        self._sems = {}
        self._names = set()
        self.vck = Clock(self, "vck")
        self.sck = Clock(self, "sck")
        self.gck = Clock(self, "gck")
        self.sp = self.sem("sp")
        self.spn = 0
        self._constv = {}
        self._const_sb = None
        self._const_ready = None

    def constf(self, val):
        val = float(val)
        if val not in self._constv:
            assert self._const_sb is None, "constf after emit_consts"
            self._constv[val] = len(self._constv)
        return val

    def const_ap(self, val):
        idx = self._constv[float(val)]
        return self._const_sb[:, idx:idx + 1]

    def emit_consts(self, block):
        n = max(1, len(self._constv))
        self._const_sb = self.sb("constbank", [128, n], F32)

        def body(geng):
            g = self.G(geng)
            for v, i in self._constv.items():
                g.memset(self._const_sb[:, i:i + 1], v)
        block.gpsimd(body)
        self._const_ready = self.gck.n
        return self.gck.n

    def sem(self, name):
        while name in self._sems:
            name = name + "_"
        self._sems[name] = self.root_sem(name)
        return self._sems[name]

    def root_sem(self, name):
        return self._root.enter_context(self.nc.semaphore(name))

    DPOOL_CAP_HW = 40
    DPOOL_CAP_SW = 14

    def ds(self, name, sw=False):
        if not hasattr(self, "_dpool"):
            self._dpool = {False: [], True: []}
            self._dpool_i = {False: 0, True: 0}
        pool = self._dpool[sw]
        cap = self.DPOOL_CAP_SW if sw else self.DPOOL_CAP_HW
        if len(pool) < cap:
            pool.append([self.root_sem(
                f"dp{'s' if sw else 'h'}{len(pool)}"), 0])
        ent = pool[self._dpool_i[sw] % len(pool)]
        self._dpool_i[sw] += 1
        return DS(ent)

    def sb(self, name, shape, dtype):
        while name in self._names:
            name = name + "_"
        self._names.add(name)
        return self.ctx.enter_context(self.nc.sbuf_tensor(name, shape, dtype))

    def ps(self, name, shape, dtype=F32):
        return self.ctx.enter_context(self.nc.psum_tensor(name, shape, dtype))

    def V(self, eng):
        return EW(eng, self.vck)

    def S(self, eng):
        return EW(eng, self.sck)

    def G(self, eng):
        return EW(eng, self.gck)

    def pe_sig(self, inst):
        inst.then_inc(self.sp, 1)
        self.spn += 1
        return self.spn


class DS:

    def __init__(self, ent):
        self.ent = ent
        self.sem = ent[0]
        self.base = ent[1]

    def inc(self, inst, k=16):
        inst.then_inc(self.sem, k)
        self.ent[1] += k
        return inst

    def at(self, local):
        return self.base + local


class WStream:

    def __init__(self, c, name, wb_ap, n_k, chunk, depth=2, dtype=BF16):
        self.c = c
        self.name = name
        self.wb = wb_ap
        self.n_k = n_k
        self.chunk = chunk
        self.depth = depth
        self.n_m = wb_ap.shape[0]
        assert wb_ap.shape[1] == n_k * 128 * chunk, (wb_ap.shape, n_k, chunk)
        self.gsz = n_k * chunk
        self.buf = c.sb(f"{name}_buf", [128, depth * self.gsz], dtype)
        self.resident = depth >= self.n_m
        if self.resident:
            self.ld = [c.ds(f"{name}_ld", sw=True)]
        else:
            self.ld = [c.ds(f"{name}_ld{i}", sw=True) for i in range(depth)]

    def emit_one(self, g, m, free_req=None):
        if m >= self.depth and free_req is not None:
            req = free_req(m - self.depth)
            if req is not None:
                g.wait_ge(req[0], req[1])
        slot = m % self.depth
        ld = self.ld[0] if self.resident else self.ld[slot]
        if m >= self.depth:
            g.wait_ge(ld.sem, ld.at(16 * (m // self.depth)))
        ld.inc(g.dma_start(
            out=self.buf[:, slot * self.gsz:(slot + 1) * self.gsz]
                .rearrange("p (kg c) -> p kg c", c=self.chunk),
            in_=self.wb[m].rearrange("(kg p c) -> p kg c", p=128,
                                     c=self.chunk),
        ))

    def emit_producer(self, g, free_req=None):
        for m in range(self.n_m):
            self.emit_one(g, m, free_req)

    def wait(self, eng, m):
        if self.resident:
            eng.wait_ge(self.ld[0].sem, self.ld[0].at(16 * self.n_m))
            return
        ld = self.ld[m % self.depth]
        eng.wait_ge(ld.sem, ld.at(16 * (m // self.depth + 1)))

    def tile(self, m, k):
        slot = m % self.depth
        base = slot * self.gsz + k * self.chunk
        return self.buf[:, base:base + self.chunk]

    @staticmethod
    def host_block(wt, chunk):
        K, M = wt.shape
        n_m = M // chunk
        return np.ascontiguousarray(
            wt.reshape(K, n_m, chunk).transpose(1, 0, 2).reshape(n_m, K * chunk))


def emit_build_h128(c, block, tag="h"):
    h128 = c.sb(tag + "_h128", [128, 128], BF16)
    jj = c.sb(tag + "_jj", [128, 128], I32)
    pc = c.sb(tag + "_pc", [128, 1], I32)
    t0 = c.sb(tag + "_t0", [128, 128], I32)
    t1 = c.sb(tag + "_t1", [128, 128], I32)

    @block.gpsimd
    def _(geng):
        g = c.G(geng)
        g.iota(jj[:], [[1, 128]], channel_multiplier=0)
        g.iota(pc[:], [[1, 1]], channel_multiplier=1)
    g_done = c.gck.n

    @block.vector
    def _(veng):
        v = c.V(veng)
        v.wait_ge(c.gck.sem, g_done)
        v.ts(t0[:], jj[:], pc[:, 0:1], None, ALU.bitwise_and)
        for sh in (4, 2, 1):
            v.ts(t1[:], t0[:], sh, None, ALU.logical_shift_right)
            v.tt(t0[:], t0[:], t1[:], ALU.bitwise_xor)
        v.ts(t0[:], t0[:], 1, None, ALU.bitwise_and)
        v.ts(h128[:], t0[:], -2, 1, ALU.mult, ALU.add)
    return h128


def emit_l1_ffn(c, block, x_bf, out_sb, w1t, w3t, w2t, T, tag="l1f",
                in_ready=None, do_fwht=True, gate_act=None, out_add_to=None,
                w_depths=(2, 2, 2), x_stride=None, x_off=0, out_dram=None,
                h128=None, out_gate=None):
    n_t = T // 512
    X = x_stride if x_stride is not None else T
    H_G = 64
    N_SL = T // 256
    gate_act_f = AF.Silu if gate_act is None else gate_act
    ws1 = WStream(c, tag + "w1", w1t, 16, 128, depth=w_depths[0])
    ws3 = WStream(c, tag + "w3", w3t, 16, 128, depth=w_depths[1])
    ws2 = WStream(c, tag + "w2", w2t, 64, 128, depth=w_depths[2])

    hid = c.sb(tag + "_hid", [128, H_G * T], BF16)
    tmp = c.sb(tag + "_tmp", [128, 32 * 256], BF16) if do_fwht else None
    sgate = c.sb(tag + "_sg", [128, n_t * 512], BF16)
    if out_dram is not None:
        out_sb = c.sb(tag + "_ost", [128, 2 * 512], BF16)
        st_sem = [c.ds(tag + f"_ost{i}") for i in range(2)]
    if h128 is None and do_fwht:
        h128 = emit_build_h128(c, block, tag)

    ps_g = c.ps(tag + "_psg", [128, n_t * 512])
    ps_l = c.ps(tag + "_psl", [128, n_t * 512])
    ps_a = c.ps(tag + "_psa", [128, 2 * 512])
    ps_o = c.ps(tag + "_pso", [128, 2 * 512])

    NI = H_G * n_t
    sp0 = c.spn
    vck0 = c.vck.n
    sck0 = c.sck.n
    gate_sp = [sp0 + 2 * i + 1 for i in range(NI)]
    lin_sp = [sp0 + 2 * i + 2 for i in range(NI)]
    n_a = NI if do_fwht else 0
    a_sp = [sp0 + 2 * NI + i + 1 for i in range(n_a)]
    o_sp = [sp0 + 2 * NI + n_a + i + 1 for i in range(16 * n_t)]
    silu_done = [sck0 + i + 1 for i in range(NI)]
    out_done = [sck0 + NI + i + 1 for i in range(16 * n_t)]
    mult_done = [vck0 + i + 1 for i in range(NI)]
    bfly_v = vck0 + NI + (N_SL * 18 if do_fwht else 0)
    evac_done = [bfly_v + i + 1 for i in range(n_a)]
    v_adds = [(evac_done[-1] if do_fwht else bfly_v) + i + 1
              for i in range(16 * n_t)] if out_add_to is not None else []

    @block.gpsimd
    def _(g):
        for ht in range(H_G):
            ws1.emit_one(g, ht, lambda m: (c.sp, gate_sp[m * n_t + n_t - 1]))
            ws3.emit_one(g, ht, lambda m: (c.sp, lin_sp[m * n_t + n_t - 1]))
        for dt_ in range(16):
            ws2.emit_one(g, dt_, lambda m: (c.sp, o_sp[m * n_t + n_t - 1]))

    @block.tensor
    def _(t):
        if in_ready is not None:
            t.wait_ge(in_ready[0], in_ready[1])
        for ht in range(H_G):
            for tt in range(n_t):
                if ht > 0:
                    t.wait_ge(c.vck.sem, mult_done[(ht - 1) * n_t + tt])
                ws1.wait(t, ht)
                for k in range(16):
                    mm = t.matmul(
                        ps_g[:, tt * 512:(tt + 1) * 512], ws1.tile(ht, k),
                        x_bf[:, k * X + x_off + tt * 512:
                             k * X + x_off + (tt + 1) * 512],
                        start=(k == 0), stop=(k == 15))
                c.pe_sig(mm)
                ws3.wait(t, ht)
                for k in range(16):
                    mm = t.matmul(
                        ps_l[:, tt * 512:(tt + 1) * 512], ws3.tile(ht, k),
                        x_bf[:, k * X + x_off + tt * 512:
                             k * X + x_off + (tt + 1) * 512],
                        start=(k == 0), stop=(k == 15))
                c.pe_sig(mm)
        if do_fwht:
            t.wait_ge(c.vck.sem, bfly_v)
            for g in range(H_G):
                for tt in range(n_t):
                    idx = g * n_t + tt
                    if idx >= 2:
                        t.wait_ge(c.vck.sem, evac_done[idx - 2])
                    mm = t.matmul(
                        ps_a[:, (idx % 2) * 512:(idx % 2) * 512 + 512], h128[:],
                        hid[:, g * T + tt * 512: g * T + (tt + 1) * 512],
                        start=True, stop=True)
                    c.pe_sig(mm)
        for dt_ in range(16):
            for tt in range(n_t):
                idx = dt_ * n_t + tt
                if idx == 0:
                    t.wait_ge(c.vck.sem,
                              evac_done[NI - 1] if do_fwht else bfly_v)
                if idx >= 2:
                    t.wait_ge(c.sck.sem, out_done[idx - 2])
                oslot = idx % 2
                ws2.wait(t, dt_)
                for g in range(64):
                    mm = t.matmul(
                        ps_o[:, oslot * 512:(oslot + 1) * 512], ws2.tile(dt_, g),
                        hid[:, g * T + tt * 512: g * T + (tt + 1) * 512],
                        start=(g == 0), stop=(g == 63))
                c.pe_sig(mm)
    assert c.spn == sp0 + (3 if do_fwht else 2) * NI + 16 * n_t, \
        (c.spn, sp0, NI)

    @block.scalar
    def _(seng):
        s = c.S(seng)
        for ht in range(H_G):
            for tt in range(n_t):
                idx = ht * n_t + tt
                if ht > 0:
                    s.wait_ge(c.vck.sem, mult_done[(ht - 1) * n_t + tt])
                s.wait_ge(c.sp, gate_sp[idx])
                s.act(sgate[:, tt * 512:(tt + 1) * 512],
                      ps_g[:, tt * 512:(tt + 1) * 512], gate_act_f)
        scale = float(1.0 / np.sqrt(8192.0)) if do_fwht else 1.0
        if out_dram is not None and out_gate is not None:
            for sem_, val_ in out_gate:
                s.wait_ge(sem_, val_)
        for dt_ in range(16):
            for tt in range(n_t):
                idx = dt_ * n_t + tt
                s.wait_ge(c.sp, o_sp[idx])
                if out_add_to is not None and idx >= 2:
                    s.wait_ge(c.vck.sem, v_adds[idx - 2])
                oslot = idx % 2
                if out_dram is not None:
                    if idx >= 2:
                        s.wait_ge(st_sem[idx % 2].sem,
                                  st_sem[idx % 2].at(16 * (idx // 2)))
                    dst = out_sb[:, oslot * 512:(oslot + 1) * 512]
                else:
                    dst = out_sb[:, dt_ * T + tt * 512:
                                 dt_ * T + (tt + 1) * 512]
                s.act(dst, ps_o[:, oslot * 512:(oslot + 1) * 512], AF.Copy,
                      scale=scale)
    assert c.sck.n == out_done[-1], (c.sck.n, out_done[-1])

    if out_dram is not None:
        @block.sync
        def _(sync):
            for dt_ in range(16):
                for tt in range(n_t):
                    idx = dt_ * n_t + tt
                    oslot = idx % 2
                    sync.wait_ge(c.sck.sem, out_done[idx])
                    st_sem[oslot].inc(sync.dma_start(
                        out=out_dram[:, dt_, tt * 512:(tt + 1) * 512],
                        in_=out_sb[:, oslot * 512:(oslot + 1) * 512]))

    @block.vector
    def _(veng):
        v = c.V(veng)
        for ht in range(H_G):
            for tt in range(n_t):
                idx = ht * n_t + tt
                v.wait_ge(c.sck.sem, silu_done[idx])
                v.wait_ge(c.sp, lin_sp[idx])
                v.tt(hid[:, ht * T + tt * 512: ht * T + (tt + 1) * 512],
                     sgate[:, tt * 512:(tt + 1) * 512],
                     ps_l[:, tt * 512:(tt + 1) * 512], ALU.mult)
        hidv = hid[:].rearrange("p (g t) -> p g t", g=H_G)
        for sl in range(N_SL if do_fwht else 0):
            t0, t1 = sl * 256, (sl + 1) * 256
            for r in range(6):
                h = 1 << r
                a4 = hidv[:, :, t0:t1].rearrange(
                    "p (b two i) t -> p b two i t", two=2, i=h)
                av = a4[:, :, 0]
                bv = a4[:, :, 1]
                nblk = H_G // (2 * h)
                tview = tmp[:].rearrange("p (b i t) -> p b i t",
                                         b=nblk, i=h)[:, :, :, 0:256]
                v.tt(tview, av, bv, ALU.subtract)
                v.tt(av, av, bv, ALU.add)
                v.copy(bv, tview)
        assert c.vck.n == bfly_v, (c.vck.n, bfly_v)
        if do_fwht:
            for g in range(H_G):
                for tt in range(n_t):
                    idx = g * n_t + tt
                    v.wait_ge(c.sp, a_sp[idx])
                    v.copy(hid[:, g * T + tt * 512: g * T + (tt + 1) * 512],
                           ps_a[:, (idx % 2) * 512:(idx % 2) * 512 + 512])
        if out_add_to is not None:
            for dt_ in range(16):
                for tt in range(n_t):
                    idx = dt_ * n_t + tt
                    v.wait_ge(c.sck.sem, out_done[idx])
                    v.tt(out_add_to[:, dt_ * T + tt * 512:
                                    dt_ * T + (tt + 1) * 512],
                         out_add_to[:, dt_ * T + tt * 512:
                                     dt_ * T + (tt + 1) * 512],
                         out_sb[:, dt_ * T + tt * 512: dt_ * T + (tt + 1) * 512],
                         ALU.add)

    r_ = dict(done_sem=c.sck.sem, done_val=out_done[-1], done_v=c.vck.n)
    if out_dram is not None:
        r_["st"] = [(st_sem[i].sem, st_sem[i].at(16 * 8 * n_t))
                    for i in range(2)]
    return r_


def emit_identity(c, block, tag="id"):
    ident = c.sb(tag + "_ident", [128, 128], BF16)
    jj = c.sb(tag + "_jj", [128, 128], F32)
    pc = c.sb(tag + "_pc", [128, 1], F32)

    @block.gpsimd
    def _(geng):
        g = c.G(geng)
        g.iota(jj[:], [[1, 128]], channel_multiplier=0)
        g.iota(pc[:], [[1, 1]], channel_multiplier=1)
    g_done = c.gck.n

    @block.vector
    def _(veng):
        v = c.V(veng)
        v.wait_ge(c.gck.sem, g_done)
        v.ts(ident[:], jj[:], pc[:, 0:1], None, ALU.is_equal)
    return ident


def emit_rope_tables(c, block, qpos_iota, T, tag="rope", scale_q=None):
    nc = c.nc
    pos = c.sb(tag + "_pos", [1, T], F32)
    posb = c.sb(tag + "_posb", [128, T], F32)
    invf = c.sb(tag + "_invf", [128, 1], F32)
    ii = c.sb(tag + "_ii", [128, 1], F32)
    msk = c.sb(tag + "_msk", [128, 1], F32)
    ang = c.sb(tag + "_ang", [128, T], F32)
    tmp = c.sb(tag + "_tmp", [128, T], F32)
    tmpi = c.sb(tag + "_tmpi", [128, T], I32)
    frac = c.sb(tag + "_frac", [128, T], F32)
    cosk = c.sb(tag + "_cosk", [128, T], F32)
    sink = c.sb(tag + "_sink", [128, T], F32)
    cosq = c.sb(tag + "_cosq", [128, T], BF16)
    sinq = c.sb(tag + "_sinq", [128, T], BF16)
    onesb = c.sb(tag + "_ones", [1, 128], F32)
    ps_b = c.ps(tag + "_psb", [128, 512])

    TWO_PI = float(2 * np.pi)
    possem = c.sem(tag + "_pos")
    if isinstance(qpos_iota, tuple):
        pattern, base = qpos_iota
        pos_dram = None
    else:
        pos_dram = qpos_iota

    @block.gpsimd
    def _(geng):
        g = c.G(geng)
        if pos_dram is None:
            g.iota(pos[:], pattern, base=base, channel_multiplier=0)
        else:
            g.e.dma_start(out=pos[:], in_=pos_dram).then_inc(possem, 16)
            g.e.wait_ge(possem, 16)
            g.iota(ii[:], [[1, 1]], channel_multiplier=1)
        g.iota(ii[:], [[1, 1]], channel_multiplier=1)
    g_done = c.gck.n

    @block.vector
    def _(veng):
        v = c.V(veng)
        v.wait_ge(c.gck.sem, g_done)
        v.memset(onesb[:], 1.0)
        v.ts(msk[:], ii[:], 64.0, None, ALU.is_ge)
        v.stt(ii[:], msk[:], -64.0, ii[:], ALU.mult, ALU.add)
    v_done = c.vck.n

    @block.scalar
    def _(seng):
        s = c.S(seng)
        s.wait_ge(c.vck.sem, v_done)
        s.act(invf[:], ii[:], AF.Exp, scale=float(-np.log(10000.0) / 64.0))
    s_done = c.sck.n

    sp_vals = []

    @block.tensor
    def _(t):
        t.wait_ge(c.vck.sem, v_done)
        for i in range(T // 512):
            mm = t.matmul(ps_b[:, 0:512], onesb[0:1, :],
                          pos[0:1, i * 512:(i + 1) * 512],
                          start=True, stop=True)
            sp_vals.append(c.pe_sig(mm))
            t.wait_ge(c.vck.sem, v_done + i + 1)

    @block.vector
    def _(veng):
        v = c.V(veng)
        for i in range(T // 512):
            v.wait_ge(c.sp, sp_vals[i])
            v.copy(posb[:, i * 512:(i + 1) * 512], ps_b[:, 0:512])

    def range_reduce(v, shift):
        v.ts(tmp[:], ang[:], float(1.0 / TWO_PI), shift, ALU.mult, ALU.add)
        v.copy(tmpi[:], tmp[:])
        v.copy(frac[:], tmpi[:])
        v.tt(frac[:], tmp[:], frac[:], ALU.subtract)
        v.ts(tmp[:], frac[:], 0.5, None, ALU.is_gt)
        v.tt(frac[:], frac[:], tmp[:], ALU.subtract)
        v.ts(tmp[:], frac[:], -0.5, None, ALU.is_lt)
        v.tt(frac[:], frac[:], tmp[:], ALU.add)

    @block.vector
    def _(veng):
        v = c.V(veng)
        v.wait_ge(c.sck.sem, s_done)
        v.ts(ang[:], posb[:], invf[:, 0:1], None, ALU.mult)
        range_reduce(v, 0.0)
    vs_done = c.vck.n

    @block.scalar
    def _(seng):
        s = c.S(seng)
        s.wait_ge(c.vck.sem, vs_done)
        s.act(sink[:], frac[:], AF.Sin, scale=TWO_PI)
    sk_done = c.sck.n

    @block.vector
    def _(veng):
        v = c.V(veng)
        v.wait_ge(c.sck.sem, sk_done)
        range_reduce(v, 0.25)
    vc_done = c.vck.n

    @block.scalar
    def _(seng):
        s = c.S(seng)
        s.wait_ge(c.vck.sem, vc_done)
        s.act(cosk[:], frac[:], AF.Sin, scale=TWO_PI)
    sc_done2 = c.sck.n

    @block.vector
    def _(veng):
        v = c.V(veng)
        v.wait_ge(c.sck.sem, sc_done2)
        sq = 1.0 if scale_q is None else scale_q
        v.ts(cosq[:], cosk[:], float(sq), None, ALU.mult)
        v.ts(sinq[0:64, :], sink[0:64, :], float(-sq), None, ALU.mult)
        v.ts(sinq[64:128, :], sink[64:128, :], float(sq), None, ALU.mult)
        v.ts(sink[0:64, :], sink[0:64, :], -1.0, None, ALU.mult)
    return dict(cosq=cosq, sinq=sinq, cosk=cosk, sink=sink,
                ready_v=c.vck.n)


def emit_mha(c, block, q_sb, k_sb, v_sb, yT_sb, T, Tk, qb, n_heads=16,
             kv_of_head=None, ident=None, tag="mha", kv_ready=None,
             extra_q=None, mask_dram=None):
    n_jt = T // 128
    nkv = (max(kv_of_head(h) for h in range(n_heads)) + 1) if kv_of_head else 1
    kvh = kv_of_head or (lambda h: 0)

    nks = [(qb[j] + 128) // 128 for j in range(n_jt)]
    nkt = [(qb[j] + 128 + 511) // 512 for j in range(n_jt)]
    W_of = [nks[j] * 128 for j in range(n_jt)]

    p_sb = c.sb(tag + "_p", [128, ((Tk + 511) // 512) * 512], BF16)
    pT_sb = c.sb(tag + "_pT", [128, 2 * 128], BF16)
    vT_sb = c.sb(tag + "_vT", [128, nkv * (Tk // 128) * 128], BF16)
    nmax = c.sb(tag + "_nmax", [128, 1], F32)
    rsum = c.sb(tag + "_rsum", [128, 1], F32)
    rcp = c.sb(tag + "_rcp", [128, 1], F32)
    mask = c.sb(tag + "_mask", [128, 896], F32)

    ps_sc = c.ps(tag + "_pssc", [128, ((Tk + 511) // 512) * 512])
    ps_trb = c.ps(tag + "_pstrb", [128, 2 * 1024], BF16)
    ps_pv = c.ps(tag + "_pspv", [128, 128])

    msem2 = c.ds(tag + "_msk", sw=True)

    @block.gpsimd
    def _(geng):
        g = c.G(geng)
        if mask_dram is not None:
            msem2.inc(geng.dma_start(out=mask[:], in_=mask_dram))
            geng.wait_ge(msem2.sem, msem2.at(16))
            g.iota(mask[0:1, 0:1], [[1, 1]], base=0, channel_multiplier=0)
            g.affine_select(mask[0:1, 0:1], mask[0:1, 0:1], [[-1, 1]],
                            ALU.is_ge, 0.0, base=-10, channel_multiplier=1)
        else:
            g.memset(mask[:], 0.0)
            g.affine_select(mask[:], mask[:], [[-1, 896]], ALU.is_ge, -1e30,
                            base=384, channel_multiplier=1)
    mask_g = c.gck.n

    sp0, v0, s0 = c.spn, c.vck.n, c.sck.n
    sp_sc, sp_tr, sp_pvlast, sp_vt = {}, {}, {}, {}
    v_negmax, v_recip, v_pnorm, v_ptev, v_ytev, v_vtev = {}, {}, {}, {}, {}, {}
    s_exp = {}
    vt_order = []
    spn, vn, sn = sp0, v0, s0
    for j in range(n_jt):
        lo = nks[j - 1] if j > 0 else 0
        for sub in range(lo, nks[j]):
            for kv in range(nkv):
                spn += 1
                sp_vt[(kv, sub)] = spn
                vn += 1
                v_vtev[(kv, sub)] = vn
                vt_order.append((kv, sub))
        for h in range(n_heads):
            spn += 1
            sp_sc[(j, h)] = spn
            vn += 1
            vn += 1
            v_negmax[(j, h)] = vn
            sn += 1
            s_exp[(j, h)] = sn
            vn += 1
            v_recip[(j, h)] = vn
            vn += 1
            v_pnorm[(j, h)] = vn
            for sub in range(nks[j]):
                spn += 1
                sp_tr[(j, h, sub)] = spn
                vn += 1
                v_ptev[(j, h, sub)] = vn
            spn += 1
            sp_pvlast[(j, h)] = spn
            vn += 1
            v_ytev[(j, h)] = vn

    @block.tensor
    def _(t):
        if kv_ready is not None:
            t.wait_ge(kv_ready[0], kv_ready[1])
        if extra_q is not None:
            t.wait_ge(extra_q[0], extra_q[1])
        vt_i = 0
        for j in range(n_jt):
            lo = nks[j - 1] if j > 0 else 0
            first_vt_of_j = True
            for sub in range(lo, nks[j]):
                for kv in range(nkv):
                    if first_vt_of_j and j > 0:
                        t.wait_ge(c.vck.sem, v_ytev[(j - 1, n_heads - 1)])
                        first_vt_of_j = False
                    if vt_i >= 2:
                        t.wait_ge(c.vck.sem, v_vtev[vt_order[vt_i - 2]])
                    slot = vt_i % 2
                    mm = t.transpose(
                        ps_trb[:, slot * 1024: slot * 1024 + 128],
                        v_sb[:, kv * Tk + sub * 128: kv * Tk + (sub + 1) * 128],
                        ident[:])
                    c.pe_sig(mm)
                    vt_i += 1
            for h in range(n_heads):
                kv = kvh(h)
                if (j, h) != (0, 0):
                    ph = (j, h - 1) if h > 0 else (j - 1, n_heads - 1)
                    t.wait_ge(c.sck.sem, s_exp[ph])
                for kt in range(nkt[j]):
                    mm = t.matmul(
                        ps_sc[:, kt * 512:(kt + 1) * 512],
                        q_sb[:, h * T + j * 128: h * T + (j + 1) * 128],
                        k_sb[:, kv * Tk + kt * 512: kv * Tk + (kt + 1) * 512],
                        start=True, stop=True)
                c.pe_sig(mm)
                t.wait_ge(c.vck.sem, v_pnorm[(j, h)])
                if h == 0 and vt_i > 0:
                    t.wait_ge(c.vck.sem, v_vtev[vt_order[vt_i - 1]])
                for sub in range(min(2, nks[j])):
                    mm = t.transpose(
                        ps_trb[:, (sub % 2) * 1024:(sub % 2) * 1024 + 128],
                        p_sb[:, sub * 128:(sub + 1) * 128], ident[:])
                    c.pe_sig(mm)
                for sub in range(nks[j]):
                    t.wait_ge(c.vck.sem, v_ptev[(j, h, sub)])
                    if sub == 0:
                        t.wait_ge(c.vck.sem, v_vtev[(kv, nks[j] - 1)])
                        if (j, h) != (0, 0):
                            ph = (j, h - 1) if h > 0 else (j - 1, n_heads - 1)
                            t.wait_ge(c.vck.sem, v_ytev[ph])
                    mm = t.matmul(
                        ps_pv[:, 0:128],
                        vT_sb[:, (kv * (Tk // 128) + sub) * 128:
                              (kv * (Tk // 128) + sub) * 128 + 128],
                        pT_sb[:, (sub % 2) * 128:(sub % 2) * 128 + 128],
                        start=(sub == 0), stop=(sub == nks[j] - 1))
                    if sub + 2 < nks[j]:
                        mm2 = t.transpose(
                            ps_trb[:, (sub % 2) * 1024:(sub % 2) * 1024 + 128],
                            p_sb[:, (sub + 2) * 128:(sub + 3) * 128], ident[:])
                        c.pe_sig(mm2)
                c.pe_sig(mm)
    assert c.spn == spn, (c.spn, spn)

    @block.scalar
    def _(seng):
        s = c.S(seng)
        for j in range(n_jt):
            for h in range(n_heads):
                W = W_of[j]
                s.wait_ge(c.vck.sem, v_negmax[(j, h)])
                s.act(p_sb[:, 0:W], ps_sc[:, 0:W], AF.Exp,
                      bias=nmax[:, 0:1], accum_out=rsum[:, 0:1])
    assert c.sck.n == sn, (c.sck.n, sn)

    @block.vector
    def _(veng):
        v = c.V(veng)
        v.wait_ge(c.gck.sem, mask_g)
        for j in range(n_jt):
            lo = nks[j - 1] if j > 0 else 0
            for sub in range(lo, nks[j]):
                for kv in range(nkv):
                    v.wait_ge(c.sp, sp_vt[(kv, sub)])
                    slot = vt_order.index((kv, sub)) % 2
                    v.copy(vT_sb[:, (kv * (Tk // 128) + sub) * 128:
                                 (kv * (Tk // 128) + sub) * 128 + 128],
                           ps_trb[:, slot * 1024: slot * 1024 + 128])
            for h in range(n_heads):
                W = W_of[j]
                v.wait_ge(c.sp, sp_sc[(j, h)])
                D = qb[j] - (nkt[j] - 1) * 512
                mw = min(512, W - (nkt[j] - 1) * 512)
                base_ = (nkt[j] - 1) * 512
                v.tt(ps_sc[:, base_: base_ + mw], ps_sc[:, base_: base_ + mw],
                     mask[:, 384 - D: 384 - D + mw], ALU.add)
                v.reduce(nmax[:, 0:1], ps_sc[:, 0:W], ALU.max, negate=True)
                assert c.vck.n == v_negmax[(j, h)]
                v.wait_ge(c.sck.sem, s_exp[(j, h)])
                v.reciprocal(rcp[:, 0:1], rsum[:, 0:1])
                v.ts(p_sb[:, 0:W], p_sb[:, 0:W], rcp[:, 0:1], None, ALU.mult)
                for sub in range(nks[j]):
                    v.wait_ge(c.sp, sp_tr[(j, h, sub)])
                    v.copy(pT_sb[:, (sub % 2) * 128:(sub % 2) * 128 + 128],
                           ps_trb[:, (sub % 2) * 1024:(sub % 2) * 1024 + 128])
                v.wait_ge(c.sp, sp_pvlast[(j, h)])
                v.copy(yT_sb[:, h * T + j * 128: h * T + (j + 1) * 128],
                       ps_pv[:, 0:128])
    assert c.vck.n == vn, (c.vck.n, vn)
    return dict()


def emit_matmul_fm(c, block, x_bf, out_cb, wsb, T, n_k, n_m, tag,
                   in_ready_v=None, in_ready_sp=None, extra_wait=None,
                   nt_free=512):
    raise NotImplementedError


def emit_l1_attn_block(c, block, x_bf, out_sb, wattn_b, wproj_b, normw_d,
                       T, qpos_iota, qb, tag="l1a", in_ready=None,
                       kv_stage=None, rms_eps=1e-6, mask_dram=None):
    nc = c.nc
    n_t = T // 512
    ident = emit_identity(c, block, tag + "id")
    rope = emit_rope_tables(c, block, qpos_iota, T, tag + "rp",
                            scale_q=float(1.0 / np.sqrt(128.0)))
    ws_a = WStream(c, tag + "wa", wattn_b, 16, 128, depth=3)
    ws_p = WStream(c, tag + "wp", wproj_b, 16, 128, depth=3)

    qkv = c.sb(tag + "_qkv", [128, 24 * T], BF16)
    qkv_swap = c.sb(tag + "_qsw", [128, 20 * T], BF16)
    q_sb = c.sb(tag + "_q", [128, 16 * T], BF16)
    k_sb = c.sb(tag + "_k", [128, 4 * T], BF16)
    y2 = c.sb(tag + "_y2", [128, 16 * T], BF16)
    y2sq = c.sb(tag + "_y2sq", [128, 2 * 512], BF16)
    ones_col = c.sb(tag + "_onesc", [128, 1], BF16)
    rstat = c.sb(tag + "_rstat", [1, T], F32)
    rinv = c.sb(tag + "_rinv", [1, T], F32)
    rnw = c.sb(tag + "_rnw", [128, 16], F32)
    ones1 = c.sb(tag + "_ones1", [1, 128], F32)
    ld = c.sem(tag + "_ld")

    ps_qkv = c.ps(tag + "_psq", [128, 2 * 512])
    ps_stb = c.ps(tag + "_psstb", [128, 512])
    ps_st = ps_stb[:, 0:512]
    ps_bc = ps_stb[:, 0:512]

    @block.sync
    def _(sync):
        sync.dma_start(out=rnw[:], in_=normw_d.rearrange("(g p) -> p g", p=128),
                       allow_slow_non_contiguous=True).then_inc(ld, 16)

    sp0, v0, s0 = c.spn, c.vck.n, c.sck.n
    qkv_sp = {}
    spn = sp0
    for m in range(24):
        for tt in range(n_t):
            spn += 1
            qkv_sp[(m, tt)] = spn
    s_qkvev = {}
    sn = s0
    for m in range(24):
        for tt in range(n_t):
            sn += 1
            s_qkvev[(m, tt)] = sn
    sck_after_qkv = sn
    vn = v0
    rope_start = vn
    vn += 20 * 3
    v_rope_done = vn
    swsem = c.sem(tag + "_sw")

    @block.sync
    def _(sync):
        sync.wait_ge(c.sck.sem, s_qkvev[(19, n_t - 1)])
        sync.dma_start(out=qkv_swap[0:64, :], in_=qkv[64:128, 0:20 * T]
                       ).then_inc(swsem, 16)
        sync.dma_start(out=qkv_swap[64:128, :], in_=qkv[0:64, 0:20 * T]
                       ).then_inc(swsem, 16)

    @block.tensor
    def _(t):
        if in_ready is not None:
            t.wait_ge(in_ready[0], in_ready[1])
        for m in range(24):
            ws_a.wait(t, m)
            for tt in range(n_t):
                idx = m * n_t + tt
                if idx >= 2:
                    t.wait_ge(c.sck.sem, s_qkvev[divmod(idx - 2, n_t)])
                for k in range(16):
                    mm = t.matmul(
                        ps_qkv[:, (idx % 2) * 512:(idx % 2) * 512 + 512],
                        ws_a.tile(m, k),
                        x_bf[:, k * T + tt * 512: k * T + (tt + 1) * 512],
                        start=(k == 0), stop=(k == 15))
                c.pe_sig(mm)
    assert c.spn == spn

    @block.gpsimd
    def _(g):
        for m in range(24):
            ws_a.emit_one(g, m, lambda mm_: (c.sp, qkv_sp[(mm_, n_t - 1)]))

    @block.scalar
    def _(seng):
        s = c.S(seng)
        for m in range(24):
            for tt in range(n_t):
                idx = m * n_t + tt
                s.wait_ge(c.sp, qkv_sp[(m, tt)])
                s.act(qkv[:, m * T + tt * 512: m * T + (tt + 1) * 512],
                      ps_qkv[:, (idx % 2) * 512:(idx % 2) * 512 + 512],
                      AF.Copy)
    assert c.sck.n == sck_after_qkv

    @block.vector
    def _(veng):
        v = c.V(veng)
        assert c.vck.n == rope_start
        cq, sq = rope["cosq"], rope["sinq"]
        ck_, sk_ = rope["cosk"], rope["sink"]
        v.wait_ge(swsem, 32)
        for h in range(20):
            src_ = qkv[:, h * T:(h + 1) * T]
            swp = qkv_swap[:, h * T:(h + 1) * T]
            if h < 16:
                dst = q_sb[:, h * T:(h + 1) * T]
                c_t, s_t = cq, sq
            else:
                dst = k_sb[:, (h - 16) * T:(h - 15) * T]
                c_t, s_t = ck_, sk_
            v.tt(dst, src_, c_t[:, 0:T], ALU.mult)
            v.tt(swp, swp, s_t[:, 0:T], ALU.mult)
            v.tt(dst, dst, swp, ALU.add)
    assert c.vck.n == v_rope_done, (c.vck.n, v_rope_done)

    if kv_stage is None:
        k_use = k_sb[:]
        v_use = qkv[:, 20 * T:24 * T]
        Tk = T
        kv_ready = (c.vck.sem, v_rope_done)
        kv_ready2 = (c.sck.sem, sck_after_qkv)
    else:
        k_use, v_use, Tk, kv_ready, kv_ready2 = kv_stage(
            c, block, k_sb, qkv[:, 20 * T:24 * T], v_rope_done, sck_after_qkv)

    yT = c.sb(tag + "_yT", [128, 16 * T], BF16)

    emit_mha(c, block, q_sb[:], k_use, v_use, yT[:], T, Tk, qb,
             n_heads=16, kv_of_head=lambda h: h // 4, ident=ident,
             tag=tag + "m", kv_ready=kv_ready, extra_q=kv_ready2,
             mask_dram=mask_dram)

    mha_v = c.vck.n
    sp1, v1, s1 = c.spn, c.vck.n, c.sck.n
    wp_sp = {}
    spn2 = sp1
    for m in range(16):
        for tt in range(n_t):
            spn2 += 1
            wp_sp[(m, tt)] = spn2
    st_sp = {}
    for tt in range(n_t):
        for m in range(16):
            spn2 += 1
            st_sp[(m, tt)] = spn2
    bc_sp = {}
    for tt in range(n_t):
        spn2 += 1
        bc_sp[tt] = spn2
    sn2 = s1
    s_y2ev = {}
    for m in range(16):
        for tt in range(n_t):
            sn2 += 1
            s_y2ev[(m, tt)] = sn2
    s_sqrt = {}
    for tt in range(n_t):
        sn2 += 1
        s_sqrt[tt] = sn2
    vn2 = v1
    v_ones = vn2 + 1
    vn2 += 2
    v_sq = {}
    for tt in range(n_t):
        for m in range(16):
            vn2 += 1
            v_sq[(m, tt)] = vn2
    v_recipb = {}
    for tt in range(n_t):
        vn2 += 1
        v_recipb[tt] = vn2
    v_out = {}
    for tt in range(n_t):
        for m in range(16):
            vn2 += 1
            v_out[(m, tt)] = vn2

    @block.tensor
    def _(t):
        t.wait_ge(ld, 16)
        t.wait_ge(c.vck.sem, mha_v)
        for m in range(16):
            ws_p.wait(t, m)
            for tt in range(n_t):
                idx = m * n_t + tt
                if idx >= 2:
                    t.wait_ge(c.sck.sem, s_y2ev[divmod(idx - 2, n_t)])
                for k in range(16):
                    mm = t.matmul(
                        ps_qkv[:, (idx % 2) * 512:(idx % 2) * 512 + 512],
                        ws_p.tile(m, k),
                        yT[:, k * T + tt * 512: k * T + (tt + 1) * 512],
                        start=(k == 0), stop=(k == 15))
                c.pe_sig(mm)
        t.wait_ge(c.vck.sem, v_ones + 1)
        for tt in range(n_t):
            if tt > 0:
                t.wait_ge(c.sck.sem, s_sqrt[tt - 1])
                t.wait_ge(c.vck.sem, v_out[(15, tt - 1)])
            for m in range(16):
                t.wait_ge(c.vck.sem, v_sq[(m, tt)])
                mm = t.matmul(ps_st[0:1, 0:512], ones_col[:, 0:1],
                              y2sq[:, (m % 2) * 512:(m % 2) * 512 + 512],
                              start=(m == 0), stop=(m == 15))
                c.pe_sig(mm)
        for tt in range(n_t):
            t.wait_ge(c.vck.sem, v_recipb[tt])
            if tt > 0:
                t.wait_ge(c.vck.sem, v_out[(15, tt - 1)])
            mm = t.matmul(ps_bc[:, 0:512], ones1[0:1, :],
                          rinv[0:1, tt * 512:(tt + 1) * 512],
                          start=True, stop=True)
            c.pe_sig(mm)
    assert c.spn == spn2, (c.spn, spn2)

    @block.gpsimd
    def _(g):
        for m in range(16):
            ws_p.emit_one(g, m, lambda mm_: (c.sp, wp_sp[(mm_, n_t - 1)]))

    @block.scalar
    def _(seng):
        s = c.S(seng)
        for m in range(16):
            for tt in range(n_t):
                s.wait_ge(c.sp, wp_sp[(m, tt)])
                idx = m * n_t + tt
                s.act(y2[:, m * T + tt * 512: m * T + (tt + 1) * 512],
                      ps_qkv[:, (idx % 2) * 512:(idx % 2) * 512 + 512],
                      AF.Copy)
        for tt in range(n_t):
            s.wait_ge(c.sp, st_sp[(15, tt)])
            s.act(rstat[0:1, tt * 512:(tt + 1) * 512], ps_st[0:1, 0:512],
                  AF.Sqrt, scale=float(1.0 / 2048.0),
                  bias=c.const_ap(rms_eps)[0:1, :])
    assert c.sck.n == sn2, (c.sck.n, sn2)

    @block.vector
    def _(veng):
        v = c.V(veng)
        v.memset(ones_col[:], 1.0)
        v.memset(ones1[:], 1.0)
        assert c.vck.n == v_ones + 1
        for tt in range(n_t):
            for m in range(16):
                v.wait_ge(c.sck.sem, s_y2ev[(m, tt)])
                gi = tt * 16 + m
                if gi >= 2:
                    pm = divmod(gi - 2, 16)
                    v.wait_ge(c.sp, st_sp[(pm[1], pm[0])])
                v.tt(y2sq[:, (m % 2) * 512:(m % 2) * 512 + 512],
                     y2[:, m * T + tt * 512: m * T + (tt + 1) * 512],
                     y2[:, m * T + tt * 512: m * T + (tt + 1) * 512], ALU.mult)
        for tt in range(n_t):
            v.wait_ge(c.sck.sem, s_sqrt[tt])
            v.reciprocal(rinv[0:1, tt * 512:(tt + 1) * 512],
                         rstat[0:1, tt * 512:(tt + 1) * 512])
        for tt in range(n_t):
            for m in range(16):
                v.wait_ge(c.sp, bc_sp[tt])
                v.stt(out_sb[:, m * T + tt * 512: m * T + (tt + 1) * 512],
                      y2[:, m * T + tt * 512: m * T + (tt + 1) * 512],
                      rnw[:, m:m + 1], ps_bc[:, 0:512], ALU.mult, ALU.mult)
    assert c.vck.n == vn2, (c.vck.n, vn2)
    return dict(done_v=c.vck.n)


def emit_nvfp4_decode(c, block, pk_d, sc_d, out_lo_d, out_hi_d, J, M,
                      tag="dec", start_gate=None):
    n_j = J // 128
    pk_sb = c.sb(tag + "_pk", [128, 2 * M], I32)
    sc_sb = c.sb(tag + "_sc", [128, 2 * M], BF16)
    b16 = c.sb(tag + "_b16", [128, M], U16)
    u_l = c.sb(tag + "_ul", [128, M], U16)
    u_h = c.sb(tag + "_uh", [128, M], U16)
    t0 = c.sb(tag + "_t0", [128, M], U16)
    t1 = c.sb(tag + "_t1", [128, M], U16)
    wlo = c.sb(tag + "_wlo", [128, 2 * M], BF16)
    whi = c.sb(tag + "_whi", [128, 2 * M], BF16)
    lds = [c.ds(tag + f"_ld{i}", sw=True) for i in range(2)]
    sts = [c.ds(tag + f"_st{i}") for i in range(2)]
    stn = [0]

    ops_per_tile = 11 + (10 if out_hi_d is not None else 0)
    v0 = c.vck.n
    v_tile = [v0 + (i + 1) * ops_per_tile for i in range(n_j)]

    @block.gpsimd
    def _(g):
        for j in range(n_j):
            slot = j % 2
            if j >= 2:
                g.wait_ge(c.vck.sem, v_tile[j - 2])
            if j == 0 and start_gate is not None:
                sg = start_gate if isinstance(start_gate, list) \
                    else [start_gate]
                for sem_, val_ in sg:
                    g.wait_ge(sem_, val_)
            lds[slot].inc(g.dma_start(
                out=pk_sb[:, slot * M:(slot + 1) * M],
                in_=pk_d[j * 128:(j + 1) * 128, :]))
            lds[slot].inc(g.dma_start(
                out=sc_sb[:, slot * M:(slot + 1) * M],
                in_=sc_d[j * 128:(j + 1) * 128, :]))

    @block.vector
    def _(veng):
        v = c.V(veng)
        for j in range(n_j):
            slot = j % 2
            v.wait_ge(lds[slot].sem, lds[slot].at(32 * (j // 2 + 1)))
            if j >= 2:
                outs = 2 if out_hi_d is not None else 1
                v.wait_ge(sts[slot].sem,
                          sts[slot].at(16 * outs * (j // 2)))
            pk = pk_sb[:, slot * M:(slot + 1) * M]
            sc = sc_sb[:, slot * M:(slot + 1) * M]
            lo = wlo[:, slot * M:(slot + 1) * M]
            hi = whi[:, slot * M:(slot + 1) * M]
            v.copy(b16[:], pk)
            v.ts(u_l[:], b16[:], 7, None, ALU.bitwise_and)
            v.ts(t0[:], b16[:], 8, None, ALU.bitwise_and)
            v.ts(t0[:], t0[:], 4096, None, ALU.mult)
            v.ts(t1[:], u_l[:], 1, 16064, ALU.is_ge, ALU.mult)
            v.tt(t0[:], t0[:], t1[:], ALU.add)
            v.ts(t1[:], u_l[:], 2, 64, ALU.is_ge, ALU.mult)
            v.tt(t0[:], t0[:], t1[:], ALU.add)
            v.ts(t1[:], u_l[:], 64, None, ALU.mult)
            v.tt(t0[:], t0[:], t1[:], ALU.add)
            v.tt(lo, t0[:].bitcast(BF16), sc, ALU.mult)
            if out_hi_d is not None:
                v.ts(u_h[:], b16[:], 4, 7, ALU.logical_shift_right,
                     ALU.bitwise_and)
                v.ts(t0[:], b16[:], 7, None, ALU.logical_shift_right)
                v.ts(t0[:], t0[:], 32768, None, ALU.mult)
                v.ts(t1[:], u_h[:], 1, 16064, ALU.is_ge, ALU.mult)
                v.tt(t0[:], t0[:], t1[:], ALU.add)
                v.ts(t1[:], u_h[:], 2, 64, ALU.is_ge, ALU.mult)
                v.tt(t0[:], t0[:], t1[:], ALU.add)
                v.ts(t1[:], u_h[:], 64, None, ALU.mult)
                v.tt(t0[:], t0[:], t1[:], ALU.add)
                v.tt(hi, t0[:].bitcast(BF16), sc, ALU.mult)
            assert c.vck.n == v_tile[j], (j, c.vck.n, v_tile[j])

    @block.sync
    def _(sync):
        for j in range(n_j):
            slot = j % 2
            sync.wait_ge(c.vck.sem, v_tile[j])
            sts[slot].inc(sync.dma_start(
                out=out_lo_d[j * 128:(j + 1) * 128, :],
                in_=wlo[:, slot * M:(slot + 1) * M]))
            stn[0] += 16
            if out_hi_d is not None:
                sts[slot].inc(sync.dma_start(
                    out=out_hi_d[j * 128:(j + 1) * 128, :],
                    in_=whi[:, slot * M:(slot + 1) * M]))
                stn[0] += 16
    outs_ = 2 if out_hi_d is not None else 1
    return dict(done=[(sts[i].sem,
                       sts[i].at(16 * outs_ * ((n_j + 1 - i) // 2)))
                      for i in range(2)])


def emit_mamba(c, block, hn_fm, weights, scratch, DM, CH, Tf, tag="mb",
               in_ready=None, dbc_comm=None):
    nk = DM // 128
    nm = CH // 128
    nt = Tf // 512
    ws_x = WStream(c, tag + "wx", weights["inprojx_b"], nk, 128, depth=2)
    ws_z = WStream(c, tag + "wz", weights["inprojz_b"], nk, 128, depth=2)
    ws_p = WStream(c, tag + "wp", weights["xproj_b"], nm, 80, depth=2)
    ws_dt = WStream(c, tag + "wdt", weights["dtproj_b"], 1, 128, depth=2)

    xbuf = c.sb(tag + "_xbuf", [128, 2 * (Tf + 4)], F32)
    convo = c.sb(tag + "_cvo", [128, Tf], F32)
    xc = c.sb(tag + "_xc", [128, nm * Tf], BF16)
    zb = c.sb(tag + "_zb", [128, Tf], BF16)
    cw = c.sb(tag + "_cw", [128, nm * 4], F32)
    cb = c.sb(tag + "_cb", [128, nm], F32)
    dtb = c.sb(tag + "_dtb", [128, nm], F32)
    A_sb = c.sb(tag + "_A", [128, nm * 16], F32)
    Dp = c.sb(tag + "_Dp", [128, nm], F32)
    dbc_sb = c.sb(tag + "_dbcs", [128, 512], F32)
    dtin_bf = c.sb(tag + "_dtin", [128, Tf], BF16)
    bcrow = c.sb(tag + "_bcrow", [1, 32 * Tf], BF16)
    Brep = c.sb(tag + "_Brep", [128, 16 * Tf], BF16)
    Crep = c.sb(tag + "_Crep", [128, 16 * Tf], BF16)
    onesr = c.sb(tag + "_ones", [1, 128], BF16)
    dtm = c.sb(tag + "_dtm", [128, Tf], F32)
    sp1_ = c.sb(tag + "_sp1", [128, Tf], F32)
    sp2_ = c.sb(tag + "_sp2", [128, Tf], F32)
    um = c.sb(tag + "_um", [128, Tf], BF16)
    dA = c.sb(tag + "_dA", [128, Tf], F32)
    uB = c.sb(tag + "_uB", [128, Tf], BF16)
    so = c.sb(tag + "_so", [128, Tf], BF16)
    yt = c.sb(tag + "_yt", [128, Tf], BF16)
    ya = c.sb(tag + "_ya", [128, Tf], F32)
    ybf = c.sb(tag + "_ybf", [128, nm * Tf], BF16)
    Am = c.sb(tag + "_Am", [128, nm * 16], F32)

    ld = c.sem(tag + "_ld")
    zst = c.sem(tag + "_zst")
    dbst = c.sem(tag + "_dbst")
    dbld = c.sem(tag + "_dbld")
    zld = c.sem(tag + "_zld")

    ps_a = c.ps(tag + "_psa", [128, 2 * 512])
    ps_b = c.ps(tag + "_psb", [128, 2 * 512])

    @block.sync
    def _(sync):
        sync.dma_start(out=cw[:], in_=weights["conv_w"]).then_inc(ld, 16)
        sync.dma_start(out=cb[:], in_=weights["conv_b"]).then_inc(ld, 16)
        sync.dma_start(out=dtb[:], in_=weights["dtproj_bias"]).then_inc(ld, 16)
        sync.dma_start(out=A_sb[:], in_=weights["A_d"]).then_inc(ld, 16)
        sync.dma_start(out=Dp[:], in_=weights["Dp_d"]).then_inc(ld, 16)

    s0a = c.sck.n

    @block.scalar
    def _(seng):
        s = c.S(seng)
        s.wait_ge(ld, 16 * 5)
        s.act(Am[:], A_sb[:], AF.Exp)
    sA = c.sck.n

    @block.vector
    def _(veng):
        v = c.V(veng)
        v.wait_ge(c.sck.sem, sA)
        v.ts(Am[:], Am[:], -1.0, None, ALU.mult)
        v.memset(xbuf[:, 0:4], 0.0)
        v.memset(xbuf[:, Tf + 4:Tf + 8], 0.0)
        v.memset(onesr[:], 1.0)
    vA = c.vck.n

    sp0, v0, s0 = c.spn, c.vck.n, c.sck.n
    spn, vn, sn = sp0, v0, s0
    x_sp, z_sp = {}, {}
    for m in range(nm):
        for tt in range(nt):
            spn += 1
            x_sp[(m, tt)] = spn
        for tt in range(nt):
            spn += 1
            z_sp[(m, tt)] = spn
    s_evx, s_evz, s_silu = {}, {}, {}
    for m in range(nm):
        for tt in range(nt):
            sn += 1
            s_evx[(m, tt)] = sn
        for tt in range(nt):
            sn += 1
            s_evz[(m, tt)] = sn
        sn += 1
        s_silu[m] = sn
    v_conv = {}
    for m in range(nm):
        vn += 4
        v_conv[m] = vn
    spA_end, vA_end, sA_end = spn, vn, sn

    @block.tensor
    def _(t):
        if in_ready is not None:
            t.wait_ge(in_ready[0], in_ready[1])
        t.wait_ge(c.vck.sem, vA)
        for m in range(nm):
            ws_x.wait(t, m)
            for tt in range(nt):
                gi = m * nt + tt
                if gi >= 2:
                    t.wait_ge(c.sck.sem, s_evx[divmod(gi - 2, nt)])
                for k in range(nk):
                    mm = t.matmul(
                        ps_a[:, (gi % 2) * 512:(gi % 2) * 512 + 512],
                        ws_x.tile(m, k),
                        hn_fm[:, k * Tf + tt * 512: k * Tf + (tt + 1) * 512],
                        start=(k == 0), stop=(k == nk - 1))
                c.pe_sig(mm)
            ws_z.wait(t, m)
            for tt in range(nt):
                gi = m * nt + tt
                if gi >= 2:
                    t.wait_ge(c.sck.sem, s_evz[divmod(gi - 2, nt)])
                if m >= 1:
                    t.wait_ge(zst, 16 * (m * nt))
                for k in range(nk):
                    mm = t.matmul(
                        ps_b[:, (gi % 2) * 512:(gi % 2) * 512 + 512],
                        ws_z.tile(m, k),
                        hn_fm[:, k * Tf + tt * 512: k * Tf + (tt + 1) * 512],
                        start=(k == 0), stop=(k == nk - 1))
                c.pe_sig(mm)
    assert c.spn == spA_end

    @block.gpsimd
    def _(g):
        for m in range(nm):
            ws_x.emit_one(g, m, lambda mm_: (c.sp, x_sp[(mm_, nt - 1)]))
            ws_z.emit_one(g, m, lambda mm_: (c.sp, z_sp[(mm_, nt - 1)]))

    @block.scalar
    def _(seng):
        s = c.S(seng)
        for m in range(nm):
            slot = m % 2
            for tt in range(nt):
                gi = m * nt + tt
                s.wait_ge(c.sp, x_sp[(m, tt)])
                s.act(xbuf[:, slot * (Tf + 4) + 4 + tt * 512:
                           slot * (Tf + 4) + 4 + (tt + 1) * 512],
                      ps_a[:, (gi % 2) * 512:(gi % 2) * 512 + 512], AF.Copy)
            for tt in range(nt):
                gi = m * nt + tt
                s.wait_ge(c.sp, z_sp[(m, tt)])
                s.act(zb[:, tt * 512:(tt + 1) * 512],
                      ps_b[:, (gi % 2) * 512:(gi % 2) * 512 + 512], AF.Copy)
            s.wait_ge(c.vck.sem, v_conv[m])
            s.act(xc[:, m * Tf:(m + 1) * Tf], convo[:], AF.Silu,
                  bias=cb[:, m:m + 1])
    assert c.sck.n == sA_end

    @block.sync
    def _(sync):
        for m in range(nm):
            for tt in range(nt):
                sync.wait_ge(c.sck.sem, s_evz[(m, tt)])
                sync.dma_start(out=scratch["z_page"]
                               .rearrange("(mm p) t -> p mm t", p=128)
                               [:, m, tt * 512:(tt + 1) * 512],
                               in_=zb[:, tt * 512:(tt + 1) * 512]
                               ).then_inc(zst, 16)

    @block.vector
    def _(veng):
        v = c.V(veng)
        for m in range(nm):
            slot = m % 2
            for tt in range(nt):
                v.wait_ge(c.sck.sem, s_evx[(m, tt)])
            if m >= 1:
                v.wait_ge(c.sck.sem, s_silu[m - 1])
            base = slot * (Tf + 4)
            v.ts(convo[:], xbuf[:, base + 1: base + 1 + Tf],
                 cw[:, 4 * m:4 * m + 1], None, ALU.mult)
            for i in (1, 2, 3):
                v.stt(convo[:], xbuf[:, base + 1 + i: base + 1 + i + Tf],
                      cw[:, 4 * m + i:4 * m + i + 1], convo[:],
                      ALU.mult, ALU.add)
    assert c.vck.n == vA_end

    spn2, vn2, sn2 = c.spn, c.vck.n, c.sck.n
    p_sp = {}
    for mo in range(2):
        for tt in range(nt):
            spn2 += 1
            p_sp[(mo, tt)] = spn2
    s_evp = {}
    for mo in range(2):
        for tt in range(nt):
            sn2 += 1
            s_evp[(mo, tt)] = sn2

    @block.tensor
    def _(t):
        t.wait_ge(c.sck.sem, sA_end)
        for mo in range(2):
            ws_p.wait(t, mo)
            for tt in range(nt):
                idx = mo * nt + tt
                if idx >= 1:
                    t.wait_ge(c.sck.sem, s_evp[divmod(idx - 1, nt)])
                for k in range(nm):
                    mm = t.matmul(
                        ps_a[0:80, 0:512],
                        ws_p.tile(mo, k),
                        xc[:, k * Tf + tt * 512: k * Tf + (tt + 1) * 512],
                        start=(k == 0), stop=(k == nm - 1))
                c.pe_sig(mm)

    @block.gpsimd
    def _(g):
        for mo in range(2):
            ws_p.emit_one(g, mo, None)

    @block.scalar
    def _(seng):
        s = c.S(seng)
        for mo in range(2):
            for tt in range(nt):
                s.wait_ge(c.sp, p_sp[(mo, tt)])
                if not (mo == 0 and tt == 0):
                    s.wait_ge(dbst, 16 * (mo * nt + tt))
                s.act(dbc_sb[0:80, 0:512], ps_a[0:80, 0:512], AF.Copy)

    @block.sync
    def _(sync):
        for mo in range(2):
            for tt in range(nt):
                sync.wait_ge(c.sck.sem, s_evp[(mo, tt)])
                sync.dma_start(out=scratch["dbc_part"]
                               [mo * 80:(mo + 1) * 80, tt * 512:(tt + 1) * 512],
                               in_=dbc_sb[0:80, 0:512]).then_inc(dbst, 16)
    assert c.spn == spn2 and c.sck.n == sn2

    if dbc_comm is None:
        dbc_full = scratch["dbc_part"]
        dbc_ready = (dbst, 16 * 2 * nt)
    else:
        dbc_full, dbc_ready = dbc_comm(c, block, 16 * 2 * nt)

    @block.gpsimd
    def _(g):
        g.wait_ge(dbc_ready[0], dbc_ready[1])
        g.dma_start(out=dtin_bf[:], in_=dbc_full[0:128, :]).then_inc(dbld, 16)
        g.dma_start(out=bcrow[:].rearrange("one (r t) -> one r t", r=32),
                    in_=dbc_full[128:160, :].unsqueeze(0)).then_inc(dbld, 16)

    spn3 = c.spn
    bc_sp = {}
    for r in range(32):
        for tt in range(nt):
            spn3 += 1
            bc_sp[(r, tt)] = spn3
    vn3 = c.vck.n
    v_bc = {}
    for r in range(32):
        for tt in range(nt):
            vn3 += 1
            v_bc[(r, tt)] = vn3

    @block.tensor
    def _(t):
        t.wait_ge(dbld, 32)
        for r in range(32):
            for tt in range(nt):
                idx = r * nt + tt
                if idx >= 2:
                    t.wait_ge(c.vck.sem, v_bc[divmod(idx - 2, nt)])
                mm = t.matmul(ps_b[:, (idx % 2) * 512:(idx % 2) * 512 + 512],
                              onesr[0:1, :],
                              bcrow[0:1, r * Tf + tt * 512: r * Tf + (tt + 1) * 512],
                              start=True, stop=True)
                c.pe_sig(mm)
    assert c.spn == spn3

    @block.vector
    def _(veng):
        v = c.V(veng)
        for r in range(32):
            dst = Brep if r < 16 else Crep
            n_ = r if r < 16 else r - 16
            for tt in range(nt):
                idx = r * nt + tt
                v.wait_ge(c.sp, bc_sp[(r, tt)])
                v.copy(dst[:, n_ * Tf + tt * 512: n_ * Tf + (tt + 1) * 512],
                       ps_b[:, (idx % 2) * 512:(idx % 2) * 512 + 512])
    assert c.vck.n == vn3

    zin = c.sb(tag + "_zin", [128, 2 * Tf], BF16)
    zg = c.sb(tag + "_zg", [128, Tf], BF16)
    dA2 = c.sb(tag + "_dA2", [128, 2 * Tf], F32)

    sp4, v4, s4 = c.spn, c.vck.n, c.sck.n
    spn4 = sp4
    dt_sp = {}
    for m in range(nm):
        for tt in range(nt):
            spn4 += 1
            dt_sp[(m, tt)] = spn4
    sn4 = s4
    s_ar, s_ln, s_dA, s_zg = {}, {}, {}, {}
    for m in range(nm):
        for tt in range(nt):
            sn4 += 2
            s_ar[(m, tt)] = sn4
        sn4 += 2
        s_ln[m] = sn4
        for n_ in range(16):
            sn4 += 1
            s_dA[(m, n_)] = sn4
        sn4 += 1
        s_zg[m] = sn4
    vn4 = v4
    v_dtm, v_um, v_scan, v_ydone, v_ybf = {}, {}, {}, {}, {}
    for m in range(nm):
        vn4 += 1
        v_dtm[m] = vn4
        vn4 += 1
        v_um[m] = vn4
        vn4 += 1
        for n_ in range(16):
            vn4 += 2
            v_scan[(m, n_)] = vn4
            vn4 += 2
            v_ydone[(m, n_)] = vn4
        vn4 += 1
        v_ybf[m] = vn4

    zld = [c.sem(tag + f"_zld{i}") for i in range(2)]

    @block.tensor
    def _(t):
        t.wait_ge(dbld, 32)
        for m in range(nm):
            ws_dt.wait(t, m)
            for tt in range(nt):
                gi = m * nt + tt
                if gi >= 2:
                    pg = divmod(gi - 2, nt)
                    t.wait_ge(c.sck.sem, s_ar[pg])
                mm = t.matmul(ps_a[:, (gi % 2) * 512:(gi % 2) * 512 + 512],
                              ws_dt.tile(m, 0),
                              dtin_bf[:, tt * 512:(tt + 1) * 512],
                              start=True, stop=True)
                c.pe_sig(mm)
    assert c.spn == spn4, (c.spn, spn4)

    @block.gpsimd
    def _(g):
        for m in range(nm):
            ws_dt.emit_one(g, m, lambda mm_: (c.sp, dt_sp[(mm_, nt - 1)]))

    @block.sync
    def _(sync):
        for m in range(nm):
            slot = m % 2
            if m >= 2:
                sync.wait_ge(c.sck.sem, s_zg[m - 2])
            sync.wait_ge(zst, 16 * nt * (m + 1))
            sync.dma_start(out=zin[:, slot * Tf:(slot + 1) * Tf],
                           in_=scratch["z_page"]
                           .rearrange("(mm p) t -> p mm t", p=128)[:, m, :]
                           ).then_inc(zld[slot], 16)

    @block.scalar
    def _(seng):
        s = c.S(seng)
        s.wait_ge(c.gck.sem, c._const_ready)
        for m in range(nm):
            for tt in range(nt):
                gi = m * nt + tt
                s.wait_ge(c.sp, dt_sp[(m, tt)])
                pa = ps_a[:, (gi % 2) * 512:(gi % 2) * 512 + 512]
                s.act(sp1_[:, tt * 512:(tt + 1) * 512], pa, AF.Abs,
                      bias=dtb[:, m:m + 1])
                s.act(sp2_[:, tt * 512:(tt + 1) * 512], pa, AF.Relu,
                      bias=dtb[:, m:m + 1])
            s.act(sp1_[:], sp1_[:], AF.Exp, scale=-1.0)
            s.act(sp1_[:], sp1_[:], AF.Ln, bias=c.const_ap(1.0))
            for n_ in range(16):
                s.wait_ge(c.vck.sem, v_dtm[m])
                if (m, n_) > (0, 1):
                    pn = (m, n_ - 2) if n_ >= 2 else (
                        (m - 1, 14 + n_) if m > 0 else None)
                    if pn is not None:
                        s.wait_ge(c.vck.sem, v_scan[pn])
                s.act(dA2[:, (n_ % 2) * Tf:(n_ % 2) * Tf + Tf], dtm[:], AF.Exp,
                      scale=Am[:, m * 16 + n_:m * 16 + n_ + 1])
            s.wait_ge(zld[m % 2], 16 * (m // 2 + 1))
            if m >= 1:
                s.wait_ge(c.vck.sem, v_ybf[m - 1])
            s.act(zg[:], zin[:, (m % 2) * Tf:(m % 2) * Tf + Tf], AF.Silu)
    assert c.sck.n == sn4, (c.sck.n, sn4)

    @block.vector
    def _(veng):
        v = c.V(veng)
        for m in range(nm):
            v.wait_ge(c.sck.sem, s_ln[m])
            v.tt(dtm[:], sp2_[:], sp1_[:], ALU.add)
            v.tt(um[:], xc[:, m * Tf:(m + 1) * Tf], dtm[:], ALU.mult)
            v.ts(ya[:], xc[:, m * Tf:(m + 1) * Tf], Dp[:, m:m + 1], None,
                 ALU.mult)
            for n_ in range(16):
                v.wait_ge(c.sck.sem, s_dA[(m, n_)])
                v.tt(uB[:], um[:], Brep[:, n_ * Tf:(n_ + 1) * Tf], ALU.mult)
                v.scan(so[:], dA2[:, (n_ % 2) * Tf:(n_ % 2) * Tf + Tf], uB[:],
                       0.0, ALU.mult, ALU.add)
                v.tt(yt[:], so[:], Crep[:, n_ * Tf:(n_ + 1) * Tf], ALU.mult)
                v.tt(ya[:], ya[:], yt[:], ALU.add)
            v.wait_ge(c.sck.sem, s_zg[m])
            v.tt(ybf[:, m * Tf:(m + 1) * Tf], ya[:], zg[:], ALU.mult)
    assert c.vck.n == vn4, (c.vck.n, vn4)

    return dict(ybf=ybf, ready_v=c.vck.n, xc=xc)


def emit_diff_mha(c, block, q_sb, k_sb, v_sb, yT_sb, lam_parts, subln_rep,
                  T, Tk, qb, ident, tag="dm", kv_ready=None, rms_eps=1e-5,
                  lam_init=0.35568673616064565, mask_dram=None):
    n_jt = T // 128
    nks = [(qb[j] + 128) // 128 for j in range(n_jt)]
    nkt = [(qb[j] + 128 + 511) // 512 for j in range(n_jt)]
    neglam, lam_ready = lam_parts
    srep, srep_ready = subln_rep

    p1_sb = c.sb(tag + "_p1", [128, ((Tk + 511) // 512) * 512], BF16)
    pd_sb = c.sb(tag + "_pd", [128, ((Tk + 511) // 512) * 512], BF16)
    pT_sb = c.sb(tag + "_pT", [128, 2 * 128], BF16)
    vT_sb = c.sb(tag + "_vT", [128, (Tk // 128) * 512], BF16)
    nmax = c.sb(tag + "_nmax", [128, 1], F32)
    rs1 = c.sb(tag + "_rs1", [128, 1], F32)
    rs2 = c.sb(tag + "_rs2", [128, 1], F32)
    rc1 = c.sb(tag + "_rc1", [128, 1], F32)
    nr2 = c.sb(tag + "_nr2", [128, 1], F32)
    ssum = c.sb(tag + "_ssum", [128, 1], F32)
    sqs = c.sb(tag + "_sqs", [128, 256], BF16)
    dn_sb = c.sb(tag + "_dn", [128, 256], BF16)
    mask = c.sb(tag + "_mask", [128, 896], F32)

    ps_sc = c.ps(tag + "_pssc", [128, ((Tk + 511) // 512) * 512])
    ps_trb = c.ps(tag + "_pstrb", [128, 2 * 1024], BF16)
    ps_pv = c.ps(tag + "_pspv", [128, 256])

    msem2 = c.sem(tag + "_msk")

    @block.gpsimd
    def _(geng):
        g = c.G(geng)
        if mask_dram is not None:
            g.e.dma_start(out=mask[:], in_=mask_dram).then_inc(msem2, 16)
            g.e.wait_ge(msem2, 16)
            g.iota(mask[0:1, 0:1], [[1, 1]], base=0, channel_multiplier=0)
            g.affine_select(mask[0:1, 0:1], mask[0:1, 0:1], [[-1, 1]],
                            ALU.is_ge, 0.0, base=-10, channel_multiplier=1)
        else:
            g.memset(mask[:], 0.0)
            g.affine_select(mask[:], mask[:], [[-1, 896]], ALU.is_ge, -1e30,
                            base=384, channel_multiplier=1)
    mask_g = c.gck.n

    sp0, v0, s0 = c.spn, c.vck.n, c.sck.n
    spn, vn, sn = sp0, v0, s0
    sp_vt, v_vtev, vt_order = {}, {}, []
    sp_sc1, sp_sc2, sp_tr, sp_pvl, sp_dtr = {}, {}, {}, {}, {}
    v_ng1, v_ng2, v_pd, v_ptev, v_dnev = {}, {}, {}, {}, {}
    s_e1, s_e2, s_sq, s_rms = {}, {}, {}, {}
    v_dn2 = {}
    for j in range(n_jt):
        lo = nks[j - 1] if j > 0 else 0
        for sub in range(lo, nks[j]):
            for kv in range(4):
                spn += 1
                sp_vt[(kv, sub)] = spn
                vn += 1
                v_vtev[(kv, sub)] = vn
                vt_order.append((kv, sub))
        for L in range(8):
            spn += 1
            sp_sc1[(j, L)] = spn
            vn += 2
            v_ng1[(j, L)] = vn
            sn += 1
            s_e1[(j, L)] = sn
            spn += 1
            sp_sc2[(j, L)] = spn
            vn += 2
            v_ng2[(j, L)] = vn
            sn += 1
            s_e2[(j, L)] = sn
            vn += 5
            v_pd[(j, L)] = vn
            for sub in range(nks[j]):
                spn += 1
                sp_tr[(j, L, sub)] = spn
                vn += 1
                v_ptev[(j, L, sub)] = vn
            spn += 1
            sp_pvl[(j, L)] = spn
            sn += 2
            s_sq[(j, L)] = sn
            vn += 3
            v_dn2[(j, L)] = vn
            spn += 2
            sp_dtr[(j, L)] = spn
            vn += 2
            v_dnev[(j, L)] = vn

    @block.tensor
    def _(t):
        if kv_ready is not None:
            t.wait_ge(kv_ready[0], kv_ready[1])
        t.wait_ge(c.vck.sem, max(lam_ready, srep_ready))
        vt_i = 0
        for j in range(n_jt):
            lo = nks[j - 1] if j > 0 else 0
            first_vt = True
            for sub in range(lo, nks[j]):
                for kv in range(4):
                    if first_vt and j > 0:
                        t.wait_ge(c.vck.sem, v_dnev[(j - 1, 7)])
                        first_vt = False
                    if vt_i >= 2:
                        t.wait_ge(c.vck.sem, v_vtev[vt_order[vt_i - 2]])
                    slot = vt_i % 2
                    mm = t.transpose(
                        ps_trb[:, slot * 1024: slot * 1024 + 128],
                        v_sb[:, kv * Tk + sub * 128: kv * Tk + (sub + 1) * 128],
                        ident[:])
                    c.pe_sig(mm)
                    vt_i += 1
            for L in range(8):
                kvp = L // 4
                if (j, L) != (0, 0):
                    ph = (j, L - 1) if L > 0 else (j - 1, 7)
                    t.wait_ge(c.sck.sem, s_e2[ph])
                for kt in range(nkt[j]):
                    mm = t.matmul(
                        ps_sc[:, kt * 512:(kt + 1) * 512],
                        q_sb[:, 2 * L * T + j * 128: 2 * L * T + (j + 1) * 128],
                        k_sb[:, 2 * kvp * Tk + kt * 512:
                             2 * kvp * Tk + (kt + 1) * 512],
                        start=True, stop=True)
                c.pe_sig(mm)
                t.wait_ge(c.sck.sem, s_e1[(j, L)])
                for kt in range(nkt[j]):
                    mm = t.matmul(
                        ps_sc[:, kt * 512:(kt + 1) * 512],
                        q_sb[:, (2 * L + 1) * T + j * 128:
                             (2 * L + 1) * T + (j + 1) * 128],
                        k_sb[:, (2 * kvp + 1) * Tk + kt * 512:
                             (2 * kvp + 1) * Tk + (kt + 1) * 512],
                        start=True, stop=True)
                c.pe_sig(mm)
                t.wait_ge(c.vck.sem, v_pd[(j, L)])
                if L == 0 and vt_i > 0:
                    t.wait_ge(c.vck.sem, v_vtev[vt_order[vt_i - 1]])
                if (j, L) != (0, 0):
                    phh = (j, L - 1) if L > 0 else (j - 1, 7)
                    t.wait_ge(c.vck.sem, v_dnev[phh])
                for sub in range(min(2, nks[j])):
                    mm = t.transpose(
                        ps_trb[:, (sub % 2) * 1024:(sub % 2) * 1024 + 128],
                        pd_sb[:, sub * 128:(sub + 1) * 128], ident[:])
                    c.pe_sig(mm)
                for sub in range(nks[j]):
                    t.wait_ge(c.vck.sem, v_ptev[(j, L, sub)])
                    if sub == 0:
                        t.wait_ge(c.vck.sem, v_vtev[(3, nks[j] - 1)])
                        if (j, L) != (0, 0):
                            ph = (j, L - 1) if L > 0 else (j - 1, 7)
                            t.wait_ge(c.vck.sem, v_dn2[ph])
                    mm = t.matmul(
                        ps_pv[:, 0:256],
                        pT_sb[:, (sub % 2) * 128:(sub % 2) * 128 + 128],
                        vT_sb[:, sub * 512 + kvp * 256: sub * 512 + kvp * 256 + 256],
                        start=(sub == 0), stop=(sub == nks[j] - 1))
                    if sub + 2 < nks[j]:
                        mm2 = t.transpose(
                            ps_trb[:, (sub % 2) * 1024:(sub % 2) * 1024 + 128],
                            pd_sb[:, (sub + 2) * 128:(sub + 3) * 128], ident[:])
                        c.pe_sig(mm2)
                c.pe_sig(mm)
                t.wait_ge(c.vck.sem, v_dn2[(j, L)])
                for half in range(2):
                    mm = t.transpose(
                        ps_trb[:, (half % 2) * 1024:(half % 2) * 1024 + 128],
                        dn_sb[:, half * 128:(half + 1) * 128], ident[:])
                    c.pe_sig(mm)
    assert c.spn == spn, (c.spn, spn)

    @block.scalar
    def _(seng):
        s = c.S(seng)
        s.wait_ge(c.gck.sem, c._const_ready)
        for j in range(n_jt):
            W = nks[j] * 128
            for L in range(8):
                s.wait_ge(c.vck.sem, v_ng1[(j, L)])
                s.act(p1_sb[:, 0:W], ps_sc[:, 0:W], AF.Exp,
                      bias=nmax[:, 0:1], accum_out=rs1[:, 0:1])
                s.wait_ge(c.vck.sem, v_ng2[(j, L)])
                s.act(pd_sb[:, 0:W], ps_sc[:, 0:W], AF.Exp,
                      bias=nmax[:, 0:1], accum_out=rs2[:, 0:1])
                s.wait_ge(c.sp, sp_pvl[(j, L)])
                s.act(sqs[:, 0:256], ps_pv[:, 0:256], AF.Square,
                      accum_out=ssum[:, 0:1])
                s.act(ssum[:, 0:1], ssum[:, 0:1], AF.Sqrt,
                      scale=float(1.0 / 256.0), bias=c.const_ap(rms_eps))
    assert c.sck.n == sn, (c.sck.n, sn)

    @block.vector
    def _(veng):
        v = c.V(veng)
        v.wait_ge(c.gck.sem, mask_g)
        for j in range(n_jt):
            lo = nks[j - 1] if j > 0 else 0
            W = nks[j] * 128
            for sub in range(lo, nks[j]):
                for kv in range(4):
                    v.wait_ge(c.sp, sp_vt[(kv, sub)])
                    slot = vt_order.index((kv, sub)) % 2
                    v.copy(vT_sb[:, sub * 512 + (kv // 2) * 256 +
                                 (kv % 2) * 128: sub * 512 + (kv // 2) * 256 +
                                 (kv % 2) * 128 + 128],
                           ps_trb[:, slot * 1024: slot * 1024 + 128])
            for L in range(8):
                D = qb[j] - (nkt[j] - 1) * 512
                mw = min(512, W - (nkt[j] - 1) * 512)
                base_ = (nkt[j] - 1) * 512
                for which in range(2):
                    v.wait_ge(c.sp, (sp_sc1 if which == 0 else sp_sc2)[(j, L)])
                    if which == 1:
                        v.wait_ge(c.sck.sem, s_e1[(j, L)])
                    v.tt(ps_sc[:, base_: base_ + mw],
                         ps_sc[:, base_: base_ + mw],
                         mask[:, 384 - D: 384 - D + mw], ALU.add)
                    v.reduce(nmax[:, 0:1], ps_sc[:, 0:W], ALU.max, negate=True)
                v.wait_ge(c.sck.sem, s_e2[(j, L)])
                v.reciprocal(rc1[:, 0:1], rs1[:, 0:1])
                v.reciprocal(nr2[:, 0:1], rs2[:, 0:1])
                v.tt(nr2[:, 0:1], nr2[:, 0:1], neglam[:, 0:1], ALU.mult)
                v.ts(p1_sb[:, 0:W], p1_sb[:, 0:W], rc1[:, 0:1], None, ALU.mult)
                v.stt(pd_sb[:, 0:W], pd_sb[:, 0:W], nr2[:, 0:1],
                      p1_sb[:, 0:W], ALU.mult, ALU.add)
                for sub in range(nks[j]):
                    v.wait_ge(c.sp, sp_tr[(j, L, sub)])
                    v.copy(pT_sb[:, (sub % 2) * 128:(sub % 2) * 128 + 128],
                           ps_trb[:, (sub % 2) * 1024:(sub % 2) * 1024 + 128])
                v.wait_ge(c.sck.sem, s_sq[(j, L)])
                v.reciprocal(ssum[:, 0:1], ssum[:, 0:1])
                v.ts(dn_sb[:, 0:256], ps_pv[:, 0:256], ssum[:, 0:1], None,
                     ALU.mult)
                v.tt(dn_sb[:, 0:256], dn_sb[:, 0:256], srep[0:128, 0:256],
                     ALU.mult)
                for half in range(2):
                    v.wait_ge(c.sp, sp_dtr[(j, L)] - 1 + half)
                    v.copy(yT_sb[:, (2 * L + half) * T + j * 128:
                                 (2 * L + half) * T + (j + 1) * 128],
                           ps_trb[:, (half % 2) * 1024:(half % 2) * 1024 + 128])
    assert c.vck.n == vn, (c.vck.n, vn)
    return dict(done_v=c.vck.n)


def emit_norm(c, block, T, stat, ps_st, ps_bc, onesc, sq_full, tag,
              src_bf, src_f32=None, mode="ln", eps=1e-5, eps2=None,
              w_col=None, out_bf=None, out_f32=None, add_to=None,
              in_ready=None, D=2048):
    n_t = T // 512
    need_mean = mode in ("ln", "ln_rms")
    asrc = src_f32 if src_f32 is not None else src_bf
    r = lambda i_, sl=None: stat[0:1, i_ * T:(i_ + 1) * T] if sl is None else \
        stat[0:1, i_ * T + sl * 512: i_ * T + (sl + 1) * 512]
    do_cast = src_f32 is not None

    def xsl(g_, tt):
        if do_cast:
            return src_bf[:, g_ * 512:(g_ + 1) * 512]
        return src_bf[:, g_ * T + tt * 512: g_ * T + (tt + 1) * 512]

    sp0, v0, s0 = c.spn, c.vck.n, c.sck.n
    spn, vn, sn = sp0, v0, s0
    s_sq, sp_sum, v_ev, v_cast = {}, {}, {}, {}
    for tt in range(n_t):
        if do_cast:
            vn += 1
        v_cast[tt] = vn
        sn += 1
        s_sq[tt] = sn
        if need_mean:
            spn += 1
            sp_sum[("m", tt)] = spn
            vn += 1
            v_ev[("m", tt)] = vn
        spn += 1
        sp_sum[("s", tt)] = spn
        vn += 1
        v_ev[("s", tt)] = vn
    nv_row1 = {"ln": 4, "rms": 1, "ln_rms": 4}[mode]
    vn += nv_row1
    v_rowa = vn
    sn += 1
    s_row = sn
    nv_row2 = {"ln": 1, "rms": 1, "ln_rms": 4}[mode]
    vn += nv_row2
    v_rowb = vn
    if mode == "ln_rms":
        sn += 1
        s_row2 = sn
        vn += 1
        v_rowc = vn
    row_ready_v = vn
    nbc = {"ln": 2, "rms": 1, "ln_rms": 3}[mode]
    napply = {"ln": 2, "rms": 1, "ln_rms": 3}[mode]
    if mode == "ln" and add_to is not None:
        napply = 3
    sp_bc, v_ap = {}, {}
    for tt in range(n_t):
        for b_ in range(nbc):
            spn += 1
            sp_bc[(tt, b_)] = spn
        vn += napply * 16
        v_ap[tt] = vn

    def sbody(seng):
        s = c.S(seng)
        s.wait_ge(c.gck.sem, c._const_ready)
        if in_ready is not None:
            s.wait_ge(in_ready[0], in_ready[1])
        for tt in range(n_t):
            if do_cast:
                s.wait_ge(c.vck.sem, v_cast[tt])
            if tt > 0:
                s.wait_ge(c.sp, sp_sum[("s", tt - 1)])
            if do_cast:
                s.act(sq_full[:],
                      src_bf[:].rearrange("p (g t) -> p g t", g=16),
                      AF.Square)
            else:
                s.act(sq_full[:],
                      src_bf[:].rearrange("p (g t) -> p g t", g=16)
                      [:, :, tt * 512:(tt + 1) * 512], AF.Square)
        s.wait_ge(c.vck.sem, v_rowa)
        s.act(r(4), r(3), AF.Sqrt, bias=c.const_ap(eps)[0:1, :])
        if mode == "ln_rms":
            s.wait_ge(c.vck.sem, v_rowb)
            s.act(r(1), r(0), AF.Sqrt, bias=c.const_ap(eps2)[0:1, :])
    block.scalar(sbody)
    assert c.sck.n == (s_row2 if mode == "ln_rms" else s_row)

    def tbody(t):
        for tt in range(n_t):
            if do_cast:
                t.wait_ge(c.vck.sem, v_cast[tt])
            if need_mean:
                if tt > 0:
                    t.wait_ge(c.vck.sem, v_ev[("s", tt - 1)])
                for gidx in range(16):
                    mm = t.matmul(ps_st[0:1, 0:512], onesc[:, 0:1],
                                  xsl(gidx, tt),
                                  start=(gidx == 0), stop=(gidx == 15))
                c.pe_sig(mm)
            t.wait_ge(c.sck.sem, s_sq[tt])
            if need_mean:
                t.wait_ge(c.vck.sem, v_ev[("m", tt)])
            elif tt > 0:
                t.wait_ge(c.vck.sem, v_ev[("s", tt - 1)])
            for gidx in range(16):
                mm = t.matmul(ps_st[0:1, 0:512], onesc[:, 0:1],
                              sq_full[:, gidx * 512:(gidx + 1) * 512],
                              start=(gidx == 0), stop=(gidx == 15))
            c.pe_sig(mm)
        t.wait_ge(c.vck.sem, row_ready_v)
        for tt in range(n_t):
            if tt > 0:
                t.wait_ge(c.vck.sem, v_ap[tt - 1])
            rows = {"ln": (2, 5), "rms": (5,), "ln_rms": (2, 5, 0)}[mode]
            for b_, ri in enumerate(rows):
                mm = t.matmul(ps_bc[:, b_ * 512:(b_ + 1) * 512],
                              c._ones_row_f32[0:1, :], r(ri, tt),
                              start=True, stop=True)
                c.pe_sig(mm)
    block.tensor(tbody)
    assert c.spn == spn, (c.spn, spn)

    def vbody(veng):
        v = c.V(veng)
        for tt in range(n_t):
            if do_cast:
                if in_ready is not None and tt == 0:
                    v.wait_ge(in_ready[0], in_ready[1])
                if tt > 0:
                    v.wait_ge(c.sp, sp_sum[("s", tt - 1)])
                v.copy(src_bf[:].rearrange("p (g t) -> p g t", g=16),
                       src_f32[:].rearrange("p (g t) -> p g t", g=16)
                       [:, :, tt * 512:(tt + 1) * 512])
            if need_mean:
                v.wait_ge(c.sp, sp_sum[("m", tt)])
                v.copy(r(0, tt), ps_st[0:1, 0:512])
            v.wait_ge(c.sp, sp_sum[("s", tt)])
            v.copy(r(1, tt), ps_st[0:1, 0:512])
        if mode in ("ln", "ln_rms"):
            v.ts(r(2), r(0), 1.0 / D, None, ALU.mult)
            v.ts(r(3), r(1), 1.0 / D, None, ALU.mult)
            v.tt(r(1), r(2), r(2), ALU.mult)
            v.tt(r(3), r(3), r(1), ALU.subtract)
        else:
            v.ts(r(3), r(1), 1.0 / D, None, ALU.mult)
        assert c.vck.n == v_rowa
        v.wait_ge(c.sck.sem, s_row)
        v.reciprocal(r(5), r(4))
        if mode == "ln_rms":
            v.ts(r(0), r(3), float(eps), None, ALU.add)
            v.reciprocal(r(1), r(0))
            v.tt(r(0), r(3), r(1), ALU.mult)
        assert c.vck.n == v_rowb
        if mode == "ln_rms":
            v.wait_ge(c.sck.sem, s_row2)
            v.reciprocal(r(0), r(1))
            assert c.vck.n == v_rowc
        sl = lambda buf, g_, tt: buf[:, g_ * T + tt * 512:
                                     g_ * T + (tt + 1) * 512]
        for tt in range(n_t):
            for b_ in range(nbc):
                v.wait_ge(c.sp, sp_bc[(tt, b_)])
            mps = ps_bc[:, 0:512]
            ips = ps_bc[:, 512:1024] if need_mean else ps_bc[:, 0:512]
            rps = ps_bc[:, 1024:1536]
            for g_ in range(16):
                wg = w_col[:, g_:g_ + 1] if w_col is not None else None
                if mode == "ln":
                    v.tt(sl(out_bf, g_, tt), sl(asrc, g_, tt), mps,
                         ALU.subtract)
                    v.tt(sl(out_bf, g_, tt), sl(out_bf, g_, tt), ips,
                         ALU.mult)
                    if add_to is not None:
                        v.tt(sl(add_to, g_, tt), sl(add_to, g_, tt),
                             sl(out_bf, g_, tt), ALU.add)
                elif mode == "rms":
                    v.stt(sl(out_bf, g_, tt), sl(asrc, g_, tt), wg, ips,
                          ALU.mult, ALU.mult)
                else:
                    v.tt(sl(out_f32, g_, tt), sl(asrc, g_, tt), mps,
                         ALU.subtract)
                    v.tt(sl(out_f32, g_, tt), sl(out_f32, g_, tt), ips,
                         ALU.mult)
                    v.stt(sl(out_bf, g_, tt), sl(out_f32, g_, tt), wg, rps,
                          ALU.mult, ALU.mult)
    block.vector(vbody)
    assert c.vck.n == vn, (c.vck.n, vn)
    return dict(done_v=c.vck.n)



import numpy as np
from contextlib import ExitStack

import concourse.bass as bass
import concourse.mybir as mybir




def emit_resadd(c, block, src_bf, res_dram, T, tag, dest_sb=None,
                dest_dram=None, in_ready=None, ld_gate=None):
    n_t = T // 512
    assert n_t <= 2
    rsl = c.sb(tag + "_rsl", [128, 2 * 16 * 512], F32)
    tmpf = c.sb(tag + "_tmp", [128, 512], F32)
    ldr = [c.ds(tag + f"_ld{i}") for i in range(2)]
    st = c.ds(tag + "_st")
    stn = [0]
    rv = res_dram.rearrange("(g p) t -> p g t", p=128)
    dv = dest_dram.rearrange("(g p) t -> p g t", p=128) \
        if dest_dram is not None else None
    in_place = dest_sb is None

    v0 = c.vck.n
    v_add = {}
    vn = v0
    for tt in range(n_t):
        for g in range(16):
            vn += 2
            v_add[(tt, g)] = vn

    @block.sync
    def _(sync):
        if ld_gate is not None:
            for sem_, val_ in ld_gate:
                sync.wait_ge(sem_, val_)
        for tt in range(n_t):
            slot = tt % 2
            if tt >= 2:
                sync.wait_ge(c.vck.sem, v_add[(tt - 2, 15)])
            ldr[slot].inc(sync.dma_start(
                out=rsl[:, slot * 8192:(slot + 1) * 8192]
                .rearrange("p (g t) -> p g t", g=16),
                in_=rv[:, :, tt * 512:(tt + 1) * 512]))

    @block.vector
    def _(veng):
        v = c.V(veng)
        if in_ready is not None:
            v.wait_ge(in_ready[0], in_ready[1])
        for tt in range(n_t):
            slot = tt % 2
            v.wait_ge(ldr[slot].sem, ldr[slot].at(16 * (tt // 2 + 1)))
            for g in range(16):
                sb_src = src_bf[:, g * T + tt * 512: g * T + (tt + 1) * 512]
                rtile = rsl[:, slot * 8192 + g * 512:
                            slot * 8192 + (g + 1) * 512]
                v.copy(tmpf[:], sb_src)
                dst = rtile if in_place else dest_sb[
                    :, g * T + tt * 512: g * T + (tt + 1) * 512]
                v.tt(dst, rtile, tmpf[:], ALU.add)
        assert c.vck.n == vn

    if dest_dram is not None:
        pass

        @block.sync
        def _(sync):
            for tt in range(n_t):
                slot = tt % 2
                sync.wait_ge(c.vck.sem, v_add[(tt, 15)])
                src = (rsl[:, slot * 8192:(slot + 1) * 8192]
                       .rearrange("p (g t) -> p g t", g=16)) if in_place \
                    else dest_sb[:].rearrange("p (g t) -> p g t", g=16)[
                        :, :, tt * 512:(tt + 1) * 512]
                st.inc(sync.dma_start(
                    out=dv[:, :, tt * 512:(tt + 1) * 512], in_=src))
                stn[0] += 16

    return dict(done_v=vn, st_sem=st.sem, st_val=st.at(stn[0]))


def emit_rope2(c, block, pos_dram, T, tag="rp2", scale_q=None):
    cosq = c.sb(tag + "_cq", [128, T], BF16)
    sinq = c.sb(tag + "_sq", [128, T], BF16)
    cosk = c.sb(tag + "_ck", [128, T], BF16)
    sink = c.sb(tag + "_sk", [128, T], BF16)
    TWO_PI = float(2 * np.pi)
    possem = c.ds(tag + "_pos")
    sq = 1.0 if scale_q is None else float(scale_q)

    es = ExitStack()
    c.push_scope(es)
    with es:
        pos = c.sb(tag + "_p", [1, T], F32)
        posb = c.sb(tag + "_pb", [128, T], F32)
        invf = c.sb(tag + "_if", [128, 1], F32)
        ii = c.sb(tag + "_ii", [128, 1], F32)
        msk = c.sb(tag + "_mk", [128, 1], F32)
        ang = c.sb(tag + "_an", [128, T], F32)
        tmp = c.sb(tag + "_tm", [128, T], F32)
        tmpi = c.sb(tag + "_ti", [128, T], I32)
        frac = c.sb(tag + "_fr", [128, T], F32)
        cf = c.sb(tag + "_cf", [128, T], F32)
        sf = c.sb(tag + "_sf", [128, T], F32)
        onesb = c.sb(tag + "_o1", [1, 128], F32)
        ps_b = c.ps(tag + "_psb", [128, 512])

        @block.sync
        def _(sync):
            possem.inc(sync.dma_start(out=pos[:], in_=pos_dram))

        @block.gpsimd
        def _(geng):
            g = c.G(geng)
            g.iota(ii[:], [[1, 1]], channel_multiplier=1)
        g_done = c.gck.n

        @block.vector
        def _(veng):
            v = c.V(veng)
            v.wait_ge(c.gck.sem, g_done)
            v.memset(onesb[:], 1.0)
            v.ts(msk[:], ii[:], 64.0, None, ALU.is_ge)
            v.stt(ii[:], msk[:], -64.0, ii[:], ALU.mult, ALU.add)
        v_done = c.vck.n

        @block.scalar
        def _(seng):
            s = c.S(seng)
            s.wait_ge(c.vck.sem, v_done)
            s.act(invf[:], ii[:], AF.Exp, scale=float(-np.log(10000.0) / 64.0))
        s_done = c.sck.n

        sp_vals = []
        v_pb = [v_done + 1 + i for i in range(T // 512)]

        @block.tensor
        def _(t):
            t.wait_ge(c.vck.sem, v_done)
            t.wait_ge(possem.sem, possem.at(16))
            for i in range(T // 512):
                if i > 0:
                    t.wait_ge(c.vck.sem, v_pb[i - 1])
                mm = t.matmul(ps_b[:, 0:512], onesb[0:1, :],
                              pos[0:1, i * 512:(i + 1) * 512],
                              start=True, stop=True)
                sp_vals.append(c.pe_sig(mm))

        @block.vector
        def _(veng):
            v = c.V(veng)
            for i in range(T // 512):
                v.wait_ge(c.sp, sp_vals[i])
                v.copy(posb[:, i * 512:(i + 1) * 512], ps_b[:, 0:512])
                assert c.vck.n == v_pb[i]

        def range_reduce(v, shift):
            v.ts(tmp[:], ang[:], float(1.0 / TWO_PI), shift, ALU.mult, ALU.add)
            v.copy(tmpi[:], tmp[:])
            v.copy(frac[:], tmpi[:])
            v.tt(frac[:], tmp[:], frac[:], ALU.subtract)
            v.ts(tmp[:], frac[:], 0.5, None, ALU.is_gt)
            v.tt(frac[:], frac[:], tmp[:], ALU.subtract)
            v.ts(tmp[:], frac[:], -0.5, None, ALU.is_lt)
            v.tt(frac[:], frac[:], tmp[:], ALU.add)

        @block.vector
        def _(veng):
            v = c.V(veng)
            v.wait_ge(c.sck.sem, s_done)
            v.ts(ang[:], posb[:], invf[:, 0:1], None, ALU.mult)
            range_reduce(v, 0.0)
        vs_done = c.vck.n

        @block.scalar
        def _(seng):
            s = c.S(seng)
            s.wait_ge(c.vck.sem, vs_done)
            s.act(sf[:], frac[:], AF.Sin, scale=TWO_PI)
        sk_done = c.sck.n

        @block.vector
        def _(veng):
            v = c.V(veng)
            v.wait_ge(c.sck.sem, sk_done)
            range_reduce(v, 0.25)
        vc_done = c.vck.n

        @block.scalar
        def _(seng):
            s = c.S(seng)
            s.wait_ge(c.vck.sem, vc_done)
            s.act(cf[:], frac[:], AF.Sin, scale=TWO_PI)
        sc_done2 = c.sck.n

        @block.vector
        def _(veng):
            v = c.V(veng)
            v.wait_ge(c.sck.sem, sc_done2)
            v.ts(cosq[:], cf[:], float(sq), None, ALU.mult)
            v.ts(sinq[0:64, :], sf[0:64, :], float(-sq), None, ALU.mult)
            v.ts(sinq[64:128, :], sf[64:128, :], float(sq), None, ALU.mult)
            v.copy(cosk[:], cf[:])
            v.ts(sink[0:64, :], sf[0:64, :], -1.0, None, ALU.mult)
            v.copy(sink[64:128, :], sf[64:128, :])
        ready_v = c.vck.n
    c.pop_scope()
    return dict(cosq=cosq, sinq=sinq, cosk=cosk, sink=sink, ready_v=ready_v)


def emit_l1_attn2(c, block, x_bf, wattn_b, wproj_b, out_dram, T, pos_dram,
                  qb, kv_comm, tag="la2", in_ready=None, mask_dram=None):
    n_t = T // 512
    ident = emit_identity(c, block, tag + "id")
    rope = emit_rope2(c, block, pos_dram, T, tag + "rp",
                      scale_q=float(1.0 / np.sqrt(128.0)))
    ws_a = WStream(c, tag + "wa", wattn_b, 16, 128, depth=2)
    qkv_q = c.sb(tag + "_q", [128, 16 * T], BF16)
    swp = c.sb(tag + "_sw", [128, 2 * T], BF16)
    yT = c.sb(tag + "_yT", [128, 16 * T], BF16)
    swsem = [c.ds(tag + f"_sw{i}") for i in range(2)]

    es_kv = ExitStack()
    c.push_scope(es_kv)
    with es_kv:
        qkv_kv = c.sb(tag + "_kv", [128, 8 * T], BF16)
        ps_qkv = c.ps(tag + "_psq", [128, 2 * 512])

        sp0, v0, s0 = c.spn, c.vck.n, c.sck.n
        qkv_sp, s_ev = {}, {}
        spn, sn = sp0, s0
        for m in range(24):
            for tt in range(n_t):
                spn += 1
                qkv_sp[(m, tt)] = spn
        for m in range(24):
            for tt in range(n_t):
                sn += 1
                s_ev[(m, tt)] = sn
        vn = v0
        v_rope = {}
        for r in range(20):
            vn += 3
            v_rope[r] = vn
        v_rope_done = vn

        def head_ap(r):
            if r < 16:
                return qkv_q[:, r * T:(r + 1) * T]
            return qkv_kv[:, (r - 16) * T:(r - 15) * T]

        @block.tensor
        def _(t):
            if in_ready is not None:
                t.wait_ge(in_ready[0], in_ready[1])
            for m in range(24):
                ws_a.wait(t, m)
                for tt in range(n_t):
                    idx = m * n_t + tt
                    if idx >= 2:
                        t.wait_ge(c.sck.sem, s_ev[divmod(idx - 2, n_t)])
                    for k in range(16):
                        mm = t.matmul(
                            ps_qkv[:, (idx % 2) * 512:(idx % 2) * 512 + 512],
                            ws_a.tile(m, k),
                            x_bf[:, k * T + tt * 512: k * T + (tt + 1) * 512],
                            start=(k == 0), stop=(k == 15))
                    c.pe_sig(mm)
        assert c.spn == spn

        @block.gpsimd
        def _(g):
            g.wait_ge(c.vck.sem, rope["ready_v"])
            for m in range(24):
                ws_a.emit_one(g, m, lambda mm_: (c.sp, qkv_sp[(mm_, n_t - 1)]))

        @block.scalar
        def _(seng):
            s = c.S(seng)
            for m in range(24):
                dst = head_ap(m) if m < 20 else \
                    qkv_kv[:, (m - 16) * T:(m - 15) * T]
                for tt in range(n_t):
                    idx = m * n_t + tt
                    s.wait_ge(c.sp, qkv_sp[(m, tt)])
                    s.act(dst[:, tt * 512:(tt + 1) * 512],
                          ps_qkv[:, (idx % 2) * 512:(idx % 2) * 512 + 512],
                          AF.Copy)
        assert c.sck.n == sn

        @block.sync
        def _(sync):
            for r in range(20):
                slot = r % 2
                if r >= 2:
                    sync.wait_ge(c.vck.sem, v_rope[r - 2])
                for tt in range(n_t):
                    sync.wait_ge(c.sck.sem, s_ev[(r, tt)])
                src = head_ap(r)
                swsem[slot].inc(sync.dma_start(
                    out=swp[0:64, slot * T:(slot + 1) * T],
                    in_=src[64:128, :]))
                swsem[slot].inc(sync.dma_start(
                    out=swp[64:128, slot * T:(slot + 1) * T],
                    in_=src[0:64, :]))

        @block.vector
        def _(veng):
            v = c.V(veng)
            v.wait_ge(c.vck.sem, rope["ready_v"])
            for r in range(20):
                slot = r % 2
                v.wait_ge(swsem[slot].sem, swsem[slot].at(32 * (r // 2 + 1)))
                dst = head_ap(r)
                sw = swp[:, slot * T:(slot + 1) * T]
                if r < 16:
                    ct, st_ = rope["cosq"], rope["sinq"]
                else:
                    ct, st_ = rope["cosk"], rope["sink"]
                v.tt(dst, dst, ct[:, 0:T], ALU.mult)
                v.tt(sw, sw, st_[:, 0:T], ALU.mult)
                v.tt(dst, dst, sw, ALU.add)
        assert c.vck.n == v_rope_done

        k_use, v_use, Tk, kv_ready = kv_comm(
            c, block, qkv_kv, v_rope_done, s_ev)
    c.pop_scope()

    @block.gpsimd
    def _(g):
        g.wait_ge(c.sck.sem, sn)
        g.wait_ge(c.vck.sem, v_rope_done)
        g.wait_ge(kv_ready[0], kv_ready[1])

    es_m = ExitStack()
    c.push_scope(es_m)
    with es_m:
        emit_mha(c, block, qkv_q[:], k_use, v_use, yT[:], T, Tk, qb,
                 n_heads=16, kv_of_head=lambda h: h // 4, ident=ident,
                 tag=tag + "m", kv_ready=kv_ready,
                 extra_q=(c.vck.sem, v_rope_done), mask_dram=mask_dram)
    c.pop_scope()
    mha_v = c.vck.n

    es_p = ExitStack()
    c.push_scope(es_p)
    with es_p:
        ws_p = WStream(c, tag + "wp", wproj_b, 16, 128, depth=2)
        stage = c.sb(tag + "_st", [128, 2 * 512], BF16)
        ps_pj = c.ps(tag + "_psp", [128, 2 * 512])
        stsem = [c.ds(tag + f"_sts{i}") for i in range(2)]

        sp1 = c.spn
        wp_sp, s_pe = {}, {}
        spn2, sn2 = sp1, c.sck.n
        for m in range(16):
            for tt in range(n_t):
                spn2 += 1
                wp_sp[(m, tt)] = spn2
        for m in range(16):
            for tt in range(n_t):
                sn2 += 1
                s_pe[(m, tt)] = sn2

        @block.tensor
        def _(t):
            t.wait_ge(c.vck.sem, mha_v)
            for m in range(16):
                ws_p.wait(t, m)
                for tt in range(n_t):
                    idx = m * n_t + tt
                    if idx >= 2:
                        t.wait_ge(c.sck.sem, s_pe[divmod(idx - 2, n_t)])
                    for k in range(16):
                        mm = t.matmul(
                            ps_pj[:, (idx % 2) * 512:(idx % 2) * 512 + 512],
                            ws_p.tile(m, k),
                            yT[:, k * T + tt * 512: k * T + (tt + 1) * 512],
                            start=(k == 0), stop=(k == 15))
                    c.pe_sig(mm)
        assert c.spn == spn2

        @block.gpsimd
        def _(g):
            g.wait_ge(c.vck.sem, mha_v)
            for m in range(16):
                ws_p.emit_one(g, m, lambda mm_: (c.sp, wp_sp[(mm_, n_t - 1)]))

        stv = [0]

        @block.scalar
        def _(seng):
            s = c.S(seng)
            for m in range(16):
                for tt in range(n_t):
                    idx = m * n_t + tt
                    s.wait_ge(c.sp, wp_sp[(m, tt)])
                    if idx >= 2:
                        s.wait_ge(stsem[idx % 2].sem,
                                  stsem[idx % 2].at(16 * (idx // 2)))
                    s.act(stage[:, (idx % 2) * 512:(idx % 2) * 512 + 512],
                          ps_pj[:, (idx % 2) * 512:(idx % 2) * 512 + 512],
                          AF.Copy)
                    s.wait_ge(c.sck.sem, s_pe[(m, tt)])
                    stsem[idx % 2].inc(s.e.dma_start(
                        out=out_dram[:, m, tt * 512:(tt + 1) * 512],
                        in_=stage[:, (idx % 2) * 512:(idx % 2) * 512 + 512]))
                    stv[0] += 16
        assert c.sck.n == sn2
    c.pop_scope()
    return dict(st=[(stsem[i].sem, stsem[i].at(stv[0] // 2))
                    for i in range(2)])


def emit_mamba8(c, block, hn_all_d, weights, scratch, tag="m8",
                in_ready=None, dbc_comm=None):
    NB, CH, Tb, nM = 4, 512, 2048, 4
    TA = NB * Tb
    n_tt = TA // 512

    hv = hn_all_d.rearrange("(r g p tb ti) -> r p g tb ti",
                            r=8, g=16, p=128, tb=8, ti=128)
    zv = scratch["z_page"].rearrange("(m p) t -> p m t", p=128)
    xcv = scratch["xc_page"].rearrange("(m p) t -> p m t", p=128)
    yv = scratch["y_page"].rearrange("(m p) t -> p m t", p=128)

    BM = lambda bm: divmod(bm, nM)
    QQ = lambda qi: (qi // 64, (qi // 16) % 4, qi % 16)

    cw = c.sb(tag + "_cw", [128, 16], F32)
    cb = c.sb(tag + "_cb", [128, 4], F32)
    dtb = c.sb(tag + "_dtb", [128, 4], F32)
    A2 = c.sb(tag + "_A2", [128, 64], F32)
    Am = c.sb(tag + "_Am", [128, 64], F32)
    Dp = c.sb(tag + "_Dp", [128, 4], F32)
    selu = c.sb(tag + "_selu", [128, 16 * 128], BF16)
    sel2t = c.sb(tag + "_sel2", [128, 16 * 128], BF16)
    selb = c.sb(tag + "_selb", [16, 128], BF16)
    diagd = c.sb(tag + "_dgd", [128, 4 * 128], BF16)
    ld0 = c.ds(tag + "_ld0")

    @block.sync
    def _(sync):
        for nm_, dst_ in (("conv_w", cw), ("conv_b", cb), ("dtb", dtb),
                          ("A2_d", A2), ("Dp", Dp)):
            ld0.inc(sync.dma_start(out=dst_[:], in_=weights[nm_]))

    @block.scalar
    def _(seng):
        s = c.S(seng)
        s.wait_ge(ld0.sem, ld0.at(16 * 5))
        s.act(Am[:], A2[:], AF.Exp)
    sA = c.sck.n

    @block.vector
    def _(veng):
        v = c.V(veng)
        v.wait_ge(c.sck.sem, sA)
        v.ts(Am[:], Am[:], -1.0, None, ALU.mult)
    vA = c.vck.n

    es0 = ExitStack()
    c.push_scope(es0)
    with es0:
        iof = c.sb(tag + "_iof", [128, 128], F32)
        iop = c.sb(tag + "_iop", [128, 128], F32)
        iopi = c.sb(tag + "_iopi", [128, 128], I32)
        t0i = c.sb(tag + "_t0i", [128, 128], I32)
        fd16 = c.sb(tag + "_fd16", [128, 128], F32)
        pd16 = c.sb(tag + "_pd16", [128, 128], F32)
        tsel = c.sb(tag + "_tsel", [128, 128], F32)
        iofs = c.sb(tag + "_iofs", [16, 128], F32)
        iops = c.sb(tag + "_iops", [16, 128], F32)
        t2i = c.sb(tag + "_t2i", [16, 128], I32)
        t2f = c.sb(tag + "_t2f", [16, 128], F32)
        identb = c.sb(tag + "_idb", [128, 128], BF16)

        @block.gpsimd
        def _(geng):
            g = c.G(geng)
            if in_ready is not None:
                geng.wait_ge(in_ready[0], in_ready[1])
            g.iota(iof[:], [[1, 128]], channel_multiplier=0)
            g.iota(iop[:], [[0, 128]], channel_multiplier=1)
            g.iota(iopi[:], [[0, 128]], channel_multiplier=1)
            g.iota(iofs[:], [[1, 128]], channel_multiplier=0)
            g.iota(iops[:], [[0, 128]], channel_multiplier=1)
        g_sel = c.gck.n

        @block.vector
        def _(veng):
            v = c.V(veng)
            v.wait_ge(c.gck.sem, g_sel)
            v.copy(t0i[:], iof[:])
            v.ts(t0i[:], t0i[:], 4, None, ALU.logical_shift_right)
            v.copy(fd16[:], t0i[:])
            v.copy(t0i[:], iopi[:])
            v.ts(t0i[:], t0i[:], 4, None, ALU.logical_shift_right)
            v.copy(pd16[:], t0i[:])
            for s_ in range(16):
                v.ts(tsel[:], fd16[:], float(8 * s_), None, ALU.add)
                v.tt(selu[:, s_ * 128:(s_ + 1) * 128], tsel[:], iop[:],
                     ALU.is_equal)
                v.ts(tsel[:], pd16[:], float(8 * s_), None, ALU.add)
                v.tt(sel2t[:, s_ * 128:(s_ + 1) * 128], tsel[:], iof[:],
                     ALU.is_equal)
            v.copy(t2i[:], iofs[:])
            v.ts(t2i[:], t2i[:], 4, None, ALU.logical_shift_right)
            v.copy(t2f[:], t2i[:])
            v.ts(t2f[:], t2f[:], -16.0, None, ALU.mult)
            v.tt(t2f[:], iofs[:], t2f[:], ALU.add)
            v.tt(selb[:], t2f[:], iops[:], ALU.is_equal)
            v.tt(identb[:], iof[:], iop[:], ALU.is_equal)
        v_sel = c.vck.n

        @block.scalar
        def _(seng):
            s = c.S(seng)
            s.wait_ge(c.vck.sem, v_sel)
            s.wait_ge(ld0.sem, ld0.at(16 * 5))
            for M in range(nM):
                s.act(diagd[:, M * 128:(M + 1) * 128], identb[:], AF.Copy,
                      scale=Dp[:, M:M + 1])
        s_dgd = c.sck.n
    c.pop_scope()

    esA = ExitStack()
    c.push_scope(esA)
    with esA:
        ws_x = WStream(c, tag + "wx", weights["wx_b"], 16, 128, depth=4)
        ws_z = WStream(c, tag + "wz", weights["wz_b"], 16, 128, depth=4)
        hns = c.sb(tag + "_hns", [128, 2 * 16 * 512], BF16)
        xpad = c.sb(tag + "_xpd", [128, 4 * 2052], F32)
        convo = c.sb(tag + "_cvo", [128, Tb], F32)
        xcs = c.sb(tag + "_xcs", [128, 2 * Tb], BF16)
        zb = c.sb(tag + "_zb", [128, 2 * 512], BF16)
        ps_a = c.ps(tag + "_psa", [128, 2 * 512])
        ps_b = c.ps(tag + "_psb", [128, 2 * 512])
        ldh = [c.ds(tag + f"_ldh{i}") for i in range(2)]
        zst = [c.ds(tag + f"_zst{i}") for i in range(2)]
        xcst = [c.ds(tag + f"_xcst{i}") for i in range(2)]

        sp0 = c.spn
        x_sp, z_sp = {}, {}
        spn = sp0
        for tt in range(n_tt):
            for m in range(nM):
                spn += 1
                x_sp[(tt, m)] = spn
            for m in range(nM):
                spn += 1
                z_sp[(tt, m)] = spn
        s0 = c.sck.n
        sn = s0
        s_evx, s_evz, s_silu = {}, {}, {}
        for tt in range(n_tt):
            for m in range(nM):
                sn += 1
                s_evx[(tt, m)] = sn
            for m in range(nM):
                sn += 1
                s_evz[(tt, m)] = sn
            if tt % 4 == 3:
                for m in range(nM):
                    sn += 1
                    s_silu[(tt // 4, m)] = sn
        v0 = c.vck.n
        vn = v0 + 1
        v_conv = {}
        for b in range(NB):
            for m in range(nM):
                vn += 4
                v_conv[(b, m)] = vn

        @block.vector
        def _(veng):
            v = c.V(veng)
            v.memset(xpad[:].rearrange("p (m t) -> p m t", m=4)[:, :, 0:3],
                     0.0)

        @block.sync
        def _(sync):
            if in_ready is not None:
                sync.wait_ge(in_ready[0], in_ready[1])
            for tt in range(n_tt):
                slot = tt % 2
                b, q4 = tt // 4, tt % 4
                if tt >= 2:
                    sync.wait_ge(c.sp, z_sp[(tt - 2, nM - 1)])
                ov = hns[:, slot * 8192:(slot + 1) * 8192].rearrange(
                    "p (g c) -> p g c", g=16)
                for s_ in range(2):
                    for tb in range(2):
                        ldh[slot].inc(sync.dma_start(
                            out=ov[:, :, tb * 256 + s_ * 128:
                                   tb * 256 + s_ * 128 + 128],
                            in_=hv[2 * b + s_][:, :, q4 * 2 + tb, :]))

        @block.gpsimd
        def _(g):
            g.wait_ge(c.gck.sem, g_sel)
            g.wait_ge(c.vck.sem, v_sel)
            g.wait_ge(c.sck.sem, s_dgd)
            for m in range(nM):
                ws_x.emit_one(g, m, None)
                ws_z.emit_one(g, m, None)

        @block.tensor
        def _(t):
            t.wait_ge(c.vck.sem, v0 + 1)
            for m in range(nM):
                ws_x.wait(t, m)
                ws_z.wait(t, m)
            for tt in range(n_tt):
                slot = tt % 2
                b, q4 = tt // 4, tt % 4
                t.wait_ge(ldh[slot].sem,
                          ldh[slot].at(16 * 4 * (tt // 2 + 1)))
                for m in range(nM):
                    gi = tt * nM + m
                    if gi >= 2:
                        t.wait_ge(c.sck.sem, s_evx[divmod(gi - 2, nM)])
                    if q4 == 0 and b > 0:
                        t.wait_ge(c.vck.sem, v_conv[(b - 1, m)])
                    for k in range(16):
                        mm = t.matmul(
                            ps_a[:, (gi % 2) * 512:(gi % 2) * 512 + 512],
                            ws_x.tile(m, k),
                            hns[:, slot * 8192 + k * 512:
                                slot * 8192 + (k + 1) * 512],
                            start=(k == 0), stop=(k == 15))
                    c.pe_sig(mm)
                for m in range(nM):
                    gi = tt * nM + m
                    if gi >= 2:
                        t.wait_ge(c.sck.sem, s_evz[divmod(gi - 2, nM)])
                    for k in range(16):
                        mm = t.matmul(
                            ps_b[:, (gi % 2) * 512:(gi % 2) * 512 + 512],
                            ws_z.tile(m, k),
                            hns[:, slot * 8192 + k * 512:
                                slot * 8192 + (k + 1) * 512],
                            start=(k == 0), stop=(k == 15))
                    c.pe_sig(mm)
        assert c.spn == spn

        @block.scalar
        def _(seng):
            s = c.S(seng)
            for tt in range(n_tt):
                b, q4 = tt // 4, tt % 4
                for m in range(nM):
                    gi = tt * nM + m
                    s.wait_ge(c.sp, x_sp[(tt, m)])
                    s.act(xpad[:, m * 2052 + 3 + q4 * 512:
                               m * 2052 + 3 + (q4 + 1) * 512],
                          ps_a[:, (gi % 2) * 512:(gi % 2) * 512 + 512],
                          AF.Copy)
                for m in range(nM):
                    gi = tt * nM + m
                    s.wait_ge(c.sp, z_sp[(tt, m)])
                    if gi >= 2:
                        s.wait_ge(zst[gi % 2].sem,
                                  zst[gi % 2].at(16 * (gi // 2)))
                    s.act(zb[:, (gi % 2) * 512:(gi % 2) * 512 + 512],
                          ps_b[:, (gi % 2) * 512:(gi % 2) * 512 + 512],
                          AF.Copy)
                    s.wait_ge(c.sck.sem, s_evz[(tt, m)])
                    zst[gi % 2].inc(s.e.dma_start(
                        out=zv[:, m, tt * 512:(tt + 1) * 512],
                        in_=zb[:, (gi % 2) * 512:(gi % 2) * 512 + 512]))
                if q4 == 3:
                    for m in range(nM):
                        bm = b * nM + m
                        s.wait_ge(c.vck.sem, v_conv[(b, m)])
                        if bm >= 2:
                            s.wait_ge(xcst[bm % 2].sem,
                                      xcst[bm % 2].at(16 * (bm // 2)))
                        s.act(xcs[:, (bm % 2) * Tb:(bm % 2) * Tb + Tb],
                              convo[:], AF.Silu, bias=cb[:, m:m + 1])
                        s.wait_ge(c.sck.sem, s_silu[(b, m)])
                        xcst[bm % 2].inc(s.e.dma_start(
                            out=xcv[:, m, b * Tb:(b + 1) * Tb],
                            in_=xcs[:, (bm % 2) * Tb:(bm % 2) * Tb + Tb]))
        assert c.sck.n == sn

        @block.vector
        def _(veng):
            v = c.V(veng)
            for b in range(NB):
                for m in range(nM):
                    for q4 in range(4):
                        v.wait_ge(c.sck.sem, s_evx[(b * 4 + q4, m)])
                    if b * nM + m >= 1:
                        v.wait_ge(c.sck.sem, s_silu[BM(b * nM + m - 1)])
                    base = m * 2052
                    v.ts(convo[:], xpad[:, base: base + Tb],
                         cw[:, 4 * m:4 * m + 1], None, ALU.mult)
                    for i in (1, 2, 3):
                        v.stt(convo[:], xpad[:, base + i: base + i + Tb],
                              cw[:, 4 * m + i:4 * m + i + 1], convo[:],
                              ALU.mult, ALU.add)
        assert c.vck.n == vn
    c.pop_scope()

    esB = ExitStack()
    c.push_scope(esB)
    with esB:
        ws_p = WStream(c, tag + "wp", weights["xpj_b"], 4, 80, depth=2)
        xcr = c.sb(tag + "_xcr", [128, 2 * 4 * 512], BF16)
        dbcs = c.sb(tag + "_dbcs", [128, 2 * 512], F32)
        ps_p = c.ps(tag + "_psp", [128, 2 * 512])
        ldxc = [c.ds(tag + f"_lxc{i}") for i in range(2)]
        dbst = [c.ds(tag + f"_dbst{i}") for i in range(2)]

        spB = c.spn
        p_sp = {}
        spn2 = spB
        for tt in range(n_tt):
            for mo in range(2):
                spn2 += 1
                p_sp[(tt, mo)] = spn2
        sB = c.sck.n
        s_evp = {}
        sn2 = sB
        for tt in range(n_tt):
            for mo in range(2):
                sn2 += 1
                s_evp[(tt, mo)] = sn2

        @block.sync
        def _(sync):
            sync.wait_ge(xcst[0].sem, xcst[0].at(16 * 8))
            sync.wait_ge(xcst[1].sem, xcst[1].at(16 * 8))
            for tt in range(n_tt):
                slot = tt % 2
                if tt >= 2:
                    sync.wait_ge(c.sp, p_sp[(tt - 2, 1)])
                ldxc[slot].inc(sync.dma_start(
                    out=xcr[:, slot * 2048:(slot + 1) * 2048].rearrange(
                        "p (m t) -> p m t", m=4),
                    in_=xcv[:, :, tt * 512:(tt + 1) * 512]))

        @block.gpsimd
        def _(g):
            g.wait_ge(c.sp, z_sp[(n_tt - 1, nM - 1)])
            for mo in range(2):
                ws_p.emit_one(g, mo, None)

        @block.tensor
        def _(t):
            ws_p.wait(t, 0)
            ws_p.wait(t, 1)
            for tt in range(n_tt):
                slot = tt % 2
                t.wait_ge(ldxc[slot].sem, ldxc[slot].at(16 * (tt // 2 + 1)))
                for mo in range(2):
                    gi = tt * 2 + mo
                    if gi >= 2:
                        t.wait_ge(c.sck.sem, s_evp[divmod(gi - 2, 2)])
                    for k in range(4):
                        mm = t.matmul(
                            ps_p[0:80, (gi % 2) * 512:(gi % 2) * 512 + 512],
                            ws_p.tile(mo, k),
                            xcr[:, slot * 2048 + k * 512:
                                slot * 2048 + (k + 1) * 512],
                            start=(k == 0), stop=(k == 3))
                    c.pe_sig(mm)
        assert c.spn == spn2

        @block.scalar
        def _(seng):
            s = c.S(seng)
            for tt in range(n_tt):
                for mo in range(2):
                    gi = tt * 2 + mo
                    s.wait_ge(c.sp, p_sp[(tt, mo)])
                    if gi >= 2:
                        s.wait_ge(dbst[gi % 2].sem,
                                  dbst[gi % 2].at(16 * (gi // 2)))
                    s.act(dbcs[0:80, (gi % 2) * 512:(gi % 2) * 512 + 512],
                          ps_p[0:80, (gi % 2) * 512:(gi % 2) * 512 + 512],
                          AF.Copy)
                    s.wait_ge(c.sck.sem, s_evp[(tt, mo)])
                    dbst[gi % 2].inc(s.e.dma_start(
                        out=scratch["dbc_part"][mo * 80:(mo + 1) * 80,
                                                tt * 512:(tt + 1) * 512],
                        in_=dbcs[0:80, (gi % 2) * 512:(gi % 2) * 512 + 512]))
        assert c.sck.n == sn2
    c.pop_scope()

    db_rdy = [(dbst[0].sem, dbst[0].at(16 * 16)),
              (dbst[1].sem, dbst[1].at(16 * 16))]
    if dbc_comm is None:
        dbc_co, dbc_ready = scratch["dbc_part"], db_rdy
    else:
        dbc_co, dbc_ready = dbc_comm(c, block, db_rdy)

    esC = ExitStack()
    c.push_scope(esC)
    with esC:
        ws_dt = WStream(c, tag + "wdt", weights["wdt_b"], 1, 128, depth=4)
        dtin = c.sb(tag + "_dtin", [128, TA], BF16)
        bcr = c.sb(tag + "_bcr", [16, Tb], BF16)
        ccr = c.sb(tag + "_ccr", [16, Tb], BF16)
        Brep = c.sb(tag + "_Br", [128, Tb], F32)
        Crep = c.sb(tag + "_Cr", [128, Tb], F32)
        sp1 = c.sb(tag + "_sp1", [128, Tb], F32)
        sp2 = c.sb(tag + "_sp2", [128, Tb], F32)
        dt_bf = c.sb(tag + "_dtbf", [128, Tb], BF16)
        um = c.sb(tag + "_um", [128, Tb], BF16)
        dAb = c.sb(tag + "_dA", [128, 2 * Tb], F32)
        uB = c.sb(tag + "_uB", [128, 2 * Tb], F32)
        so = c.sb(tag + "_so", [128, 2 * Tb], F32)
        scb = c.sb(tag + "_scb", [128, 2 * Tb], BF16)
        xcin = c.sb(tag + "_xci", [128, 2 * Tb], BF16)
        zin = c.sb(tag + "_zin", [128, 2 * Tb], BF16)
        zs = c.sb(tag + "_zs", [128, 2 * 512], F32)
        ybst = c.sb(tag + "_ybs", [128, 2 * Tb], BF16)
        ps_r1 = c.ps(tag + "_pr1", [128, 2 * 512])
        ps_r2 = c.ps(tag + "_pr2", [128, 2 * 512])
        ps_y = c.ps(tag + "_py", [128, 4 * 512])
        ldin = c.ds(tag + "_ldin", sw=True)
        ldbc = c.ds(tag + "_ldbc", sw=True)
        ldx2 = [c.ds(tag + f"_lx2{i}") for i in range(2)]
        ldz2 = [c.ds(tag + f"_lz2{i}") for i in range(2)]
        sty = [c.ds(tag + f"_sty{i}") for i in range(2)]

        spC = c.spn
        bc_sp, dt_sp, r1_sp, r2_sp, y_sp, dd_sp = {}, {}, {}, {}, {}, {}
        spn3 = spC
        sC, vC, gC = c.sck.n, c.vck.n, c.gck.n
        sn3, vn3, gn3 = sC, vC, gC
        s_bc, s_ar, s_spl, s_dA, s_zs = {}, {}, {}, {}, {}
        v_dt, v_um, v_uB, v_scan, v_y = {}, {}, {}, {}, {}
        g_sc = {}
        for b in range(NB):
            for r in range(2):
                for tt in range(4):
                    spn3 += 1
                    bc_sp[(b, r, tt)] = spn3
                    sn3 += 1
                    s_bc[(b, r, tt)] = sn3
            for M in range(nM):
                for tt in range(4):
                    spn3 += 1
                    dt_sp[(b, M, tt)] = spn3
                for tt in range(4):
                    sn3 += 2
                    s_ar[(b, M, tt)] = sn3
                sn3 += 2
                s_spl[(b, M)] = sn3
                vn3 += 1
                v_dt[(b, M)] = vn3
                vn3 += 1
                v_um[(b, M)] = vn3
                for s_ in range(16):
                    for tt in range(4):
                        spn3 += 1
                        r1_sp[(b, M, s_, tt)] = spn3
                        spn3 += 1
                        r2_sp[(b, M, s_, tt)] = spn3
                    for pr in range(2):
                        sn3 += 1
                        s_dA[(b, M, s_, pr)] = sn3
                        vn3 += 1
                        v_uB[(b, M, s_, pr)] = vn3
                    vn3 += 1
                    v_scan[(b, M, s_)] = vn3
                    gn3 += 1
                    g_sc[(b, M, s_)] = gn3
                    if s_ >= 1:
                        for tt in range(4):
                            spn3 += 1
                            y_sp[(b, M, s_ - 1, tt)] = spn3
                for tt in range(4):
                    spn3 += 1
                    y_sp[(b, M, 15, tt)] = spn3
                for tt in range(4):
                    spn3 += 1
                    dd_sp[(b, M, tt)] = spn3
                    sn3 += 1
                    s_zs[(b, M, tt)] = sn3
                    vn3 += 1
                    v_y[(b, M, tt)] = vn3

        @block.gpsimd
        def _(geng):
            g = c.G(geng)
            geng.wait_ge(dbst[0].sem, dbst[0].at(16 * 16))
            geng.wait_ge(dbst[1].sem, dbst[1].at(16 * 16))
            for M in range(nM):
                ws_dt.emit_one(geng, M, None)
            for dr in dbc_ready:
                geng.wait_ge(dr[0], dr[1])
            ldin.inc(geng.dma_start(out=dtin[:], in_=dbc_co[0:128, :]))
            for b in range(NB):
                if b > 0:
                    geng.wait_ge(c.sp, bc_sp[(b - 1, 1, 3)])
                ldbc.inc(geng.dma_start(
                    out=bcr[:], in_=dbc_co[128:144, b * Tb:(b + 1) * Tb]))
                ldbc.inc(geng.dma_start(
                    out=ccr[:], in_=dbc_co[144:160, b * Tb:(b + 1) * Tb]))
                for M in range(nM):
                    for s_ in range(16):
                        qi = (b * nM + M) * 16 + s_
                        sl = s_ % 2
                        if M == 0 and s_ == 0:
                            g.wait_ge(c.sck.sem, s_bc[(b, 1, 3)])
                        g.wait_ge(c.vck.sem, v_scan[(b, M, s_)])
                        if qi >= 2:
                            g.wait_ge(c.sp, y_sp[QQ(qi - 2) + (3,)])
                        g.tt(scb[:, sl * Tb:(sl + 1) * Tb],
                             so[:, sl * Tb:(sl + 1) * Tb], Crep[:], ALU.mult)
        assert c.gck.n == gn3

        @block.sync
        def _(sync):
            sync.wait_ge(xcst[0].sem, xcst[0].at(16 * 8))
            sync.wait_ge(xcst[1].sem, xcst[1].at(16 * 8))
            sync.wait_ge(zst[0].sem, zst[0].at(16 * 32))
            sync.wait_ge(zst[1].sem, zst[1].at(16 * 32))
            for bm in range(16):
                b, M = BM(bm)
                sl = bm % 2
                if bm >= 2:
                    sync.wait_ge(c.sp, dd_sp[BM(bm - 2) + (3,)])
                ldx2[sl].inc(sync.dma_start(
                    out=xcin[:, sl * Tb:(sl + 1) * Tb],
                    in_=xcv[:, M, b * Tb:(b + 1) * Tb]))
                if bm >= 2:
                    sync.wait_ge(c.sck.sem, s_zs[BM(bm - 2) + (3,)])
                ldz2[sl].inc(sync.dma_start(
                    out=zin[:, sl * Tb:(sl + 1) * Tb],
                    in_=zv[:, M, b * Tb:(b + 1) * Tb]))

        def emit_y(t, b, M, s_):
            bm = b * nM + M
            t.wait_ge(c.gck.sem, g_sc[(b, M, s_)])
            if s_ == 0 and bm > 0:
                for tt in range(4):
                    t.wait_ge(c.vck.sem, v_y[BM(bm - 1) + (tt,)])
            for tt in range(4):
                mm = t.matmul(
                    ps_y[:, tt * 512:(tt + 1) * 512],
                    sel2t[:, s_ * 128:(s_ + 1) * 128],
                    scb[:, (s_ % 2) * Tb + tt * 512:
                        (s_ % 2) * Tb + (tt + 1) * 512],
                    start=(s_ == 0), stop=False)
                c.pe_sig(mm)

        @block.tensor
        def _(t):
            t.wait_ge(ldin.sem, ldin.at(16))
            t.wait_ge(c.vck.sem, v_sel)
            t.wait_ge(c.sck.sem, s_dgd)
            for M in range(nM):
                ws_dt.wait(t, M)
            for b in range(NB):
                t.wait_ge(ldbc.sem, ldbc.at(32 * (b + 1)))
                for r in range(2):
                    src = bcr if r == 0 else ccr
                    for tt in range(4):
                        gi = r * 4 + tt
                        if gi >= 2:
                            t.wait_ge(c.sck.sem, s_bc[(b,) + divmod(gi - 2, 4)])
                        elif b > 0:
                            t.wait_ge(c.vck.sem,
                                      v_uB[(b - 1, nM - 1, 15, 1)])
                        mm = t.matmul(
                            ps_r2[:, (gi % 2) * 512:(gi % 2) * 512 + 512],
                            selb[0:16, :],
                            src[0:16, tt * 512:(tt + 1) * 512],
                            start=True, stop=True)
                        c.pe_sig(mm)
                for M in range(nM):
                    bm = b * nM + M
                    sl = bm % 2
                    for tt in range(4):
                        if tt >= 2:
                            t.wait_ge(c.sck.sem, s_ar[(b, M, tt - 2)])
                        elif bm > 0:
                            t.wait_ge(c.sck.sem,
                                      s_dA[BM(bm - 1) + (15, 1)])
                        mm = t.matmul(
                            ps_r1[:, (tt % 2) * 512:(tt % 2) * 512 + 512],
                            ws_dt.tile(M, 0),
                            dtin[:, b * Tb + tt * 512:
                                 b * Tb + (tt + 1) * 512],
                            start=True, stop=True)
                        c.pe_sig(mm)
                    t.wait_ge(c.vck.sem, v_um[(b, M)])
                    for s_ in range(16):
                        for tt in range(4):
                            if tt >= 2:
                                t.wait_ge(c.sck.sem, s_dA[(b, M, s_, 0)])
                                t.wait_ge(c.vck.sem, v_uB[(b, M, s_, 0)])
                            elif s_ > 0:
                                t.wait_ge(c.sck.sem, s_dA[(b, M, s_ - 1, 1)])
                                t.wait_ge(c.vck.sem, v_uB[(b, M, s_ - 1, 1)])
                            else:
                                t.wait_ge(c.sck.sem, s_bc[(b, 1, 2 + tt)]) \
                                    if tt < 2 else None
                                if bm > 0 and tt < 2:
                                    t.wait_ge(c.vck.sem,
                                              v_uB[BM(bm - 1) + (15, 1)])
                            mm = t.matmul(
                                ps_r1[:, (tt % 2) * 512:(tt % 2) * 512 + 512],
                                selu[:, s_ * 128:(s_ + 1) * 128],
                                dt_bf[:, tt * 512:(tt + 1) * 512],
                                start=True, stop=True)
                            c.pe_sig(mm)
                            mm = t.matmul(
                                ps_r2[:, (tt % 2) * 512:(tt % 2) * 512 + 512],
                                selu[:, s_ * 128:(s_ + 1) * 128],
                                um[:, tt * 512:(tt + 1) * 512],
                                start=True, stop=True)
                            c.pe_sig(mm)
                        if s_ >= 1:
                            emit_y(t, b, M, s_ - 1)
                    emit_y(t, b, M, 15)
                    t.wait_ge(ldx2[sl].sem,
                              ldx2[sl].at(16 * (bm // 2 + 1)))
                    for tt in range(4):
                        mm = t.matmul(
                            ps_y[:, tt * 512:(tt + 1) * 512],
                            diagd[:, M * 128:(M + 1) * 128],
                            xcin[:, sl * Tb + tt * 512:
                                 sl * Tb + (tt + 1) * 512],
                            start=False, stop=True)
                        c.pe_sig(mm)
        assert c.spn == spn3, (c.spn, spn3)

        @block.scalar
        def _(seng):
            s = c.S(seng)
            s.wait_ge(c.gck.sem, c._const_ready)
            for b in range(NB):
                for r in range(2):
                    dst = Brep if r == 0 else Crep
                    if b > 0:
                        if r == 0:
                            s.wait_ge(c.vck.sem,
                                      v_uB[(b - 1, nM - 1, 15, 1)])
                        else:
                            s.wait_ge(c.gck.sem, g_sc[(b - 1, nM - 1, 15)])
                    for tt in range(4):
                        gi = r * 4 + tt
                        s.wait_ge(c.sp, bc_sp[(b, r, tt)])
                        s.act(dst[:, tt * 512:(tt + 1) * 512],
                              ps_r2[:, (gi % 2) * 512:(gi % 2) * 512 + 512],
                              AF.Copy)
                for M in range(nM):
                    bm = b * nM + M
                    sl = bm % 2
                    if bm > 0:
                        s.wait_ge(c.vck.sem, v_dt[BM(bm - 1)])
                    for tt in range(4):
                        s.wait_ge(c.sp, dt_sp[(b, M, tt)])
                        pa = ps_r1[:, (tt % 2) * 512:(tt % 2) * 512 + 512]
                        s.act(sp1[:, tt * 512:(tt + 1) * 512], pa, AF.Abs,
                              bias=dtb[:, M:M + 1])
                        s.act(sp2[:, tt * 512:(tt + 1) * 512], pa, AF.Relu,
                              bias=dtb[:, M:M + 1])
                    s.act(sp1[:], sp1[:], AF.Exp, scale=-1.0)
                    s.act(sp1[:], sp1[:], AF.Ln, bias=c.const_ap(1.0))
                    for s_ in range(16):
                        qi = bm * 16 + s_
                        for pr in range(2):
                            s.wait_ge(c.sp, r1_sp[(b, M, s_, 2 * pr + 1)])
                            if pr == 0 and qi >= 2:
                                s.wait_ge(c.vck.sem, v_scan[QQ(qi - 2)])
                            s.act(dAb[:, (s_ % 2) * Tb + pr * 1024:
                                      (s_ % 2) * Tb + (pr + 1) * 1024],
                                  ps_r1[:, 0:1024],
                                  AF.Exp,
                                  scale=Am[:, M * 16 + s_: M * 16 + s_ + 1])
                            assert c.sck.n == s_dA[(b, M, s_, pr)]
                    for tt in range(4):
                        zidx = bm * 4 + tt
                        s.wait_ge(ldz2[sl].sem,
                                  ldz2[sl].at(16 * (bm // 2 + 1)))
                        if zidx >= 2:
                            pz = divmod(zidx - 2, 4)
                            s.wait_ge(c.vck.sem, v_y[BM(pz[0]) + (pz[1],)])
                        s.act(zs[:, (tt % 2) * 512:(tt % 2) * 512 + 512],
                              zin[:, sl * Tb + tt * 512:
                                  sl * Tb + (tt + 1) * 512], AF.Silu)
                        assert c.sck.n == s_zs[(b, M, tt)]
                    s.wait_ge(c.vck.sem, v_y[(b, M, 3)])
                    sty[sl].inc(s.e.dma_start(
                        out=yv[:, M, b * Tb:(b + 1) * Tb],
                        in_=ybst[:, sl * Tb:(sl + 1) * Tb]))
        assert c.sck.n == sn3, (c.sck.n, sn3)

        @block.vector
        def _(veng):
            v = c.V(veng)
            for b in range(NB):
                for M in range(nM):
                    bm = b * nM + M
                    sl = bm % 2
                    v.wait_ge(c.sck.sem, s_spl[(b, M)])
                    if bm > 0:
                        v.wait_ge(c.sp, r1_sp[BM(bm - 1) + (15, 3)])
                    v.tt(dt_bf[:], sp2[:], sp1[:], ALU.add)
                    assert c.vck.n == v_dt[(b, M)]
                    v.wait_ge(ldx2[sl].sem, ldx2[sl].at(16 * (bm // 2 + 1)))
                    if bm > 0:
                        v.wait_ge(c.sp, r2_sp[BM(bm - 1) + (15, 3)])
                    v.tt(um[:], xcin[:, sl * Tb:(sl + 1) * Tb], dt_bf[:],
                         ALU.mult)
                    assert c.vck.n == v_um[(b, M)]
                    for s_ in range(16):
                        qi = bm * 16 + s_
                        ssl = s_ % 2
                        for pr in range(2):
                            v.wait_ge(c.sp, r2_sp[(b, M, s_, 2 * pr + 1)])
                            v.wait_ge(c.sck.sem, s_bc[(b, 0, 3)])
                            v.tt(uB[:, ssl * Tb + pr * 1024:
                                    ssl * Tb + (pr + 1) * 1024],
                                 ps_r2[:, 0:1024],
                                 Brep[:, pr * 1024:(pr + 1) * 1024], ALU.mult)
                            assert c.vck.n == v_uB[(b, M, s_, pr)]
                        v.wait_ge(c.sck.sem, s_dA[(b, M, s_, 1)])
                        if qi >= 2:
                            v.wait_ge(c.gck.sem, g_sc[QQ(qi - 2)])
                        v.scan(so[:, ssl * Tb:(ssl + 1) * Tb],
                               dAb[:, ssl * Tb:(ssl + 1) * Tb],
                               uB[:, ssl * Tb:(ssl + 1) * Tb],
                               0.0, ALU.mult, ALU.add)
                        assert c.vck.n == v_scan[(b, M, s_)]
                    for tt in range(4):
                        v.wait_ge(c.sp, dd_sp[(b, M, tt)])
                        v.wait_ge(c.sck.sem, s_zs[(b, M, tt)])
                        if bm >= 2:
                            v.wait_ge(sty[bm % 2].sem,
                                      sty[bm % 2].at(16 * (bm // 2)))
                        v.tt(ybst[:, sl * Tb + tt * 512:
                                  sl * Tb + (tt + 1) * 512],
                             ps_y[:, tt * 512:(tt + 1) * 512],
                             zs[:, (tt % 2) * 512:(tt % 2) * 512 + 512],
                             ALU.mult)
                        assert c.vck.n == v_y[(b, M, tt)]
        assert c.vck.n == vn3, (c.vck.n, vn3)
    c.pop_scope()
    return dict(y_st=[(sty[0].sem, sty[0].at(16 * 8)),
                      (sty[1].sem, sty[1].at(16 * 8))], sty=sty)



import os
import sys
import numpy as np
import ml_dtypes
from contextlib import ExitStack

import concourse.bass as bass
import concourse.mybir as mybir
from concourse.bass_utils import run_bass_kernel_spmd



DM = 2048
TB = 1024
LAM_INIT = float(0.8 - 0.6 * np.exp(-0.3))
N_BLOCKS = int(os.environ.get("N_BLOCKS", "4"))
SIGMA = np.concatenate([np.arange(0, DM, 2), np.arange(1, DM, 2)])
ROPE_PERM = np.concatenate([np.arange(0, 128, 2), np.arange(1, 128, 2)])
QB1 = [(2 * j + 1) * 128 for j in range(8)]


DBG_PAGES = os.environ.get("DBG_PAGES", "") == "1"
_DBG_SET = {"hh_page", "mlp_page", "y_page", "xc_page", "z_page",
            "attn_page", "l1f_page", "xr_page", "dbc_part"}


def build_program():
    nc = bass.Bass("TRN2", target_bir_lowering=False, debug=False)
    E = lambda n, s, dt_=F32: nc.dram_tensor(n, s, dt_,
                                             kind="ExternalInput").ap()

    def IN(n, s, dt_=F32):
        kind = "ExternalOutput" if (DBG_PAGES and n in _DBG_SET) \
            else "Internal"
        return nc.dram_tensor(n, s, dt_, kind=kind).ap()

    d = {}
    for n_, s_, dt_ in [
        ("x0", [DM, TB], F32), ("nrmw", [128, 16 * 5], F32),
        ("wx_b", [4, 2048 * 128], F32), ("wz_b", [4, 2048 * 128], F32),
        ("xpj_b", [2, 512 * 80], F32), ("wdt_b", [4, 128 * 128], F32),
        ("conv_w", [128, 16], F32), ("conv_b", [128, 4], F32),
        ("dtb", [128, 4], F32), ("A2_d", [128, 64], F32),
        ("Dp", [128, 4], F32), ("wmo_b", [16, 512 * 128], F32),
        ("wqkv_b", [24, 2048 * 128], F32), ("wqkvb", [128, 24], F32),
        ("wopj_b", [16, 2048 * 128], F32), ("wopjb", [128, 16], F32),
        ("lv", [128, 4], F32), ("subw", [1, 256], F32),
        ("dmask", [128, 896], F32),
        ("fc1g_b", [64, 2048 * 128], F32), ("fc1u_b", [64, 2048 * 128], F32),
        ("fc2_b", [16, 8192 * 128], F32),
        ("gpk", [1024, 2048], I32), ("gsc", [1024, 2048], F32),
        ("upk", [1024, 2048], I32), ("usc", [1024, 2048], F32),
        ("dpk", [1024, 2048], I32), ("dsc", [1024, 2048], F32),
        ("wat_b", [24, 2048 * 128], F32), ("wpj_b", [16, 2048 * 128], F32),
        ("qpos", [1, TB], F32),
        ("w1_b", [64, 2048 * 128], F32), ("w3_b", [64, 2048 * 128], F32),
        ("w2_b", [16, 8192 * 128], F32),
    ]:
        d[n_] = E(n_, s_, dt_)
    out = nc.dram_tensor("out", [DM, TB], F32, kind="ExternalOutput").ap()

    hh_page = IN("hh_page", [DM, TB])
    xr_page = IN("xr_page", [DM, TB])
    hn_cin = IN("hn_cin", [DM, TB], BF16)
    hn_cout = IN("hn_cout", [8 * DM * TB], BF16)
    z_page = IN("z_page", [512, 8192], BF16)
    xc_page = IN("xc_page", [512, 8192], BF16)
    y_page = IN("y_page", [512, 8192], BF16)
    dbc_part = IN("dbc_part", [160, 8192])
    dbc_co = IN("dbc_co", [160, 8192])
    mo_cin = IN("mo_cin", [8 * DM * TB])
    mo_cout = IN("mo_cout", [DM, TB])
    kv_cin = IN("kv_cin", [1024, TB], BF16)
    kv_cout = IN("kv_cout", [2 * 1024 * TB], BF16)
    kv2_cin = IN("kv2_cin", [1024, TB], BF16)
    kv2_cout = IN("kv2_cout", [2 * 1024 * TB], BF16)
    ffp_cin = IN("ffp_cin", [8 * DM * TB])
    ffp_cout = IN("ffp_cout", [DM, TB])
    wgdec = IN("wgdec", [16 * 2048 * 128], BF16)
    wudec = IN("wudec", [16 * 2048 * 128], BF16)
    wddec = IN("wddec", [16 * 2048 * 128], BF16)
    mlp_page = IN("mlp_page", [DM, TB], BF16)
    attn_page = IN("attn_page", [DM, TB], BF16)
    l1f_page = IN("l1f_page", [DM, TB], BF16)
    dbg1 = nc.dram_tensor("dbg1", [DM, TB], F32,
                          kind="ExternalOutput").ap() if DBG_PAGES else None
    dbg2 = nc.dram_tensor("dbg2", [DM, TB], F32,
                          kind="ExternalOutput").ap() if DBG_PAGES else None
    dbg3 = nc.dram_tensor("dbg3", [DM * TB], F32,
                          kind="ExternalOutput").ap() if DBG_PAGES else None

    ctx = ExitStack()
    final_st = []
    with ctx:
        c = Ctx(nc, ctx)
        c.constf(1.0)
        c.constf(1e-5)
        c.constf(1e-6)
        csem = c.sem("csem")
        csn = [0]

        def coll(g, kind, op, grp, cin, cout):
            g.collective_compute(kind, op, grp, [cin], [cout]
                                 ).then_inc(csem, 1)
            csn[0] += 1
            return csn[0]

        def gate_all(eng):
            eng.wait_ge(c.vck.sem, c.vck.n)
            eng.wait_ge(c.sck.sem, c.sck.n)
            eng.wait_ge(c.sp, c.spn)

        nrm = c.sb("nrm", [128, 16 * 5], F32)
        onesc = c.sb("onesc", [128, 1], BF16)
        onescf = c.sb("onescf", [128, 1], F32)
        onesrf = c.sb("onesrf", [1, 128], F32)
        c._ones_row_f32 = onesrf
        ld0 = c.ds("ld0")
        NRM_M, NRM_A, NRM_ML, NRM_L1, NRM_OB = 0, 1, 2, 3, 4
        nw = lambda i: nrm[:, i * 16:(i + 1) * 16]

        with nc.Block() as blk:
            c.emit_consts(blk)

            @blk.sync
            def _(sync):
                ld0.inc(sync.dma_start(out=nrm[:], in_=d["nrmw"]))

            @blk.vector
            def _(veng):
                v = c.V(veng)
                v.wait_ge(ld0.sem, ld0.at(16))
                v.memset(onesc[:], 1.0)
                v.memset(onescf[:], 1.0)
                v.memset(onesrf[:], 1.0)
        init_v = c.vck.n

        def nkit(tag, scr=True):
            k = dict(
                stat=c.sb(tag + "st", [1, 6 * TB], F32),
                sq=c.sb(tag + "sq", [128, 16 * 512], BF16),
                ps_st=c.ps(tag + "pst", [128, 512]),
                ps_bc=c.ps(tag + "pbc", [128, 3 * 512]))
            k["scr"] = c.sb(tag + "scr", [128, 16 * 512], BF16) \
                if scr else None
            return k

        def norm(blk, tag, k, src_f32, mode, out_bf, out_f32=None,
                 w_col=None, eps=1e-5, eps2=None, in_ready=None, src_bf=None):
            return emit_norm(c, blk, TB, k["stat"], k["ps_st"], k["ps_bc"],
                             onesc, k["sq"], tag,
                             k["scr"] if src_bf is None else src_bf,
                             src_f32=src_f32, mode=mode, eps=eps, eps2=eps2,
                             w_col=w_col, out_bf=out_bf, out_f32=out_f32,
                             in_ready=in_ready)

        fm_out = lambda sb_: sb_.rearrange("p (g t) -> p g t", g=16)
        fm_in = lambda dr_: dr_.rearrange("(g p) t -> p g t", p=128)

        es1 = ExitStack()
        c.push_scope(es1)
        with es1:
            xr_sb = c.sb("b1_xr", [128, 16 * TB], F32)
            hn_l = c.sb("b1_hn", [128, 16 * TB], BF16)
            ldx = c.ds("b1_ldx")
            st1 = c.ds("b1_st")
            sth = c.ds("b1_sth")
            with nc.Block() as blk:
                @blk.sync
                def _(sync):
                    ldx.inc(sync.dma_start(out=fm_out(xr_sb[:]),
                                           in_=fm_in(d["x0"])))

                k1 = nkit("b1")
                r1 = norm(blk, "b1n", k1, xr_sb, "ln_rms", hn_l,
                          out_f32=xr_sb, w_col=nw(NRM_M), eps2=1e-5,
                          in_ready=(ldx.sem, ldx.at(16)))

                @blk.sync
                def _(sync):
                    sync.wait_ge(c.vck.sem, r1["done_v"])
                    st1.inc(sync.dma_start(out=fm_in(hn_cin),
                                           in_=fm_out(hn_l[:])))
                    sth.inc(sync.dma_start(out=fm_in(hh_page),
                                           in_=fm_out(xr_sb[:])))

                @blk.gpsimd
                def _(g):
                    g.wait_ge(st1.sem, st1.at(16))
                    g.wait_ge(sth.sem, sth.at(16))
                    coll(g, "AllGather", ALU.bypass, ALL8, hn_cin, hn_cout)
        c.pop_scope()
        ag_hn = csn[0]

        def dbc_comm(c_, blk_, ready):
            def gb(g):
                for sem_, val_ in ready:
                    g.wait_ge(sem_, val_)
                coll(g, "AllReduce", ALU.add, ALL8, dbc_part, dbc_co)
            blk_.gpsimd(gb)
            return dbc_co, [(csem, csn[0])]

        with nc.Block() as blk:
            rm = emit_mamba8(
                c, blk, hn_cout,
                dict(wx_b=d["wx_b"], wz_b=d["wz_b"], xpj_b=d["xpj_b"],
                     wdt_b=d["wdt_b"], conv_w=d["conv_w"], conv_b=d["conv_b"],
                     dtb=d["dtb"], A2_d=d["A2_d"], Dp=d["Dp"]),
                dict(z_page=z_page, xc_page=xc_page, y_page=y_page,
                     dbc_part=dbc_part),
                tag="m8", in_ready=(csem, ag_hn), dbc_comm=dbc_comm)

        es3 = ExitStack()
        c.push_scope(es3)
        with es3:
            ws_mo = WStream(c, "b3_wmo", d["wmo_b"], 4, 128, depth=16)
            yrhs = c.sb("b3_yr", [128, 2 * 4 * 512], BF16)
            stage = c.sb("b3_stg", [128, 2 * 16 * 512], F32)
            ldy = [c.ds(f"b3_ldy{i}") for i in range(2)]
            most = [c.ds(f"b3_mst{i}") for i in range(2)]
            ps3 = c.ps("b3_ps", [128, 2 * 512])
            yv3 = y_page.rearrange("(m p) t -> p m t", p=128)
            mo_view = mo_cin.rearrange("(r g p tb ti) -> r p g tb ti",
                                       r=8, g=16, p=128, tb=8)

            sp0 = c.spn
            mo_sp = {}
            spn = sp0
            for tt in range(16):
                for m in range(16):
                    spn += 1
                    mo_sp[(tt, m)] = spn
            s0 = c.sck.n
            s_ev = {}
            sn = s0
            for tt in range(16):
                for m in range(16):
                    sn += 1
                    s_ev[(tt, m)] = sn

            with nc.Block() as blk:
                @blk.gpsimd
                def _(g):
                    gate_all(g)
                    for m in range(16):
                        ws_mo.emit_one(g, m, None)

                @blk.sync
                def _(sync):
                    for tt in range(16):
                        b = tt // 4
                        slot = tt % 2
                        for i in range(2):
                            sync.wait_ge(rm["sty"][i].sem,
                                         rm["sty"][i].at(16 * 2 * (b + 1)))
                        if tt >= 2:
                            sync.wait_ge(c.sp, mo_sp[(tt - 2, 15)])
                        ldy[slot].inc(sync.dma_start(
                            out=yrhs[:, slot * 2048:(slot + 1) * 2048]
                            .rearrange("p (m t) -> p m t", m=4),
                            in_=yv3[:, :, tt * 512:(tt + 1) * 512]))

                @blk.tensor
                def _(t):
                    for m in range(16):
                        ws_mo.wait(t, m)
                    for tt in range(16):
                        slot = tt % 2
                        t.wait_ge(ldy[slot].sem,
                                  ldy[slot].at(16 * (tt // 2 + 1)))
                        for m in range(16):
                            gi = tt * 16 + m
                            if gi >= 2:
                                pg = divmod(gi - 2, 16)
                                t.wait_ge(c.sck.sem, s_ev[pg])
                            for kk in range(4):
                                mm = t.matmul(
                                    ps3[:, (gi % 2) * 512:(gi % 2) * 512 + 512],
                                    ws_mo.tile(m, kk),
                                    yrhs[:, slot * 2048 + kk * 512:
                                         slot * 2048 + (kk + 1) * 512],
                                    start=(kk == 0), stop=(kk == 3))
                            c.pe_sig(mm)
                assert c.spn == spn

                @blk.scalar
                def _(seng):
                    s = c.S(seng)
                    for tt in range(16):
                        b, q4 = tt // 4, tt % 4
                        sl = tt % 2
                        for m in range(16):
                            gi = tt * 16 + m
                            s.wait_ge(c.sp, mo_sp[(tt, m)])
                            if tt >= 2 and m == 0:
                                s.wait_ge(most[sl].sem,
                                          most[sl].at(4 * 16 * (tt // 2)))
                            s.act(stage[:, sl * 8192 + m * 512:
                                        sl * 8192 + (m + 1) * 512],
                                  ps3[:, (gi % 2) * 512:(gi % 2) * 512 + 512],
                                  AF.Copy)
                        s.wait_ge(c.sck.sem, s_ev[(tt, 15)])
                        sv = stage[:, sl * 8192:(sl + 1) * 8192] \
                            .rearrange("p (g tb s2 ti) -> p g tb s2 ti",
                                       g=16, tb=2, s2=2)
                        for tb in range(2):
                            for s2 in range(2):
                                most[sl].inc(s.e.dma_start(
                                    out=mo_view[2 * b + s2][:, :,
                                                            q4 * 2 + tb, :],
                                    in_=sv[:, :, tb, s2, :]))
                assert c.sck.n == sn

                @blk.gpsimd
                def _(g):
                    for i in range(2):
                        g.wait_ge(most[i].sem, most[i].at(4 * 16 * 8))
                    coll(g, "ReduceScatter", ALU.add, ALL8, mo_cin, mo_cout)
        c.pop_scope()
        rs_mo = csn[0]

        es_b4 = ExitStack()
        c.push_scope(es_b4)
        with es_b4:
            hn2 = c.sb("b4_hn2", [128, 16 * TB], BF16)

            es_hh = ExitStack()
            c.push_scope(es_hh)
            with es_hh:
                hh_sb = c.sb("b3_hh", [128, 16 * TB], F32)
                ldm = c.ds("b3_ldm")
                ldh3 = c.ds("b3_ldh")
                sth3 = c.ds("b3_sth")
                es_add = ExitStack()
                c.push_scope(es_add)
                with es_add:
                    moin = c.sb("b3_moin", [128, 16 * TB], F32)
                    with nc.Block() as blk:
                        @blk.sync
                        def _(sync):
                            sync.wait_ge(csem, rs_mo)
                            if DBG_PAGES:
                                ldm.inc(sync.dma_start(
                                    out=dbg3,
                                    in_=mo_cin[0:DM * TB]))
                            ldm.inc(sync.dma_start(out=fm_out(moin[:]),
                                                   in_=fm_in(mo_cout)))
                            sync.wait_ge(sth.sem, sth.at(16))
                            ldh3.inc(sync.dma_start(out=fm_out(hh_sb[:]),
                                                    in_=fm_in(hh_page)))

                        @blk.vector
                        def _(veng):
                            v = c.V(veng)
                            v.wait_ge(ldm.sem, ldm.at(32 if DBG_PAGES else 16))
                            v.wait_ge(ldh3.sem, ldh3.at(16))
                            for g_ in range(16):
                                v.tt(hh_sb[:, g_ * TB:(g_ + 1) * TB],
                                     hh_sb[:, g_ * TB:(g_ + 1) * TB],
                                     moin[:, g_ * TB:(g_ + 1) * TB], ALU.add)
                        v_hh = c.vck.n

                        @blk.sync
                        def _(sync):
                            sync.wait_ge(c.vck.sem, v_hh)
                            sth3.inc(sync.dma_start(out=fm_in(hh_page),
                                                    in_=fm_out(hh_sb[:])))
                            if DBG_PAGES:
                                sth3.inc(sync.dma_start(
                                    out=fm_in(dbg1), in_=fm_out(hh_sb[:])))
                                sth3.inc(sync.dma_start(
                                    out=fm_in(dbg2), in_=fm_out(moin[:])))
                c.pop_scope()

                with nc.Block() as blk:
                    es_k = ExitStack()
                    c.push_scope(es_k)
                    with es_k:
                        k4 = nkit("b4")
                        norm(blk, "b4n", k4, hh_sb, "rms", hn2,
                             w_col=nw(NRM_A),
                             in_ready=(sth3.sem, sth3.at(48))
                             if DBG_PAGES else (c.vck.sem, v_hh))
                    c.pop_scope()
            c.pop_scope()

            qkv_q = c.sb("b4_q", [128, 16 * TB], BF16)
            k_g = c.sb("b4_kg", [128, 4 * 2048], BF16)
            v_g = c.sb("b4_vg", [128, 4 * 2048], BF16)
            yT4 = c.sb("b4_yT", [128, 16 * TB], BF16)
            wqb2 = c.sb("b4_wqb", [128, 24], F32)
            lvb = c.sb("b4_lv", [128, 4], F32)
            subr = c.sb("b4_sub", [128, 256], BF16)
            subwf = c.sb("b4_subw", [1, 256], F32)
            lcol = c.sb("b4_lcol", [128, 1], F32)
            sca = c.sb("b4_sca", [1, 2], F32)

            ldg = c.ds("b4_ldg")
            es_qp = ExitStack()
            c.push_scope(es_qp)
            with es_qp:
                qkv_kv = c.sb("b4_kv", [128, 8 * TB], BF16)
                ws_qkv = WStream(c, "b4_wq", d["wqkv_b"], 16, 128, depth=2)
                ps_pj = c.ps("b4_psq", [128, 2 * 512])
                ldb4 = c.ds("b4_ldb")
                st4 = c.ds("b4_st4")
                SQ = float(1.0 / np.sqrt(128.0))

                sp0 = c.spn
                q_sp = {}
                spn = sp0
                for m in range(24):
                    for tt in range(2):
                        spn += 1
                        q_sp[(m, tt)] = spn
                s0 = c.sck.n
                sq_ev = {}
                sn = s0
                for m in range(24):
                    for tt in range(2):
                        sn += 1
                        sq_ev[(m, tt)] = sn

                with nc.Block() as blk:
                    @blk.sync
                    def _(sync):
                        ldb4.inc(sync.dma_start(out=wqb2[:], in_=d["wqkvb"]))
                        ldb4.inc(sync.dma_start(out=lvb[:], in_=d["lv"]))
                        ldb4.inc(sync.dma_start(out=subwf[:], in_=d["subw"]))

                    @blk.vector
                    def _(veng):
                        v = c.V(veng)
                        v.wait_ge(ldb4.sem, ldb4.at(48))
                        v.ts(wqb2[:, 0:16], wqb2[:, 0:16], SQ, None, ALU.mult)
                        v.tt(lvb[:, 0:1], lvb[:, 0:1], lvb[:, 1:2], ALU.mult)
                        v.tt(lvb[:, 2:3], lvb[:, 2:3], lvb[:, 3:4], ALU.mult)
                    v_pre4 = c.vck.n

                    @blk.gpsimd
                    def _(g):
                        gate_all(g)
                        for m in range(24):
                            ws_qkv.emit_one(
                                g, m, lambda mm_: (c.sp, q_sp[(mm_, 1)]))

                    @blk.tensor
                    def _(t):
                        t.wait_ge(c.vck.sem, v_pre4)
                        t.wait_ge(sth3.sem, sth3.at(48 if DBG_PAGES else 16))
                        for m in range(24):
                            ws_qkv.wait(t, m)
                            for tt in range(2):
                                idx = m * 2 + tt
                                if idx >= 2:
                                    t.wait_ge(c.sck.sem,
                                              sq_ev[divmod(idx - 2, 2)])
                                for kk in range(16):
                                    mm = t.matmul(
                                        ps_pj[:, (idx % 2) * 512:
                                              (idx % 2) * 512 + 512],
                                        ws_qkv.tile(m, kk),
                                        hn2[:, kk * TB + tt * 512:
                                            kk * TB + (tt + 1) * 512],
                                        start=(kk == 0), stop=(kk == 15))
                                c.pe_sig(mm)
                    assert c.spn == spn

                    @blk.scalar
                    def _(seng):
                        s = c.S(seng)
                        for m in range(24):
                            if m < 16:
                                dst = qkv_q[:, m * TB:(m + 1) * TB]
                                sc = SQ
                            else:
                                dst = qkv_kv[:, (m - 16) * TB:(m - 15) * TB]
                                sc = 1.0
                            for tt in range(2):
                                idx = m * 2 + tt
                                s.wait_ge(c.sp, q_sp[(m, tt)])
                                s.act(dst[:, tt * 512:(tt + 1) * 512],
                                      ps_pj[:, (idx % 2) * 512:
                                            (idx % 2) * 512 + 512],
                                      AF.Identity, bias=wqb2[:, m:m + 1],
                                      scale=sc)
                    assert c.sck.n == sn

                    @blk.sync
                    def _(sync):
                        for m in range(16, 24):
                            for tt in range(2):
                                sync.wait_ge(c.sck.sem, sq_ev[(m, tt)])
                        st4.inc(sync.dma_start(
                            out=fm_in(kv_cin.rearrange("(a b) -> a b", a=1024)
                                      if False else kv_cin),
                            in_=qkv_kv[:].rearrange("p (g t) -> p g t", g=8)))

                    @blk.gpsimd
                    def _(g):
                        g.wait_ge(st4.sem, st4.at(16))
                        coll(g, "AllGather", ALU.bypass, PAIRS,
                             kv_cin, kv_cout)
                ag_kv = csn[0]

                with nc.Block() as blk:
                    @blk.sync
                    def _(sync):
                        sync.wait_ge(csem, ag_kv)
                        cv = kv_cout.rearrange(
                            "(s g p tb ti) -> s p g tb ti",
                            s=2, g=8, p=128, tb=8, ti=128)
                        for s_ in range(2):
                            for g_ in range(4):
                                kk_ = k_g[:, g_ * 2048:(g_ + 1) * 2048] \
                                    .rearrange("p (tb s2 ti) -> p tb s2 ti",
                                               tb=8, s2=2)
                                vv_ = v_g[:, g_ * 2048:(g_ + 1) * 2048] \
                                    .rearrange("p (tb s2 ti) -> p tb s2 ti",
                                               tb=8, s2=2)
                                ldg.inc(sync.dma_start(
                                    out=kk_[:, :, s_, :],
                                    in_=cv[s_][:, g_, :, :]))
                                ldg.inc(sync.dma_start(
                                    out=vv_[:, :, s_, :],
                                    in_=cv[s_][:, 4 + g_, :, :]))
            c.pop_scope()

            es_dm = ExitStack()
            c.push_scope(es_dm)
            with es_dm:
                ps_x = c.ps("b4_psx", [128, 512])
                ident4 = emit_identity(c, None, "b4id") if False else None
                with nc.Block() as blk:
                    ident4 = emit_identity(c, blk, "b4id")
                    spl = []

                    @blk.tensor
                    def _(t):
                        t.wait_ge(c.vck.sem, v_pre4)
                        mm = t.matmul(ps_x[0:1, 0:1], lvb[:, 0:1],
                                      onescf[:, 0:1], start=True, stop=True)
                        spl.append(c.pe_sig(mm))
                        mm = t.matmul(ps_x[0:1, 1:2], lvb[:, 2:3],
                                      onescf[:, 0:1], start=True, stop=True)
                        spl.append(c.pe_sig(mm))

                    @blk.scalar
                    def _(seng):
                        s = c.S(seng)
                        s.wait_ge(c.sp, spl[1])
                        s.act(sca[0:1, 0:2], ps_x[0:1, 0:2], AF.Exp)
                    s_lam = c.sck.n

                    @blk.vector
                    def _(veng):
                        v = c.V(veng)
                        v.wait_ge(c.sck.sem, s_lam)
                        v.tt(sca[0:1, 0:1], sca[0:1, 1:2], sca[0:1, 0:1],
                             ALU.subtract)
                        v.ts(sca[0:1, 0:1], sca[0:1, 0:1], -LAM_INIT, None,
                             ALU.add)
                    v_lam = c.vck.n

                    spb = []

                    @blk.tensor
                    def _(t):
                        t.wait_ge(c.vck.sem, v_lam)
                        mm = t.matmul(ps_x[:, 0:1], onesrf[0:1, :],
                                      sca[0:1, 0:1], start=True, stop=True)
                        spb.append(c.pe_sig(mm))
                        mm = t.matmul(ps_x[:, 256:512], onesrf[0:1, :],
                                      subwf[0:1, :], start=True, stop=True)
                        spb.append(c.pe_sig(mm))

                    @blk.vector
                    def _(veng):
                        v = c.V(veng)
                        v.wait_ge(c.sp, spb[1])
                        v.copy(lcol[:], ps_x[:, 0:1])
                        v.ts(subr[:], ps_x[:, 256:512],
                             float(1 - LAM_INIT), None, ALU.mult)
                    v_rep = c.vck.n

                    emit_diff_mha(c, blk, qkv_q[:], k_g[:], v_g[:], yT4[:],
                                  (lcol, v_rep), (subr, v_rep), TB, 2048,
                                  QB1, ident4, tag="b4dm",
                                  kv_ready=(ldg.sem, ldg.at(16 * 16)),
                                  rms_eps=1e-5, lam_init=LAM_INIT,
                                  mask_dram=d["dmask"])
            c.pop_scope()

            es_opj = ExitStack()
            c.push_scope(es_opj)
            with es_opj:
                ws_opj = WStream(c, "b4_wo", d["wopj_b"], 16, 128, depth=2)
                stg1 = c.sb("b4_st1", [128, 2 * 512], F32)
                stg2 = c.sb("b4_st2", [128, 2 * 512], F32)
                hhs = c.sb("b4_hhs", [128, 2 * 512], F32)
                ps_o = c.ps("b4_pso", [128, 2 * 512])
                ldho = [c.ds(f"b4_lho{i}") for i in range(2)]
                stho = [c.ds(f"b4_sho{i}") for i in range(2)]
                ldob = c.ds("b4_ldob")
                hhv = fm_in(hh_page)

                sp0 = c.spn
                o_sp = {}
                spn = sp0
                for m in range(16):
                    for tt in range(2):
                        spn += 1
                        o_sp[(m, tt)] = spn
                s0 = c.sck.n
                so_ev = {}
                sn = s0
                for m in range(16):
                    for tt in range(2):
                        sn += 1
                        so_ev[(m, tt)] = sn
                v0 = c.vck.n
                vo_add = {}
                vn = v0
                for m in range(16):
                    for tt in range(2):
                        vn += 1
                        vo_add[(m, tt)] = vn

                with nc.Block() as blk:
                    @blk.sync
                    def _(sync):
                        ldob.inc(sync.dma_start(out=wqb2[:, 0:16],
                                                in_=d["wopjb"]))
                        sync.wait_ge(sth3.sem, sth3.at(48 if DBG_PAGES else 16))
                        for m in range(16):
                            for tt in range(2):
                                idx = m * 2 + tt
                                sl = idx % 2
                                if idx >= 2:
                                    sync.wait_ge(c.vck.sem,
                                                 vo_add[divmod(idx - 2, 2)])
                                ldho[sl].inc(sync.dma_start(
                                    out=hhs[:, sl * 512:(sl + 1) * 512],
                                    in_=hhv[:, m, tt * 512:(tt + 1) * 512]))

                    @blk.gpsimd
                    def _(g):
                        gate_all(g)
                        for m in range(16):
                            ws_opj.emit_one(
                                g, m, lambda mm_: (c.sp, o_sp[(mm_, 1)]))

                    @blk.tensor
                    def _(t):
                        for m in range(16):
                            ws_opj.wait(t, m)
                            for tt in range(2):
                                idx = m * 2 + tt
                                if idx >= 2:
                                    t.wait_ge(c.sck.sem,
                                              so_ev[divmod(idx - 2, 2)])
                                for kk in range(16):
                                    mm = t.matmul(
                                        ps_o[:, (idx % 2) * 512:
                                             (idx % 2) * 512 + 512],
                                        ws_opj.tile(m, kk),
                                        yT4[:, kk * TB + tt * 512:
                                            kk * TB + (tt + 1) * 512],
                                        start=(kk == 0), stop=(kk == 15))
                                c.pe_sig(mm)
                    assert c.spn == spn

                    @blk.scalar
                    def _(seng):
                        s = c.S(seng)
                        s.wait_ge(ldob.sem, ldob.at(16))
                        for m in range(16):
                            for tt in range(2):
                                idx = m * 2 + tt
                                sl = idx % 2
                                s.wait_ge(c.sp, o_sp[(m, tt)])
                                if idx >= 2:
                                    s.wait_ge(c.vck.sem,
                                              vo_add[divmod(idx - 2, 2)])
                                s.act(stg1[:, sl * 512:(sl + 1) * 512],
                                      ps_o[:, sl * 512:(sl + 1) * 512],
                                      AF.Identity, bias=wqb2[:, m:m + 1])
                                s.wait_ge(c.vck.sem, vo_add[(m, tt)])
                                stho[sl].inc(s.e.dma_start(
                                    out=hhv[:, m, tt * 512:(tt + 1) * 512],
                                    in_=stg2[:, sl * 512:(sl + 1) * 512]))
                    assert c.sck.n == sn

                    @blk.vector
                    def _(veng):
                        v = c.V(veng)
                        for m in range(16):
                            for tt in range(2):
                                idx = m * 2 + tt
                                sl = idx % 2
                                v.wait_ge(c.sck.sem, so_ev[(m, tt)])
                                v.wait_ge(ldho[sl].sem,
                                          ldho[sl].at(16 * (idx // 2 + 1)))
                                if idx >= 2:
                                    v.wait_ge(stho[sl].sem,
                                              stho[sl].at(16 * (idx // 2)))
                                v.tt(stg2[:, sl * 512:(sl + 1) * 512],
                                     stg1[:, sl * 512:(sl + 1) * 512],
                                     hhs[:, sl * 512:(sl + 1) * 512],
                                     ALU.add)
                    assert c.vck.n == vn
            c.pop_scope()
        c.pop_scope()
        b4_hh_st = None

        es5 = ExitStack()
        c.push_scope(es5)
        with es5:
            hn3 = c.sb("b5_hn3", [128, 16 * TB], BF16)
            es5a = ExitStack()
            c.push_scope(es5a)
            with es5a:
                hh5 = c.sb("b5_hh", [128, 16 * TB], F32)
                ldh5 = c.ds("b5_ldh")
                with nc.Block() as blk:
                    @blk.sync
                    def _(sync):
                        gate_all(sync)
                        for i in range(2):
                            sync.wait_ge(stho[i].sem, stho[i].at(16 * 16))
                        ldh5.inc(sync.dma_start(out=fm_out(hh5[:]),
                                                in_=fm_in(hh_page)))
                    k5 = nkit("b5")
                    r5 = norm(blk, "b5n", k5, hh5, "rms", hn3,
                              w_col=nw(NRM_ML),
                              in_ready=(ldh5.sem, ldh5.at(16)))
            c.pop_scope()

            mlpv = fm_in(mlp_page)
            r5f = None
            for chv in range(2):
                es5b = ExitStack()
                c.push_scope(es5b)
                with es5b:
                    with nc.Block() as blk:
                        @blk.gpsimd
                        def _(g):
                            gate_all(g)

                        r5f = emit_l1_ffn(
                            c, blk, hn3[:], None, d["fc1g_b"], d["fc1u_b"],
                            d["fc2_b"], 512, tag=f"b5f{chv}",
                            in_ready=(c.vck.sem, r5["done_v"]),
                            do_fwht=False, gate_act=AF.Silu,
                            x_stride=TB, x_off=chv * 512,
                            out_dram=mlpv[:, :, chv * 512:(chv + 1) * 512],
                            out_gate=None if chv == 0 else r5f["st"])
                c.pop_scope()
        c.pop_scope()

        es6o = ExitStack()
        c.push_scope(es6o)
        with es6o:
            lnb6 = c.sb("b6_lnb", [128, 16 * TB], BF16)
            es6a = ExitStack()
            c.push_scope(es6a)
            with es6a:
                hh6 = c.sb("b6_hh", [128, 16 * TB], F32)
                es6aa = ExitStack()
                c.push_scope(es6aa)
                with es6aa:
                    mlp_bf = c.sb("b6_mlp", [128, 16 * TB], BF16)
                    ldm6 = c.ds("b6_ldm")
                    with nc.Block() as blk:
                        @blk.sync
                        def _(sync):
                            gate_all(sync)
                            for sem_, val_ in r5f["st"]:
                                sync.wait_ge(sem_, val_)
                            ldm6.inc(sync.dma_start(out=fm_out(mlp_bf[:]),
                                                    in_=fm_in(mlp_page)))
                        emit_resadd(c, blk, mlp_bf[:], hh_page, TB, "b6ra",
                                    dest_sb=hh6,
                                    in_ready=(ldm6.sem, ldm6.at(16)),
                                    ld_gate=[(stho[i].sem,
                                              stho[i].at(16 * 16))
                                             for i in range(2)])
                c.pop_scope()
                es6ab = ExitStack()
                c.push_scope(es6ab)
                with es6ab:
                    with nc.Block() as blk:
                        k6 = nkit("b6")
                        r6 = norm(blk, "b6n", k6, hh6, "ln", lnb6)
                c.pop_scope()
            c.pop_scope()

            if N_BLOCKS == 1:
                with nc.Block() as blk:
                    rfin = emit_resadd(c, blk, lnb6[:], d["x0"], TB, "fin",
                                       dest_dram=out,
                                       in_ready=(c.vck.sem, r6["done_v"]))
                final_st.append((rfin["st_sem"], rfin["st_val"]))
            else:
                es7h = ExitStack()
                c.push_scope(es7h)
                with es7h:
                    hn4 = c.sb("b7_hn4", [128, 16 * TB], BF16)
                    st7 = c.ds("b7_st7")
                    es7x = ExitStack()
                    c.push_scope(es7x)
                    with es7x:
                        xr7 = c.sb("b7_xr", [128, 16 * TB], F32)
                        es7ra = ExitStack()
                        c.push_scope(es7ra)
                        with es7ra:
                            with nc.Block() as blk:
                                r7a = emit_resadd(
                                    c, blk, lnb6[:], d["x0"], TB, "b7ra",
                                    dest_sb=xr7, dest_dram=xr_page,
                                    in_ready=(c.vck.sem, r6["done_v"]))
                        c.pop_scope()
                        es7xb = ExitStack()
                        c.push_scope(es7xb)
                        with es7xb:
                            with nc.Block() as blk:
                                k7 = nkit("b7")
                                r7 = norm(blk, "b7n", k7, xr7, "ln", hn4,
                                          in_ready=(c.vck.sem,
                                                    r7a["done_v"]))
                        c.pop_scope()
                    c.pop_scope()
                    with nc.Block() as blk:
                        @blk.sync
                        def _(sync):
                            sync.wait_ge(c.vck.sem, r7["done_v"])
                            st7.inc(sync.dma_start(out=fm_in(hn_cin),
                                                   in_=fm_out(hn4[:])))

                        @blk.gpsimd
                        def _(g):
                            g.wait_ge(st7.sem, st7.at(16))
                            coll(g, "AllGather", ALU.bypass, ALL8,
                                 hn_cin, hn_cout)
                c.pop_scope()
        c.pop_scope()
        ag_h4 = csn[0]

        if N_BLOCKS >= 2:
            dec_done = []
            prev_done = None
            for nm_, pk_, sc_, dst_ in (("dcg", "gpk", "gsc", wgdec),
                                        ("dcu", "upk", "usc", wudec),
                                        ("dcd", "dpk", "dsc", wddec)):
                esd = ExitStack()
                c.push_scope(esd)
                with esd:
                    w3v = dst_.rearrange("(mt k n) -> k mt n",
                                         mt=16, k=2048, n=128)
                    with nc.Block() as blk:
                        @blk.gpsimd
                        def _(g):
                            gate_all(g)
                            g.wait_ge(csem, ag_h4)
                            g.wait_ge(r7a["st_sem"], r7a["st_val"])
                        rdec = emit_nvfp4_decode(
                            c, blk, d[pk_], d[sc_],
                            w3v[0:1024], w3v[1024:2048], 1024, 2048, tag=nm_,
                            start_gate=prev_done)
                    dec_done += rdec["done"]
                    prev_done = rdec["done"]
                c.pop_scope()

            es7m = ExitStack()
            c.push_scope(es7m)
            with es7m:
                xcb = c.sb("b7_xc", [128, 2 * 16 * TB], BF16)
                inter = c.sb("b7_int", [128, 16 * TB], BF16)
                wgs = c.sb("b7_wgs", [128, 2 * 16 * 128], BF16)
                wus = c.sb("b7_wus", [128, 2 * 16 * 128], BF16)
                wds = c.sb("b7_wds", [128, 2 * 16 * 128], BF16)
                gsb = c.sb("b7_gsb", [128, 2 * 512], BF16)
                usb = c.sb("b7_usb", [128, 2 * 512], BF16)
                stage7 = c.sb("b7_stg", [128, 2 * 16 * 512], F32)
                ps_g = c.ps("b7_psg", [128, 2 * 512])
                ps_u = c.ps("b7_psu", [128, 2 * 512])
                ps_dn = c.ps("b7_psd", [128, 2 * 512])
                lxc = [c.ds(f"b7_lxc{i}") for i in range(2)]
                lwg = [c.ds(f"b7_lwg{i}") for i in range(2)]
                lwu = [c.ds(f"b7_lwu{i}") for i in range(2)]
                lwd = [c.ds(f"b7_lwd{i}") for i in range(2)]
                fst = [c.ds(f"b7_fst{i}") for i in range(2)]
                wgv = wgdec.rearrange("(mt kt p n) -> mt p kt n",
                                      mt=16, kt=16, p=128, n=128)
                wuv = wudec.rearrange("(mt kt p n) -> mt p kt n",
                                      mt=16, kt=16, p=128, n=128)
                wdv = wddec.rearrange("(mt kt p n) -> mt p kt n",
                                      mt=16, kt=16, p=128, n=128)
                hcv = hn_cout.rearrange("(r g p t) -> r p g t",
                                        r=8, g=16, p=128)
                ffv = ffp_cin.rearrange("(r g p t) -> r p g t",
                                        r=8, g=16, p=128)

                sp0 = c.spn
                g_sp, u_sp, d_sp = {}, {}, {}
                spn = sp0
                for ch in range(8):
                    for mt in range(16):
                        for tt in range(2):
                            spn += 1
                            g_sp[(ch, mt, tt)] = spn
                        for tt in range(2):
                            spn += 1
                            u_sp[(ch, mt, tt)] = spn
                    for m2 in range(16):
                        for tt in range(2):
                            spn += 1
                            d_sp[(ch, m2, tt)] = spn
                s0 = c.sck.n
                s_g, s_u, s_d = {}, {}, {}
                sn = s0
                for ch in range(8):
                    for mt in range(16):
                        for tt in range(2):
                            sn += 2
                            s_g[(ch, mt, tt)] = sn - 1
                            s_u[(ch, mt, tt)] = sn
                    for m2 in range(16):
                        for tt in range(2):
                            sn += 1
                            s_d[(ch, m2, tt)] = sn
                v0 = c.vck.n
                v_int = {}
                vn = v0
                for ch in range(8):
                    for mt in range(16):
                        for tt in range(2):
                            vn += 1
                            v_int[(ch, mt, tt)] = vn

                with nc.Block() as blk:
                    @blk.sync
                    def _(sync):
                        gate_all(sync)
                        sync.wait_ge(csem, ag_h4)
                        for sem_, val_ in dec_done:
                            sync.wait_ge(sem_, val_)
                        for ch in range(8):
                            sl = ch % 2
                            if ch >= 2:
                                sync.wait_ge(c.sp, u_sp[(ch - 2, 15, 1)])
                            lxc[sl].inc(sync.dma_start(
                                out=xcb[:, sl * 16 * TB:(sl + 1) * 16 * TB]
                                .rearrange("p (g t) -> p g t", g=16),
                                in_=hcv[ch]))
                            for mt in range(16):
                                gidx = ch * 16 + mt
                                wsl = gidx % 2
                                if gidx >= 2:
                                    sync.wait_ge(
                                        c.sp,
                                        g_sp[((gidx - 2) // 16,
                                              (gidx - 2) % 16, 1)])
                                lwg[wsl].inc(sync.dma_start(
                                    out=wgs[:, wsl * 2048:(wsl + 1) * 2048]
                                    .rearrange("p (kt n) -> p kt n", kt=16),
                                    in_=wgv[mt]))
                                if gidx >= 2:
                                    sync.wait_ge(
                                        c.sp,
                                        u_sp[((gidx - 2) // 16,
                                              (gidx - 2) % 16, 1)])
                                lwu[wsl].inc(sync.dma_start(
                                    out=wus[:, wsl * 2048:(wsl + 1) * 2048]
                                    .rearrange("p (kt n) -> p kt n", kt=16),
                                    in_=wuv[mt]))
                            for m2 in range(16):
                                gidx = ch * 16 + m2
                                wsl = gidx % 2
                                if gidx >= 2:
                                    sync.wait_ge(
                                        c.sp,
                                        d_sp[((gidx - 2) // 16,
                                              (gidx - 2) % 16, 1)])
                                lwd[wsl].inc(sync.dma_start(
                                    out=wds[:, wsl * 2048:(wsl + 1) * 2048]
                                    .rearrange("p (kt n) -> p kt n", kt=16),
                                    in_=wdv[m2]))

                    @blk.tensor
                    def _(t):
                        for ch in range(8):
                            sl = ch % 2
                            t.wait_ge(lxc[sl].sem,
                                      lxc[sl].at(16 * (ch // 2 + 1)))
                            for mt in range(16):
                                gidx = ch * 16 + mt
                                wsl = gidx % 2
                                t.wait_ge(lwg[wsl].sem,
                                          lwg[wsl].at(16 * (gidx // 2 + 1)))
                                for tt in range(2):
                                    if ch > 0 or mt > 0:
                                        pm = (ch, mt - 1) if mt > 0 else \
                                            (ch - 1, 15)
                                        t.wait_ge(c.sck.sem,
                                                  s_g[pm + (tt,)])
                                    if mt == 0 and tt == 0 and ch > 0:
                                        t.wait_ge(c.sp, d_sp[(ch - 1, 15, 1)])
                                    for kt in range(16):
                                        mm = t.matmul(
                                            ps_g[:, tt * 512:(tt + 1) * 512],
                                            wgs[:, wsl * 2048 + kt * 128:
                                                wsl * 2048 + (kt + 1) * 128],
                                            xcb[:, sl * 16 * TB + kt * TB +
                                                tt * 512: sl * 16 * TB +
                                                kt * TB + (tt + 1) * 512],
                                            start=(kt == 0), stop=(kt == 15))
                                    c.pe_sig(mm)
                                t.wait_ge(lwu[wsl].sem,
                                          lwu[wsl].at(16 * (gidx // 2 + 1)))
                                for tt in range(2):
                                    if ch > 0 or mt > 0:
                                        pm = (ch, mt - 1) if mt > 0 else \
                                            (ch - 1, 15)
                                        t.wait_ge(c.sck.sem,
                                                  s_u[pm + (tt,)])
                                    for kt in range(16):
                                        mm = t.matmul(
                                            ps_u[:, tt * 512:(tt + 1) * 512],
                                            wus[:, wsl * 2048 + kt * 128:
                                                wsl * 2048 + (kt + 1) * 128],
                                            xcb[:, sl * 16 * TB + kt * TB +
                                                tt * 512: sl * 16 * TB +
                                                kt * TB + (tt + 1) * 512],
                                            start=(kt == 0), stop=(kt == 15))
                                    c.pe_sig(mm)
                            for m2 in range(16):
                                gidx = ch * 16 + m2
                                wsl = gidx % 2
                                t.wait_ge(lwd[wsl].sem,
                                          lwd[wsl].at(16 * (gidx // 2 + 1)))
                                for tt in range(2):
                                    if m2 == 0:
                                        t.wait_ge(c.vck.sem,
                                                  v_int[(ch, 15, 1)])
                                    if ch > 0 or m2 > 0:
                                        pm = (ch, m2 - 1) if m2 > 0 else \
                                            (ch - 1, 15)
                                        t.wait_ge(c.sck.sem, s_d[pm + (tt,)])
                                    for kt in range(16):
                                        mm = t.matmul(
                                            ps_dn[:, tt * 512:(tt + 1) * 512],
                                            wds[:, wsl * 2048 + kt * 128:
                                                wsl * 2048 + (kt + 1) * 128],
                                            inter[:, kt * TB + tt * 512:
                                                  kt * TB + (tt + 1) * 512],
                                            start=(kt == 0), stop=(kt == 15))
                                    c.pe_sig(mm)
                    assert c.spn == spn

                    @blk.scalar
                    def _(seng):
                        s = c.S(seng)
                        for ch in range(8):
                            for mt in range(16):
                                for tt in range(2):
                                    s.wait_ge(c.sp, g_sp[(ch, mt, tt)])
                                    if ch > 0 or mt > 0:
                                        pm = (ch, mt - 1) if mt > 0 else \
                                            (ch - 1, 15)
                                        s.wait_ge(c.vck.sem,
                                                  v_int[pm + (tt,)])
                                    s.act(gsb[:, tt * 512:(tt + 1) * 512],
                                          ps_g[:, tt * 512:(tt + 1) * 512],
                                          AF.Gelu_apprx_tanh)
                                    s.wait_ge(c.sp, u_sp[(ch, mt, tt)])
                                    s.act(usb[:, tt * 512:(tt + 1) * 512],
                                          ps_u[:, tt * 512:(tt + 1) * 512],
                                          AF.Copy)
                            for m2 in range(16):
                                for tt in range(2):
                                    s.wait_ge(c.sp, d_sp[(ch, m2, tt)])
                                    if m2 == 0 and ch >= 1:
                                        s.wait_ge(fst[tt].sem,
                                                  fst[tt].at(16 * ch))
                                    s.act(stage7[:, tt * 8192 + m2 * 512:
                                                 tt * 8192 + (m2 + 1) * 512],
                                          ps_dn[:, tt * 512:(tt + 1) * 512],
                                          AF.Copy)
                            for tt in range(2):
                                s.wait_ge(c.sck.sem, s_d[(ch, 15, tt)])
                                fst[tt].inc(s.e.dma_start(
                                    out=ffv[ch][:, :, tt * 512:(tt + 1) * 512],
                                    in_=stage7[:, tt * 8192:(tt + 1) * 8192]
                                    .rearrange("p (g t) -> p g t", g=16)))
                    assert c.sck.n == sn

                    @blk.vector
                    def _(veng):
                        v = c.V(veng)
                        for ch in range(8):
                            for mt in range(16):
                                for tt in range(2):
                                    v.wait_ge(c.sck.sem, s_u[(ch, mt, tt)])
                                    if ch > 0 and mt == 0 and tt == 0:
                                        v.wait_ge(c.sp, d_sp[(ch - 1, 15, 1)])
                                    v.tt(inter[:, mt * TB + tt * 512:
                                               mt * TB + (tt + 1) * 512],
                                         gsb[:, tt * 512:(tt + 1) * 512],
                                         usb[:, tt * 512:(tt + 1) * 512],
                                         ALU.mult)
                    assert c.vck.n == vn

                    @blk.gpsimd
                    def _(g):
                        for i in range(2):
                            g.wait_ge(fst[i].sem, fst[i].at(16 * 8))
                        coll(g, "ReduceScatter", ALU.add, ALL8,
                             ffp_cin, ffp_cout)
            c.pop_scope()
        rs_ff = csn[0]

        if N_BLOCKS >= 2:
            es8h = ExitStack()
            c.push_scope(es8h)
            with es8h:
                hn8 = c.sb("b8_hn8", [128, 16 * TB], BF16)
                es8x = ExitStack()
                c.push_scope(es8x)
                with es8x:
                    lnb7 = c.sb("b7_lnb", [128, 16 * TB], BF16)
                    es8xa = ExitStack()
                    c.push_scope(es8xa)
                    with es8xa:
                        ff8 = c.sb("b7_ff", [128, 16 * TB], F32)
                        ldf = c.ds("b7_ldf")
                        with nc.Block() as blk:
                            @blk.sync
                            def _(sync):
                                gate_all(sync)
                                sync.wait_ge(csem, rs_ff)
                                ldf.inc(sync.dma_start(out=fm_out(ff8[:]),
                                                       in_=fm_in(ffp_cout)))
                            k7c = nkit("b7c")
                            r7c = norm(blk, "b7cn", k7c, ff8, "ln", lnb7,
                                       in_ready=(ldf.sem, ldf.at(16)))
                    c.pop_scope()

                    if N_BLOCKS == 2:
                        with nc.Block() as blk:
                            rfin = emit_resadd(
                                c, blk, lnb7[:], xr_page, TB, "fin",
                                dest_dram=out,
                                in_ready=(c.vck.sem, r7c["done_v"]),
                                ld_gate=[(r7a["st_sem"], r7a["st_val"])])
                        final_st.append((rfin["st_sem"], rfin["st_val"]))
                    else:
                        es8xb = ExitStack()
                        c.push_scope(es8xb)
                        with es8xb:
                            xr8 = c.sb("b8_xr", [128, 16 * TB], F32)
                            es8ra = ExitStack()
                            c.push_scope(es8ra)
                            with es8ra:
                                with nc.Block() as blk:
                                    r8a = emit_resadd(
                                        c, blk, lnb7[:], xr_page, TB, "b8ra",
                                        dest_sb=xr8, dest_dram=xr_page,
                                        in_ready=(c.vck.sem, r7c["done_v"]),
                                        ld_gate=[(r7a["st_sem"],
                                                  r7a["st_val"])])
                            c.pop_scope()
                            es8xc = ExitStack()
                            c.push_scope(es8xc)
                            with es8xc:
                                with nc.Block() as blk:
                                    k8 = nkit("b8")
                                    r8 = norm(blk, "b8n", k8, xr8, "ln", hn8,
                                              in_ready=(c.vck.sem,
                                                        r8a["done_v"]))
                            c.pop_scope()
                        c.pop_scope()
                c.pop_scope()

                if N_BLOCKS >= 3:
                    es8m = ExitStack()
                    c.push_scope(es8m)
                    with es8m:
                        k_g8 = c.sb("b8_kg", [128, 4 * 2048], BF16)
                        v_g8 = c.sb("b8_vg", [128, 4 * 2048], BF16)

                        def kv_comm8(c_, blk_, kv_sb, rope_v, s_ev8):
                            st8 = c_.ds("b8kv")
                            ldg8 = c_.ds("b8ldg")

                            def sb_(sync):
                                sync.wait_ge(c_.vck.sem, rope_v)
                                for m in range(20, 24):
                                    for tt in range(2):
                                        sync.wait_ge(c_.sck.sem,
                                                     s_ev8[(m, tt)])
                                st8.inc(sync.dma_start(
                                    out=kv2_cin.rearrange(
                                        "(g p) t -> p g t", p=128),
                                    in_=kv_sb[:].rearrange(
                                        "p (g t) -> p g t", g=8)))
                            blk_.sync(sb_)

                            def gb_(g):
                                g.wait_ge(st8.sem, st8.at(16))
                                coll(g, "AllGather", ALU.bypass, PAIRS,
                                     kv2_cin, kv2_cout)
                            blk_.gpsimd(gb_)
                            agv = csn[0]
                            cv = kv2_cout.rearrange(
                                "(s g p tb ti) -> s p g tb ti",
                                s=2, g=8, p=128, tb=8, ti=128)

                            def sb2_(sync):
                                sync.wait_ge(csem, agv)
                                for s_ in range(2):
                                    for g_ in range(4):
                                        kk_ = k_g8[:, g_ * 2048:
                                                   (g_ + 1) * 2048].rearrange(
                                            "p (tb s2 ti) -> p tb s2 ti",
                                            tb=8, s2=2)
                                        vv_ = v_g8[:, g_ * 2048:
                                                   (g_ + 1) * 2048].rearrange(
                                            "p (tb s2 ti) -> p tb s2 ti",
                                            tb=8, s2=2)
                                        ldg8.inc(sync.dma_start(
                                            out=kk_[:, :, s_, :],
                                            in_=cv[s_][:, g_, :, :]))
                                        ldg8.inc(sync.dma_start(
                                            out=vv_[:, :, s_, :],
                                            in_=cv[s_][:, 4 + g_, :, :]))
                            blk_.sync(sb2_)
                            return (k_g8[:], v_g8[:], 2048,
                                    (ldg8.sem, ldg8.at(16 * 16)))

                        with nc.Block() as blk:
                            @blk.gpsimd
                            def _(g):
                                gate_all(g)
                                g.wait_ge(r8a["st_sem"], r8a["st_val"])

                            @blk.sync
                            def _(sync):
                                gate_all(sync)
                                sync.wait_ge(r8a["st_sem"], r8a["st_val"])

                            @blk.sync
                            def _(sync):
                                gate_all(sync)
                                sync.wait_ge(r8a["st_sem"], r8a["st_val"])

                            r8m = emit_l1_attn2(
                                c, blk, hn8[:], d["wat_b"], d["wpj_b"],
                                fm_in(attn_page), TB, d["qpos"], QB1,
                                kv_comm8, tag="b8a",
                                in_ready=(c.vck.sem, r8["done_v"]),
                                mask_dram=d["dmask"])
                    c.pop_scope()
            c.pop_scope()

        if N_BLOCKS >= 3:
            es9h = ExitStack()
            c.push_scope(es9h)
            with es9h:
                hn9 = c.sb("b9_hn9", [128, 16 * TB], BF16)
                es9x = ExitStack()
                c.push_scope(es9x)
                with es9x:
                    lnb8 = c.sb("b8_lnb", [128, 16 * TB], BF16)
                    es9v = ExitStack()
                    c.push_scope(es9v)
                    with es9v:
                        v18 = c.sb("b8_v1", [128, 16 * TB], BF16)
                        es9xa = ExitStack()
                        c.push_scope(es9xa)
                        with es9xa:
                            y8 = c.sb("b8_y", [128, 16 * TB], BF16)
                            ldy8 = c.ds("b8_ldy")
                            with nc.Block() as blk:
                                @blk.sync
                                def _(sync):
                                    gate_all(sync)
                                    for sem_, val_ in r8m["st"]:
                                        sync.wait_ge(sem_, val_)
                                    ldy8.inc(sync.dma_start(
                                        out=fm_out(y8[:]),
                                        in_=fm_in(attn_page)))
                                k8c = nkit("b8c", scr=False)
                                r8c = norm(blk, "b8cn", k8c, None, "rms",
                                           v18, w_col=nw(NRM_L1), eps=1e-6,
                                           src_bf=y8,
                                           in_ready=(ldy8.sem, ldy8.at(16)))
                        c.pop_scope()
                        es9xb = ExitStack()
                        c.push_scope(es9xb)
                        with es9xb:
                            with nc.Block() as blk:
                                k8d = nkit("b8d", scr=False)
                                r8d = norm(blk, "b8dn", k8d, None, "ln",
                                           lnb8, src_bf=v18,
                                           in_ready=(c.vck.sem,
                                                     r8c["done_v"]))
                        c.pop_scope()
                    c.pop_scope()

                    if N_BLOCKS == 3:
                        with nc.Block() as blk:
                            rfin = emit_resadd(
                                c, blk, lnb8[:], xr_page, TB, "fin",
                                dest_dram=out,
                                in_ready=(c.vck.sem, r8d["done_v"]),
                                ld_gate=[(r8a["st_sem"], r8a["st_val"])])
                        final_st.append((rfin["st_sem"], rfin["st_val"]))
                    else:
                        es9xc = ExitStack()
                        c.push_scope(es9xc)
                        with es9xc:
                            xr9 = c.sb("b9_xr", [128, 16 * TB], F32)
                            es9ra = ExitStack()
                            c.push_scope(es9ra)
                            with es9ra:
                                with nc.Block() as blk:
                                    r9a = emit_resadd(
                                        c, blk, lnb8[:], xr_page, TB, "b9ra",
                                        dest_sb=xr9, dest_dram=xr_page,
                                        in_ready=(c.vck.sem, r8d["done_v"]),
                                        ld_gate=[(r8a["st_sem"],
                                                  r8a["st_val"])])
                            c.pop_scope()
                            es9xd = ExitStack()
                            c.push_scope(es9xd)
                            with es9xd:
                                with nc.Block() as blk:
                                    k9 = nkit("b9")
                                    r9 = norm(blk, "b9n", k9, xr9, "ln", hn9,
                                              in_ready=(c.vck.sem,
                                                        r9a["done_v"]))
                            c.pop_scope()
                        c.pop_scope()
                c.pop_scope()

                if N_BLOCKS >= 4:
                    l1v = fm_in(l1f_page)
                    r9f = None
                    for chv in range(2):
                        es9m = ExitStack()
                        c.push_scope(es9m)
                        with es9m:
                            with nc.Block() as blk:
                                @blk.gpsimd
                                def _(g):
                                    gate_all(g)
                                    g.wait_ge(r9a["st_sem"], r9a["st_val"])

                                @blk.sync
                                def _(sync):
                                    gate_all(sync)
                                    sync.wait_ge(r9a["st_sem"],
                                                 r9a["st_val"])

                                r9f = emit_l1_ffn(
                                    c, blk, hn9[:], None, d["w1_b"],
                                    d["w3_b"], d["w2_b"], 512,
                                    tag=f"b9f{chv}",
                                    in_ready=(c.vck.sem, r9["done_v"]),
                                    do_fwht=True, x_stride=TB,
                                    x_off=chv * 512,
                                    out_dram=l1v[:, :, chv * 512:
                                                 (chv + 1) * 512],
                                    out_gate=None if chv == 0 else
                                    r9f["st"])
                        c.pop_scope()
            c.pop_scope()

        if N_BLOCKS >= 4:
            es10 = ExitStack()
            c.push_scope(es10)
            with es10:
                lnb9 = c.sb("b9_lnb", [128, 16 * TB], BF16)
                es10a = ExitStack()
                c.push_scope(es10a)
                with es10a:
                    lf_bf = c.sb("b9_lf", [128, 16 * TB], BF16)
                    ldl = c.ds("b9_ldl")
                    with nc.Block() as blk:
                        @blk.sync
                        def _(sync):
                            gate_all(sync)
                            for sem_, val_ in r9f["st"]:
                                sync.wait_ge(sem_, val_)
                            ldl.inc(sync.dma_start(out=fm_out(lf_bf[:]),
                                                   in_=fm_in(l1f_page)))

                        k10 = nkit("b9c", scr=False)
                        r10 = norm(blk, "b9cn", k10, None, "ln", lnb9,
                                   src_bf=lf_bf,
                                   in_ready=(ldl.sem, ldl.at(16)))
                c.pop_scope()
                with nc.Block() as blk:
                    rfin = emit_resadd(c, blk, lnb9[:], xr_page, TB,
                                       "fin", dest_dram=out,
                                       in_ready=(c.vck.sem, r10["done_v"]),
                                       ld_gate=[(r9a["st_sem"],
                                                 r9a["st_val"])])
                final_st.append((rfin["st_sem"], rfin["st_val"]))
            c.pop_scope()

        with nc.Block() as blk:
            @blk.sync
            def _(sync):
                for sem_, val_ in final_st:
                    sync.wait_ge(sem_, val_)
    return nc


def _hb(wt, chunk):
    return WStream.host_block(np.ascontiguousarray(wt.astype(np.float32)),
                              chunk)


def _fm(v, g=16):
    return np.ascontiguousarray(v.reshape(g, 128).T)


def prep_inputs(inp):
    f = lambda k: np.asarray(inp[k], dtype=np.float32)
    x = f("L0_attn_x")
    SG = SIGMA

    sh = {}
    nrmw = np.zeros((128, 16 * 5), np.float32)
    for i, k in enumerate(("L0_attn_mamba_norm_w", "L0_attn_attn_norm_w",
                           "L0_attn_mlp_norm_w", "L1_attn_norm_w")):
        nrmw[:, i * 16:(i + 1) * 16] = _fm(f(k)[SG])
    nrmw[:, 64:80] = _fm(f("L0_attn_out_proj_b")[SG])
    sh["nrmw"] = nrmw

    wqkv = f("L0_attn_wqkv_w")
    sh["wqkv_b"] = _hb(wqkv[:, SG].T, 128)
    sh["wqkvb"] = np.ascontiguousarray(
        f("L0_attn_wqkv_b").reshape(24, 128).T)
    wopj = f("L0_attn_out_proj_w")
    sh["wopj_b"] = _hb(wopj[SG].T, 128)
    sh["wopjb"] = _fm(f("L0_attn_out_proj_b")[SG])
    sh["lv"] = np.ascontiguousarray(np.stack(
        [f("L0_attn_lambda_q1"), f("L0_attn_lambda_k1"),
         f("L0_attn_lambda_q2"), f("L0_attn_lambda_k2")], axis=1))
    sh["subw"] = f("L0_attn_subln_w").reshape(1, 256)
    fc1 = f("L0_attn_fc1_w")
    sh["fc1g_b"] = _hb(fc1[0:8192, SG].T, 128)
    sh["fc1u_b"] = _hb(fc1[8192:16384, SG].T, 128)
    sh["fc2_b"] = _hb(f("L0_attn_fc2_w")[SG].T, 128)

    wat = f("L1_attn_w_attn")
    wa = wat[SG].copy()
    for h in range(16):
        wa[:, h * 128:(h + 1) * 128] = wa[:, h * 128:(h + 1) * 128][:,
                                                                    ROPE_PERM]
    for h in range(4):
        b0 = 2048 + h * 128
        wa[:, b0:b0 + 128] = wa[:, b0:b0 + 128][:, ROPE_PERM]
    sh["wat_b"] = _hb(wa, 128)
    sh["wpj_b"] = _hb(f("L1_attn_w_proj")[:, SG], 128)
    sh["w1_b"] = _hb(f("L1_ffn_W1")[:, SG].T, 128)
    sh["w3_b"] = _hb(f("L1_ffn_W3")[:, SG].T, 128)
    sh["w2_b"] = _hb(f("L1_ffn_W2")[SG].T, 128)

    rel = np.arange(896) - 384
    pp = np.arange(128)
    masks = []
    for h in range(2):
        m = np.where(rel[None, :] > pp[:, None] - 128 * (1 - h),
                     np.float32(-1e30), np.float32(0.0))
        masks.append(np.ascontiguousarray(m.astype(np.float32)))

    in_maps = []
    A_log = f("L0_attn_A_log")
    in_proj = f("L0_attn_in_proj_w")
    xproj = f("L0_attn_x_proj_w")
    dtproj = f("L0_attn_dt_proj_w")
    convw = f("L0_attn_conv_w")
    convb = f("L0_attn_conv_b")
    dtbias = f("L0_attn_dt_proj_b")
    Dparam = f("L0_attn_D_param")
    wmo = f("L0_attn_mamba_out_w")
    gpk = np.asarray(inp["L0_ffn_gate_W_packed"])
    gsc = f("L0_ffn_gate_scales")
    upk = np.asarray(inp["L0_ffn_up_W_packed"])
    usc = f("L0_ffn_up_scales")
    dpk = np.asarray(inp["L0_ffn_down_W_packed"])
    dsc = f("L0_ffn_down_scales")
    jmap = np.arange(1024) // 16

    for cc in range(8):
        b, h = cc // 2, cc % 2
        m = dict(sh)
        toks = x[b].reshape(16, 128, DM)[h::2].reshape(1024, DM)
        m["x0"] = np.ascontiguousarray(toks[:, SG].T)
        sl = slice(cc * 512, (cc + 1) * 512)
        m["wx_b"] = _hb(in_proj[0:4096][sl][:, SG].T, 128)
        m["wz_b"] = _hb(in_proj[4096:8192][sl][:, SG].T, 128)
        m["xpj_b"] = _hb(xproj[:, sl].T, 80)
        m["wdt_b"] = _hb(dtproj[sl].T, 128)
        m["conv_w"] = np.ascontiguousarray(
            convw[sl].reshape(4, 128, 4).transpose(1, 0, 2).reshape(128, 16))
        m["conv_b"] = np.ascontiguousarray(convb[sl].reshape(4, 128).T)
        m["dtb"] = np.ascontiguousarray(dtbias[sl].reshape(4, 128).T)
        m["Dp"] = np.ascontiguousarray(Dparam[sl].reshape(4, 128).T)
        A2 = np.empty((128, 64), np.float32)
        pq = np.arange(128)
        for M in range(4):
            for s in range(16):
                A2[:, M * 16 + s] = A_log[cc * 512 + M * 128 + s * 8 +
                                          pq // 16, pq % 16]
        m["A2_d"] = A2
        m["wmo_b"] = _hb(wmo[SG][:, sl].T, 128)
        order = np.concatenate([np.arange(0, 2048, 2), np.arange(1, 2048, 2)])
        idx = cc * 2048 + order
        m["gpk"] = np.ascontiguousarray(gpk[idx].T)
        m["gsc"] = np.ascontiguousarray(gsc[idx][:, jmap].T)
        m["upk"] = np.ascontiguousarray(upk[idx].T)
        m["usc"] = np.ascontiguousarray(usc[idx][:, jmap].T)
        jloc = cc * 1024 + np.arange(1024)
        m["dpk"] = np.ascontiguousarray(dpk[SG][:, jloc].T)
        m["dsc"] = np.ascontiguousarray(dsc[SG][:, jloc // 16].T)
        m["dmask"] = masks[h]
        pos = ((2 * np.arange(8)[:, None] + h) * 128 +
               np.arange(128)[None, :]).reshape(1, 1024)
        m["qpos"] = pos.astype(np.float32)
        in_maps.append(m)
    return in_maps


_CACHE = {}


def kernel(**inputs):
    in_maps = prep_inputs(inputs)
    key = "prog"
    if key not in _CACHE:
        _CACHE[key] = build_program()
    nc = _CACHE[key]
    res = run_bass_kernel_spmd(nc, in_maps, core_ids=list(range(8)))
    SG_inv = np.argsort(SIGMA)
    xo = np.empty((4, 2048, DM), np.float32)
    for cc in range(8):
        b, h = cc // 2, cc % 2
        o = np.asarray(res.results[cc]["out"])[SG_inv]
        xo[b].reshape(16, 128, DM)[h::2] = o.T.reshape(8, 128, DM)
    return xo


if __name__ == "__main__":
    print("module ok; N_BLOCKS =", N_BLOCKS)
    nc = build_program()
    print("build ok")
